# version 10
# baseline (speedup 1.0000x reference)
import numpy as np
from contextlib import ExitStack

import concourse.bass as bass
import concourse.mybir as mybir

F32 = mybir.dt.float32
BF16 = mybir.dt.bfloat16
ALU = mybir.AluOpType
AF = mybir.ActivationFunctionType

DM = 4096
NDC = 32
NRC = 145
NCC = 64
CS = 128
EPS = 1e-5

DEBUG = False


class Sem:
    def __init__(self, stack, nc, name):
        self.h = stack.enter_context(nc.semaphore(name))
        self.n = 0

    def add(self, k):
        self.n += k
        return self.n


class Eng:
    def __init__(self, eng, selfsem):
        self.e = eng
        self.s = selfsem

    def wait(self, sem, n):
        self.e.wait_ge(sem.h, n)

    def op(self, name, *args, dep=True, **kw):
        if dep is True:
            if self.s.n:
                self.e.wait_ge(self.s.h, self.s.n)
        elif dep is not None and dep > 0:
            self.e.wait_ge(self.s.h, dep)
        inst = getattr(self.e, name)(*args, **kw)
        inst.then_inc(self.s.h, 1)
        return self.s.add(1)

    def dma(self, out, in_, sem, **kw):
        inst = self.e.dma_start(out=out, in_=in_, **kw)
        inst.then_inc(sem.h, 16)
        return sem.add(16)


class MM:
    def __init__(self, eng, sem):
        self.e = eng
        self.s = sem

    def wait(self, sem, n):
        self.e.wait_ge(sem.h, n)

    def mm(self, out, lhsT, rhs, start, stop, mark=False, **kw):
        inst = self.e.matmul(out, lhsT, rhs, start=start, stop=stop, **kw)
        if mark:
            inst.then_inc(self.s.h, 1)
            return self.s.add(1)
        return None

    def tr(self, out, in_, ident, mark=False):
        inst = self.e.transpose(out, in_, ident)
        if mark:
            inst.then_inc(self.s.h, 1)
            return self.s.add(1)
        return None




def emit_phase_c(nc, S, Sem, ctx, stk):
    NT = ctx["NT"]; NCH = ctx["NCH"]
    cons = ctx["cons"]; identF = ctx["identF"]; identB = ctx["identB"]; maskT = ctx["maskT"]
    onesrowB = ctx["onesrowB"]; onecolB = ctx["onecolB"]; iot = ctx["iot"]
    nwTc = ctx["nwTc"]; Dex = ctx["Dex"]; dtv = ctx["dtv"]; A_dt = ctx["A_dt"]
    gates = ctx["gates"]; xcs = ctx["xcs"]; ygs = ctx["ygs"]; bncA = ctx["bncA"]; bncB = ctx["bncB"]

    sb = nc.sbuf_tensor
    xcb = [stk.enter_context(sb(f"xcb{i}", [128, 80 * 128], BF16)) for i in range(2)]
    gtb = stk.enter_context(sb("gtb", [128, 16 * 128], BF16))
    acs = [stk.enter_context(sb(f"acs{i}", [128, CS], F32)) for i in range(2)]
    decp = stk.enter_context(sb("decp", [128, CS], F32))
    sdf = stk.enter_context(sb("sdf", [128, CS], F32))
    decf = stk.enter_context(sb("decf", [128, CS], F32))
    dlc = [stk.enter_context(sb(f"dlc{i}", [128, 1], F32)) for i in range(2)]
    dl2 = [stk.enter_context(sb(f"dl2{i}", [2, 64], F32)) for i in range(2)]
    half01 = stk.enter_context(sb("half01", [2, 128], F32))
    i2 = stk.enter_context(sb("i2", [2, 128], mybir.dt.int32))
    h0t = stk.enter_context(sb("h0t", [2, 128], F32))
    sdT = stk.enter_context(sb("sdT", [128, 128], F32))
    decayT = stk.enter_context(sb("decayT", [128, 128], F32))
    dtT = stk.enter_context(sb("dtT", [128, 128], F32))
    ddtT = stk.enter_context(sb("ddtT", [128, 128], F32))
    dexpA = stk.enter_context(sb("dexpA", [128, 64 * NCH], F32))
    ahb = stk.enter_context(sb("ahb", [128, CS], BF16))
    ln32 = stk.enter_context(sb("ln32", [128, CS], F32))
    lnhb = stk.enter_context(sb("lnhb", [128, CS], BF16))
    lnlb = stk.enter_context(sb("lnlb", [128, CS], BF16))
    ah32 = stk.enter_context(sb("ah32", [128, CS], F32))
    al32 = stk.enter_context(sb("al32", [128, CS], F32))
    albf = stk.enter_context(sb("albf", [128, CS], BF16))
    nahb = stk.enter_context(sb("nahb", [128, CS], BF16))
    nalb = stk.enter_context(sb("nalb", [128, CS], BF16))
    lhsT4 = stk.enter_context(sb("lhsT4", [6, 4096], BF16))
    rhs4 = stk.enter_context(sb("rhs4", [6, 4096], BF16))
    Lex = [stk.enter_context(sb(f"Lex{i}", [128, 1024], BF16)) for i in range(2)]
    Lt = stk.enter_context(sb("Lt", [128, 1024], F32))
    tril01 = stk.enter_context(sb("tril01", [128, 128], BF16))
    Msb = [stk.enter_context(sb(f"Msb{i}", [128, 1024], BF16)) for i in range(2)]
    xdt = [stk.enter_context(sb(f"xdt{i}", [128, 512], BF16)) for i in range(2)]
    xdd = [stk.enter_context(sb(f"xdd{i}", [128, 512], BF16)) for i in range(2)]
    Zs = [stk.enter_context(sb(f"Zsb{i}", [128, 512], F32)) for i in range(2)]
    BTsb = stk.enter_context(sb("BTsb", [128, 8 * 128], BF16))
    Gsb = stk.enter_context(sb("Gsb", [128, 8 * 128], BF16))
    Ssb = stk.enter_context(sb("Ssb", [128, 64 * 128], F32))
    Sbfst = [stk.enter_context(sb("Sbfs0", [128, 1024], BF16))] * 2
    STsb = stk.enter_context(sb("STsb", [128, 64 * 128], BF16))
    Ybf = stk.enter_context(sb("Ybf", [128, 64 * 128], BF16))
    rtmp = stk.enter_context(sb("rtmp", [128, 512], F32))
    dtmp = None
    y2st = [stk.enter_context(sb("y2st0", [128, 1024], BF16))] * 2
    sgst = [stk.enter_context(sb("sgst0", [128, 1024], BF16))] * 2
    t1st = [stk.enter_context(sb(f"t1st{i}", [128, 1024], BF16)) for i in range(2)]
    ygst = [stk.enter_context(sb("ygst0", [128, 1024], BF16))] * 2
    rrow = stk.enter_context(sb("rrow", [1, 128], F32))
    rrowB = stk.enter_context(sb("rrowB", [1, 128], BF16))
    RSTD = stk.enter_context(sb("RSTD", [128, 128], F32))
    zeroB = stk.enter_context(sb("zeroB", [128, 128], F32))

    psL = stk.enter_context(nc.psum_tensor("psL", [128, 1024], F32))
    psY = [stk.enter_context(nc.psum_tensor(f"psYc{i}", [128, 512], F32)) for i in range(2)]
    psZ = stk.enter_context(nc.psum_tensor("psZ", [128, 512], F32))
    psSt = [stk.enter_context(nc.psum_tensor(f"psSt{i}", [128, 512], F32)) for i in range(2)]
    psT = [stk.enter_context(nc.psum_tensor("psTc0", [128, 1024], BF16))] * 2

    ldc = [Sem(stk, nc, f"ldc{i}") for i in range(2)]
    ldg = Sem(stk, nc, "ldg")
    bnc = Sem(stk, nc, "bnc")
    rla = Sem(stk, nc, "rla")
    rdl = [Sem(stk, nc, "rdl0"), Sem(stk, nc, "rdl1")]
    bncd = Sem(stk, nc, "bncd")
    wry = [Sem(stk, nc, f"wry{i}") for i in range(2)]

    class Marks(dict):
        strict = False
        def __missing__(self, key):
            if self.strict:
                raise KeyError(key)
            return 0
    M = Marks()

    def pool_stream(p, rec):
        k = p.op("iota", i2[:], pattern=[[1, 128]], base=0, channel_multiplier=-64)
        if rec: M["i2"] = k
        k = p.dma(xcb[0][:].rearrange("p (a t) -> p a t", a=80), xcs[0].rearrange("a p t -> p a t"), ldc[0])
        if rec: M["ldx", 0] = k
        for c in range(NCH):
            if c + 1 < NCH:
                if c >= 1:
                    p.wait(S["pe"], M["rnd_end", c - 1])
                    p.wait(S["dve"], M["yg", c - 1, 7])
                k = p.dma(xcb[(c + 1) % 2][:].rearrange("p (a t) -> p a t", a=80), xcs[c + 1].rearrange("a p t -> p a t"), ldc[(c + 1) % 2])
                if rec: M["ldx", c + 1] = k
            for h in range(4):
                if (c, h) != (0, 0):
                    pc, ph = (c, h * 2 - 1) if h >= 1 else (c - 1, 7)
                    p.wait(S["act"], M["silu", pc, ph])
                k = p.dma(gtb[:].rearrange("p (a t) -> p a t", a=16),
                          gates[c, h * 16:(h + 1) * 16].rearrange("a p t -> p a t"), ldg)
                if rec: M["ldg", c, h] = k

    def sync_stream(s, rec):
        s.wait(S["dve"], M["onesm"])
        s.dma(lhsT4[2:4, :], rhs4[0:2, :], rla)
        k = s.dma(rhs4[4:6, :], rhs4[0:2, :], rla)
        if rec: M["onesr"] = k
        s.wait(rla, M["onesr"])
        for c in range(NCH):
            s.wait(S["dve"], M["presplit", c])
            off = c * 128 * CS
            s.dma(bncB[0, off:off + 128 * CS].rearrange("(p l) -> p l", p=128), nahb[:], bnc)
            s.dma(bncB[1, off:off + 128 * CS].rearrange("(p l) -> p l", p=128), nalb[:], bnc)
            s.dma(bncB[2, off:off + 128 * CS].rearrange("(p l) -> p l", p=128), ahb[:], bnc)
            s.dma(bncB[3, off:off + 128 * CS].rearrange("(p l) -> p l", p=128), albf[:], bnc)
            s.dma(bncB[4, off:off + 128 * CS].rearrange("(p l) -> p l", p=128), lnhb[:], bnc)
            k = s.dma(bncB[5, off:off + 128 * CS].rearrange("(p l) -> p l", p=128), lnlb[:], bnc)
            if rec: M["bncw", c] = k
            s.wait(S["act"], M["predl", c])
            k = s.dma(bncA[2, c * 128:(c + 1) * 128].rearrange("(p o) -> p o", p=128), dlc[c % 2][:], bncd)
            if rec: M["bncdw", c] = k
            s.wait(bncd, M["bncdw", c])
            if c >= 2:
                s.wait(S["pe"], M["predmm", c - 2])
            k = s.dma(dl2[c % 2][:], bncA[2, c * 128:(c + 1) * 128].rearrange("(c q) -> q c", q=2),
                      rdl[c % 2], allow_slow_non_contiguous=True)
            if rec: M["rldl", c] = k
        for c in range(NCH):
            for q in range(4):
                s.wait(bnc, M["bncw", c])
                if (c, q) != (0, 0):
                    pc, pq = (c, q - 1) if q > 0 else (c - 1, 3)
                    s.wait(S["pe"], M["lmm", pc, pq * 4 + 3])
                off = c * 128 * CS + q * 4096
                s.dma(lhsT4[0:2, :], bncB[0:2, off:off + 4096], rla)
                s.dma(lhsT4[4:6, :], bncB[4:6, off:off + 4096], rla)
                k = s.dma(rhs4[2:4, :], bncB[2:4, off:off + 4096], rla)
                if rec: M["rlq", c, q] = k
            for gr in range(8):
                s.wait(S["dve"], M["yg", c, gr])
                for j in range(8):
                    k = s.dma(ygs[c, gr * 8 + j], ygst[0][:, j * 128:(j + 1) * 128], wry[gr % 2])
                if rec: M["wyg", c, gr] = k
        s.wait(wry[0], wry[0].n if not rec else M["wyg", NCH - 1, 6])
        s.wait(wry[1], wry[1].n if not rec else M["wyg", NCH - 1, 7])

    def pe_stream(p, rec):
        for c in range(NCH):
            p.wait(rdl[c % 2], M["rldl", c])
            k = p.mm(psZ[:, (c % 8) * 64:(c % 8) * 64 + 64], half01[:], dl2[c % 2][:], True, True,
                     mark=True)
            if rec: M["predmm", c] = k
            if rec and (c % 8 == 7 or c == NCH - 1):
                M["predexp", c // 8] = k
            if c % 8 == 7 and c + 1 < NCH:
                p.wait(S["dve"], M["dexpevA", c // 8])
        for c in range(NCH):
            p.wait(ldc[c % 2], M["ldx", c])
            if c >= 1:
                p.wait(S["act"], M["yev", c - 1, 15])
            for g in range(8):
                k = p.mm(psY[g // 4][:, (g % 4) * 128:(g % 4 + 1) * 128],
                         xcb[c % 2][:, (64 + g) * 128:(65 + g) * 128],
                         xcb[c % 2][:, (72 + g) * 128:(73 + g) * 128],
                         True, True, mark=(g in (3, 7)))
                if rec and g == 3: M["gmm", c, 0] = k
                if rec and g == 7: M["gmm", c, 1] = k
            p.wait(S["act"], M["sd", c])
            if c >= 1:
                p.wait(S["dve"], M["rstdev", c - 1])
            p.tr(psSt[0][:, 0:128], sdf[:], identF[:])
            p.tr(psSt[0][:, 128:256], decf[:], identF[:])
            k = p.tr(psSt[0][:, 256:384], dtv[:, c * CS:(c + 1) * CS], identF[:], mark=True)
            if rec: M["ptr", c] = k
            if c >= 1:
                p.wait(S["dve"], M["stev", c - 1, 7])
            for g in range(8):
                k = p.tr(psT[0][:, g * 128:(g + 1) * 128],
                         xcb[c % 2][:, (64 + g) * 128:(65 + g) * 128], identB[:],
                         mark=(g == 7))
                if rec and g == 7: M["btr", c] = k
            for r in range(16):
                q = r // 4
                if c >= 1:
                    if r >= 1:
                        p.wait(S["dve"], M["zsev", c, r - 1])
                    elif c == 1:
                        p.wait(S["dve"], M["dexpfin"])
                    else:
                        p.wait(S["dve"], M["zsev", c - 1, 15])
                    g = r // 2
                    for j in range(4):
                        pp = r * 4 + j
                        k = p.mm(psZ[:, j * 128:(j + 1) * 128],
                                 xcb[c % 2][:, (72 + g) * 128:(73 + g) * 128],
                                 STsb[:, pp * 128:(pp + 1) * 128],
                                 True, True, mark=(j == 3))
                    if rec: M["zmm", c, r] = k
                p.wait(rla, M["rlq", c, q])
                if r == 0 and c == 0:
                    p.wait(rla, M["onesr"])
                if r >= 1:
                    p.wait(S["act"], M["lrelu", c, r - 1])
                elif c >= 1:
                    p.wait(S["act"], M["lrelu", c - 1, 15])
                for hh in range(8):
                    off = ((r % 4) * 8 + hh) * 128
                    col = slice(hh * 128, (hh + 1) * 128)
                    k = p.mm(psL[:, col], lhsT4[:, off:off + 128], rhs4[:, off:off + 128],
                             True, True, mark=(hh == 7))
                if rec: M["lmm", c, r] = k
                if r >= 1:
                    p.wait(S["act"], M["xdtc", c, r - 1])
                else:
                    p.wait(S["dve"], M["btev", c])
                for j in range(4):
                    pp = r * 4 + j
                    k = p.tr(psT[0][:, j * 128:(j + 1) * 128],
                             xcb[c % 2][:, pp * 128:(pp + 1) * 128], identB[:], mark=(j == 3))
                if rec: M["xtr", c, r] = k
                if r >= 2:
                    p.wait(S["act"], M["yev", c, r - 2])
                if r == 0:
                    p.wait(S["dve"], M["gev", c])
                if c >= 1:
                    p.wait(S["dve"], M["zsev", c, r])
                    for j in range(4):
                        k = p.mm(psY[r % 2][:, j * 128:(j + 1) * 128],
                                 Zs[r % 2][:, j * 128:(j + 1) * 128],
                                 identF[:], j == 0, False, is_transpose=True,
                                 skip_group_check=True, mark=(j == 3))
                    if rec: M["ztr", c, r] = k
                else:
                    for j in range(4):
                        k = p.mm(psY[r % 2][:, j * 128:(j + 1) * 128], zeroB[:],
                                 identF[:], j == 0, False, is_transpose=True,
                                 skip_group_check=True, mark=(j == 3))
                    if rec:
                        M["zmm", c, r] = None
                        M["ztr", c, r] = k
                p.wait(S["dve"], M["msb", c, r])
                p.wait(S["act"], M["xdt", c, r])
                for hh in range(8):
                    k = p.mm(psY[r % 2][(hh % 2) * 64:(hh % 2) * 64 + 64,
                                        (hh // 2) * 128:(hh // 2 + 1) * 128],
                             xdt[r % 2][:, hh * 64:(hh + 1) * 64],
                             Msb[r % 2][:, hh * 128:(hh + 1) * 128],
                             False, hh >= 6, skip_group_check=True, mark=(hh == 7))
                if rec: M["ydiag", c, r] = k
                p.wait(S["dve"], M["xdd", c, r])
                if r == 0:
                    p.wait(S["dve"], M["ptrev", c])
                    p.wait(S["dve"], M["btev", c])
                    p.wait(S["dve"], M["rstdev", c - 1] if c >= 1 else 0)
                elif r == 1:
                    p.wait(S["act"], M["rec", c - 1, 15] if c >= 1 else 0)
                else:
                    p.wait(S["act"], M["rec", c, r - 2])
                g = r // 2
                for j in range(4):
                    k = p.mm(psSt[r % 2][:, j * 128:(j + 1) * 128],
                             xdd[r % 2][:, j * 128:(j + 1) * 128],
                             BTsb[:, g * 128:(g + 1) * 128],
                             True, True, mark=(j == 3))
                if rec: M["smm", c, r] = k
                p.wait(S["dve"], M["dgb", c, r])
                if c >= 1:
                    p.wait(S["act"], M["rec", c - 1, r])
                for j in range(4):
                    k = p.mm(psSt[r % 2][:, j * 128:(j + 1) * 128],
                             rtmp[:, j * 128:(j + 1) * 128],
                             Ssb[:, r * 512 + j * 128:r * 512 + (j + 1) * 128],
                             False, True, skip_group_check=True, mark=(j == 3))
                if rec: M["recmm", c, r] = k
                if rec and r == 15: M["rnd_end", c] = k
            p.wait(S["act"], M["rec", c, 14])
            for gr in range(8):
                p.wait(S["dve"], M["y2", c, gr])
                for j in range(8):
                    k = p.mm(psSt[0][0:1, 0:128], onecolB[:],
                             y2st[0][:, j * 128:(j + 1) * 128],
                             gr == 0 and j == 0, gr == 7 and j == 7, mark=(j == 7))
                if rec: M["stat", c, gr] = k
            p.wait(S["dve"], M["rrowB", c])
            k = p.mm(psSt[0][:, 128:256], onesrowB[0:1, :], rrowB[0:1, :], True, True, mark=True)
            if rec: M["rbc", c] = k
            for sg in range(8):
                p.wait(S["act"], M["sbf", c, sg])
                if sg >= 1:
                    p.wait(S["dve"], M["stev", c, sg - 1])
                for j in range(8):
                    k = p.tr(psT[0][:, j * 128:(j + 1) * 128],
                             Sbfst[0][:, j * 128:(j + 1) * 128], identB[:], mark=(j == 7))
                if rec: M["str", c, sg] = k

    def dve_stream(d, rec):
        d.op("memset", Ssb[:], 0.0)
        d.op("memset", zeroB[:], 0.0)
        k = d.op("memset", rhs4[0:2, :], 1.0)
        M["onesm"] = k
        d.op("tensor_scalar", tril01[:], iot[:], 0.0, None, ALU.is_ge)
        d.wait(S["pool"], M["i2"])
        d.op("tensor_scalar", half01[:], i2[:], 0.0, None, ALU.is_ge)
        d.op("tensor_scalar", h0t[:], i2[:], 64.0, None, ALU.is_lt)
        d.op("tensor_tensor", half01[:], half01[:], h0t[:], ALU.mult)
        for c in range(NCH):
            if c >= 1:
                d.wait(bnc, M["bncw", c - 1])
            km = d.op("tensor_tensor_scan", acs[c % 2][:], cons[:, 0:1].to_broadcast((128, CS)),
                      A_dt[:, c * CS:(c + 1) * CS], 0.0, ALU.mult, ALU.add,
                      dep=M["presplit", c - 2] if c >= 2 else None)
            if rec: M["prescan", c] = km
            km = d.op("tensor_copy", ahb[:], acs[c % 2][:], dep=km)
            km = d.op("tensor_copy", ah32[:], ahb[:], dep=km)
            km = d.op("tensor_tensor", al32[:], acs[c % 2][:], ah32[:], ALU.subtract, dep=km)
            km = d.op("tensor_copy", albf[:], al32[:], dep=km)
            km = d.op("tensor_scalar", nahb[:], ahb[:], -1.0, None, ALU.mult, dep=km)
            km = d.op("tensor_scalar", nalb[:], albf[:], -1.0, None, ALU.mult, dep=km)
            d.wait(S["act"], M["preln", c])
            km = d.op("tensor_copy", lnhb[:], ln32[:], dep=km)
            km = d.op("tensor_copy", ah32[:], lnhb[:], dep=km)
            km = d.op("tensor_tensor", al32[:], ln32[:], ah32[:], ALU.subtract, dep=km)
            k = d.op("tensor_copy", lnlb[:], al32[:], dep=km)
            if rec: M["presplit", c] = k
            if c % 8 == 7 or c == NCH - 1:
                h = c // 8
                nn = min(8, NCH - 8 * h)
                d.wait(S["pe"], M["predexp", h])
                k = d.op("tensor_copy", dexpA[:, h * 512:h * 512 + nn * 64], psZ[:, 0:nn * 64])
                if rec: M["dexpevA", h] = k
                if rec and c == NCH - 1: M["dexpfin"] = k
        for c in range(NCH):
            if c >= 2:
                d.wait(S["act"], M["sd", c - 2])
            km = d.op("tensor_tensor_scan", acs[c % 2][:], cons[:, 0:1].to_broadcast((128, CS)),
                      A_dt[:, c * CS:(c + 1) * CS], 0.0, ALU.mult, ALU.add,
                      dep=M["dexpfin"] if c <= 1 else None,
                      )
            if rec: M["scan", c] = km
            d.op("tensor_scalar", decp[:], acs[c % 2][:], acs[c % 2][:, CS - 1:CS], None,
                 ALU.subtract, dep=km)
            if rec: M["decp", c] = d.s.n
            d.wait(S["pe"], M["ptr", c])
            d.op("tensor_copy", sdT[:], psSt[0][:, 0:128])
            d.op("tensor_copy", decayT[:], psSt[0][:, 128:256])
            d.op("tensor_copy", dtT[:], psSt[0][:, 256:384])
            k = d.op("tensor_tensor", ddtT[:], dtT[:], decayT[:], ALU.mult)
            if rec: M["ptrev", c] = k
            d.wait(S["pe"], M["gmm", c, 1])
            d.op("tensor_tensor", Gsb[:, 0:512].rearrange("p (a b) -> p a b", a=4),
                 psY[0][:, :].rearrange("p (a b) -> p a b", a=4),
                 tril01[:].unsqueeze(1).to_broadcast((128, 4, 128)), ALU.mult)
            k = d.op("tensor_tensor", Gsb[:, 512:1024].rearrange("p (a b) -> p a b", a=4),
                 psY[1][:, :].rearrange("p (a b) -> p a b", a=4),
                 tril01[:].unsqueeze(1).to_broadcast((128, 4, 128)), ALU.mult)
            if rec: M["gev", c] = k
            d.wait(S["pe"], M["btr", c])
            k = d.op("tensor_copy", BTsb[:, :], psT[0][:, :])
            if rec: M["btev", c] = k
            for r in range(16):
                if r >= 1:
                    p_recmm = M["recmm", c, r - 1]
                elif c >= 1:
                    p_recmm = M["recmm", c - 1, 15]
                else:
                    p_recmm = 0
                d.wait(S["pe"], p_recmm)
                k = d.op("tensor_tensor", rtmp[:].rearrange("p (a b) -> p a b", a=4),
                     identF[:].unsqueeze(1).to_broadcast((128, 4, 128)),
                     dexpA[:, c * 64 + r * 4:c * 64 + r * 4 + 4].unsqueeze(2).to_broadcast((128, 4, 128)),
                     ALU.mult,
                     dep=M["dgb", c, r - 1] if r >= 1 else (M["dgb", c - 1, 15] if c >= 1 else None))
                if rec: M["dgb", c, r] = k
                if c >= 1:
                    d.wait(S["pe"], M["zmm", c, r])
                    if r >= 1:
                        d.wait(S["pe"], M["ztr", c, r - 1])
                    k = d.op("tensor_tensor", Zs[r % 2][:].rearrange("p (a b) -> p a b", a=8),
                             psZ[:].rearrange("p (a b) -> p a b", a=8),
                             sdT[:, r * 8:r * 8 + 8].unsqueeze(2).to_broadcast((128, 8, 64)), ALU.mult,
                             dep=M["zsev", c, r - 2] if r >= 2 else None)
                    if rec: M["zsev", c, r] = k
                elif rec:
                    M["zsev", c, r] = None
                d.wait(S["act"], M["xdt", c, r])
                k = d.op("tensor_tensor", xdd[r % 2][:].rearrange("p (a b) -> p a b", a=8),
                         xdt[r % 2][:].rearrange("p (a b) -> p a b", a=8),
                         ddtT[:, r * 8:r * 8 + 8].unsqueeze(2).to_broadcast((128, 8, 64)), ALU.mult,
                         dep=M["xdd", c, r - 2] if r >= 2 else None)
                if rec: M["xdd", c, r] = k
                d.wait(S["act"], M["lex", c, r])
                if r >= 2:
                    d.wait(S["pe"], M["ydiag", c, r - 2])
                g = r // 2
                k = d.op("tensor_tensor", Msb[r % 2][:].rearrange("p (a b) -> p a b", a=8),
                         Lex[r % 2][:].rearrange("p (a b) -> p a b", a=8),
                         Gsb[:, g * 128:(g + 1) * 128].unsqueeze(1).to_broadcast((128, 8, 128)),
                         ALU.mult,
                         dep=M["msb", c, r - 2] if r >= 2 else (M["gev", c] if r == 0 else None))
                if rec: M["msb", c, r] = k
            for gr in range(8):
                d.wait(S["act"], M["yev", c, 2 * gr + 1])
                km = d.op("tensor_tensor", t1st[gr % 2][:].rearrange("p (a b) -> p a b", a=8),
                          xcb[c % 2][:, gr * 1024:(gr + 1) * 1024].rearrange("p (a b) -> p a b", a=8),
                          Dex[:, gr * 8:gr * 8 + 8].unsqueeze(2).to_broadcast((128, 8, 128)), ALU.mult,
                          dep=M["dadd", c, gr - 2] if gr >= 2 else (M["yg", c - 1, 6 + gr] if c >= 1 else None))
                km = d.op("tensor_tensor", Ybf[:, gr * 1024:(gr + 1) * 1024],
                          Ybf[:, gr * 1024:(gr + 1) * 1024], t1st[gr % 2][:], ALU.add, dep=km)
                if rec: M["dadd", c, gr] = km
                if gr >= 1:
                    d.wait(S["pe"], M["stat", c, gr - 1])
                k = d.op("tensor_tensor", y2st[0][:], Ybf[:, gr * 1024:(gr + 1) * 1024],
                         Ybf[:, gr * 1024:(gr + 1) * 1024], ALU.mult, dep=km)
                if rec: M["y2", c, gr] = k
            d.wait(S["pe"], M["stat", c, 7])
            d.op("tensor_scalar", rrow[0:1, :], psSt[0][0:1, 0:128], 1.0 / 8192.0, None, ALU.mult)
            if rec: M["msq", c] = d.s.n
            d.wait(S["act"], M["rstd", c])
            k = d.op("tensor_copy", rrowB[0:1, :], rrow[0:1, :])
            if rec: M["rrowB", c] = k
            d.wait(S["pe"], M["rbc", c])
            k = d.op("tensor_copy", RSTD[:], psSt[0][:, 128:256])
            if rec: M["rstdev", c] = k
            for sg in range(8):
                d.wait(S["pe"], M["str", c, sg])
                k = d.op("tensor_copy", STsb[:, sg * 1024:(sg + 1) * 1024], psT[0][:, :],
                         dep=None)
                if rec: M["stev", c, sg] = k
            for gr in range(8):
                d.wait(S["act"], M["silu", c, gr])
                if (c, gr) >= (0, 1):
                    pc, pg = (c, gr - 1) if gr >= 1 else (c - 1, 7)
                    if pc >= 0:
                        d.wait(wry[pg % 2], M["wyg", pc, pg])
                km = d.op("tensor_tensor", t1st[gr % 2][:].rearrange("p (a b) -> p a b", a=8),
                     Ybf[:, gr * 1024:(gr + 1) * 1024].rearrange("p (a b) -> p a b", a=8),
                     RSTD[:].unsqueeze(1).to_broadcast((128, 8, 128)), ALU.mult, dep=max(M["dadd", c, 7], M["yg", c, gr - 2] if gr >= 2 else 0))
                km = d.op("tensor_tensor", t1st[gr % 2][:].rearrange("p (a b) -> p a b", a=8),
                     t1st[gr % 2][:].rearrange("p (a b) -> p a b", a=8),
                     nwTc[:, gr * 8:(gr + 1) * 8].unsqueeze(2).to_broadcast((128, 8, 128)), ALU.mult, dep=km)
                k = d.op("tensor_tensor", ygst[0][:], t1st[gr % 2][:], sgst[0][:], ALU.mult, dep=km)
                if rec: M["yg", c, gr] = k

    def act_stream(a, rec):
        for c in range(NCH):
            a.wait(S["dve"], M["prescan", c])
            if c >= 2:
                a.wait(bncd, M["bncdw", c - 2])
            k = a.op("activation", dlc[c % 2][:], acs[c % 2][:, CS - 1:CS], AF.Exp,
                     dep=M["predl", c - 2] if c >= 2 else None)
            if rec: M["predl", c] = k
            if c >= 1:
                a.wait(S["dve"], M["presplit", c - 1])
            k = a.op("activation", ln32[:], dtv[:, c * CS:(c + 1) * CS], AF.Ln,
                     bias=cons[:, 14:15], dep=None)
            if rec: M["preln", c] = k
        for c in range(NCH):
            a.wait(S["dve"], M["decp", c])
            a.op("activation", sdf[:], acs[c % 2][:], AF.Exp,
                 dep=M["sd", c - 1] if c >= 1 else None)
            k = a.op("activation", decf[:], decp[:], AF.Exp, scale=-1.0, dep=None)
            if rec: M["sd", c] = k
            for r in range(16):
                a.wait(S["pe"], M["lmm", c, r])
                km = a.op("activation", Lt[:], psL[:], AF.Relu, scale=-1.0,
                          dep=M["lex", c, r - 1] if r >= 1 else (M["lex", c - 1, 15] if c >= 1 else None))
                if rec: M["lrelu", c, r] = km
                if r >= 2:
                    a.wait(S["dve"], M["msb", c, r - 2])
                k = a.op("activation", Lex[r % 2][:], Lt[:], AF.Exp, scale=-1.0, dep=km)
                if rec: M["lex", c, r] = k
                a.wait(S["pe"], M["xtr", c, r])
                if r >= 2:
                    a.wait(S["dve"], M["xdd", c, r - 2])
                k = a.op("activation", xdt[r % 2][:], psT[0][:, 0:512], AF.Copy,
                         dep=None)
                if rec:
                    M["xdt", c, r] = k
                    M["xdtc", c, r] = k
                if r >= 1:
                    a.wait(S["pe"], M["ydiag", c, r - 1])
                    k = a.op("activation", Ybf[:, (r - 1) * 512:r * 512], psY[(r - 1) % 2][:, :],
                             AF.Copy, dep=None)
                    if rec: M["yev", c, r - 1] = k
                    a.wait(S["pe"], M["recmm", c, r - 1])
                    k = a.op("activation", Ssb[:, (r - 1) * 512:r * 512], psSt[(r - 1) % 2][:, :],
                             AF.Copy, dep=None)
                    if rec: M["rec", c, r - 1] = k
                    else: assert M["rec", c, r - 1] == k, (c, r - 1, M["rec", c, r - 1], k)
            a.wait(S["pe"], M["ydiag", c, 15])
            k = a.op("activation", Ybf[:, 15 * 512:16 * 512], psY[1][:, :], AF.Copy, dep=None)
            if rec: M["yev", c, 15] = k
            a.wait(S["pe"], M["recmm", c, 15])
            k = a.op("activation", Ssb[:, 15 * 512:16 * 512], psSt[1][:, :], AF.Copy, dep=None)
            if rec: M["rec", c, 15] = k
            a.wait(S["dve"], M["msq", c])
            a.op("activation", rrow[0:1, :], rrow[0:1, :], AF.Ln, bias=cons[0:1, 2:3])
            k = a.op("activation", rrow[0:1, :], rrow[0:1, :], AF.Exp, scale=-0.5)
            if rec: M["rstd", c] = k
            for sg in range(8):
                if sg >= 1:
                    a.wait(S["pe"], M["str", c, sg - 1])
                k = a.op("activation", Sbfst[0][:], Ssb[:, sg * 1024:(sg + 1) * 1024], AF.Copy,
                         dep=max(M["rec", c, 2 * sg + 1], M["sbf", c, sg - 1] if sg >= 1 else 0))
                if rec: M["sbf", c, sg] = k
            for gr in range(8):
                a.wait(ldg, M["ldg", c, gr // 2])
                if gr >= 1:
                    a.wait(S["dve"], M["yg", c, gr - 1])
                elif c >= 1:
                    a.wait(S["dve"], M["yg", c - 1, 7])
                k = a.op("activation", sgst[0][:], gtb[:, (gr % 2) * 1024:(gr % 2 + 1) * 1024], AF.Silu,
                         dep=M["silu", c, gr - 1] if gr >= 1 else (M["silu", c - 1, 7] if c >= 1 else None))
                if rec: M["silu", c, gr] = k

    class CntEng:
        def __init__(self, key, cnt, dmac):
            self.key = key; self.c = cnt; self.d = dmac
            class _S:
                pass
            self.s = _S()
            self.s.n = cnt.get(key, 0)
        def op(self, *a, dep=None, **k):
            self.c[self.key] += 1
            self.s.n = self.c[self.key]
            return self.c[self.key]
        def dma(self, out, in_, sem, **kw):
            key = id(sem)
            self.d[key] = self.d.get(key, sem.n) + 16
            return self.d[key]
        def wait(self, *a, **k):
            pass
        def mm(self, *a, mark=False, **k):
            if mark:
                self.c["pe"] += 1
                return self.c["pe"]
            return None
        def tr(self, *a, mark=False, **k):
            if mark:
                self.c["pe"] += 1
                return self.c["pe"]
            return None

    cnt = {"pe": S["pe"].n, "dve": S["dve"].n, "act": S["act"].n, "pool": S["pool"].n}
    dmac = {}
    pool_stream(CntEng("pool", cnt, dmac), True)
    sync_stream(CntEng("sync", {"sync": 0, **cnt}, dmac), True)
    pe_stream(CntEng("pe", cnt, dmac), True)
    dve_stream(CntEng("dve", cnt, dmac), True)
    act_stream(CntEng("act", cnt, dmac), True)

    M.strict = True
    M["final"] = dict(cnt)
    Eng = ctx["Eng"]; MMc = ctx["MM"]
    with nc.Block() as blk:
        @blk.gpsimd
        def _(g):
            pool_stream(Eng(g, S["pool"]), False)

        @blk.sync
        def _(sy):
            sync_stream(Eng(sy, None), False)

        @blk.tensor
        def _(t):
            pe_stream(MMc(t, S["pe"]), False)

        @blk.vector
        def _(v):
            dve_stream(Eng(v, S["dve"]), False)

        @blk.scalar
        def _(a):
            act_stream(Eng(a, S["act"]), False)

    fin = M["final"]
    for e in ("pe", "dve", "act", "pool"):
        assert S[e].n == fin[e], f"count mismatch {e}: emit={S[e].n} count={fin[e]}"
    return M


def build(NT=2048, debug_names=()):
    assert NT % 512 == 0
    NTT = NT // 512
    NCH = NT // CS
    nc = bass.Bass("TRN2", target_bir_lowering=False, debug=False)

    x0T = nc.dram_tensor("x0T", [NDC, 128, NT], F32, kind="ExternalInput").ap()
    WinB = nc.dram_tensor("WinB", [NRC, 128, NDC, 128], F32, kind="ExternalInput").ap()
    WoutB = nc.dram_tensor("WoutB", [NDC, 128, NCC, 128], F32, kind="ExternalInput").ap()
    SgB = nc.dram_tensor("SgB", [NDC, 128, NDC, 128], F32, kind="ExternalInput").ap()
    convW = nc.dram_tensor("convW", [128, 80, 4], F32, kind="ExternalInput").ap()
    convBi = nc.dram_tensor("convBi", [128, 80], F32, kind="ExternalInput").ap()
    dtBi = nc.dram_tensor("dtBi", [128, 1], F32, kind="ExternalInput").ap()
    Alog = nc.dram_tensor("Alog", [128, 1], F32, kind="ExternalInput").ap()
    Dexp = nc.dram_tensor("Dexp", [128, 64], F32, kind="ExternalInput").ap()
    nwT = nc.dram_tensor("nwT", [128, NCC], F32, kind="ExternalInput").ap()
    scal = nc.dram_tensor("scal", [1, 1], F32, kind="ExternalInput").ap()

    outT = nc.dram_tensor("outT", [NDC, 128, NT], F32, kind="ExternalOutput").ap()

    x1s = nc.dram_tensor("x1s", [NDC, 128, NT], F32).ap()
    x2s = nc.dram_tensor("x2s", [NDC, 128, NT], F32).ap()
    gates = nc.dram_tensor("gates", [NCH, 64, 128, 128], BF16).ap()
    xcs = nc.dram_tensor("xcs", [NCH, 80, 128, 128], BF16).ap()
    ygs = nc.dram_tensor("ygs", [NCH, 64, 128, 128], BF16).ap()
    bncA = nc.dram_tensor("bncA", [4, 16384 * 2], F32).ap()
    bncB = nc.dram_tensor("bncB", [6, 16384 * (NT // 128)], BF16).ap()

    dbg = {}
    if DEBUG:
        def dbgt(nm, shape, dt=F32):
            dbg[nm] = nc.dram_tensor(nm, shape, dt, kind="ExternalOutput").ap()
        if "x1" in debug_names: dbgt("dbg_x1", [NDC, 128, NT])
        if "xcs" in debug_names: dbgt("dbg_xcs", [NCH, 80, 128, 128], BF16)
        if "gates" in debug_names: dbgt("dbg_gates", [NCH, 64, 128, 128], BF16)
        if "dtv" in debug_names: dbgt("dbg_dtv", [128, NT])
        if "yg" in debug_names: dbgt("dbg_yg", [NCH, 64, 128, 128], BF16)
        if "x2" in debug_names: dbgt("dbg_x2", [NDC, 128, NT])

    st = ExitStack()

    cons = st.enter_context(nc.sbuf_tensor("cons", [128, 16], F32))
    identB = st.enter_context(nc.sbuf_tensor("identB", [128, 128], BF16))
    identF = st.enter_context(nc.sbuf_tensor("identF", [128, 128], F32))
    maskT = st.enter_context(nc.sbuf_tensor("maskT", [128, 128], F32))
    onesrowB = st.enter_context(nc.sbuf_tensor("onesrowB", [1, 128], BF16))
    onecolB = st.enter_context(nc.sbuf_tensor("onecolB", [128, 1], BF16))
    iot = st.enter_context(nc.sbuf_tensor("iot", [128, 128], mybir.dt.int32))
    cwcol = st.enter_context(nc.sbuf_tensor("cwcol", [128, 320], F32))
    cbcol = st.enter_context(nc.sbuf_tensor("cbcol", [128, 80], F32))
    nwTc = st.enter_context(nc.sbuf_tensor("nwTc", [128, NCC], F32))
    Dex = st.enter_context(nc.sbuf_tensor("Dex", [128, 64], F32))
    dtv = st.enter_context(nc.sbuf_tensor("dtv", [128, NT], F32))
    A_dt = st.enter_context(nc.sbuf_tensor("A_dt", [128, NT], F32))

    S = {}
    for nm in ["pe", "dve", "act", "pool", "ldS", "fin"]:
        S[nm] = Sem(st, nc, nm)
    marks = {}

    with nc.Block() as blk:
        @blk.sync
        def _(sy):
            s = Eng(sy, None)
            s.dma(cwcol[:], convW.rearrange("p a b -> p (a b)"), S["ldS"])
            s.dma(cbcol[:], convBi[:], S["ldS"])
            s.dma(nwTc[:], nwT[:], S["ldS"])
            s.dma(cons[:, 3:4], dtBi[:], S["ldS"])
            s.dma(cons[:, 7:8], Alog[:], S["ldS"])
            s.dma(Dex[:], Dexp[:], S["ldS"])
            s.dma(cons[:, 8:9], scal[0:1, 0:1].to_broadcast((128, 1)), S["ldS"])

        @blk.gpsimd
        def _(g):
            p = Eng(g, S["pool"])
            marks["iota"] = p.op("iota", iot[:], pattern=[[1, 128]], base=0, channel_multiplier=-1)

        @blk.vector
        def _(v):
            d = Eng(v, S["dve"])
            d.wait(S["pool"], marks["iota"])
            d.op("memset", cons[:, 0:1], 1.0)
            d.op("memset", cons[:, 2:3], EPS)
            d.op("memset", cons[:, 6:7], 1.0 / DM)
            d.op("memset", cons[:, 14:15], 1e-30)
            d.op("memset", onesrowB[:], 1.0)
            d.op("memset", onecolB[:], 1.0)
            d.op("tensor_scalar", identF[:], iot[:], 0.0, None, ALU.is_equal)
            d.op("tensor_scalar", identB[:], iot[:], 0.0, None, ALU.is_equal)
            d.op("tensor_scalar", maskT[:], iot[:], 0.0, 1.0, ALU.is_ge, ALU.subtract)
            marks["msk"] = d.op("tensor_scalar", maskT[:], maskT[:], 1e30, None, ALU.mult)

        @blk.scalar
        def _(a):
            c = Eng(a, S["act"])
            c.wait(S["ldS"], S["ldS"].n)
            c.wait(S["dve"], marks["msk"])
            c.op("activation", cons[:, 4:5], cons[:, 7:8], AF.Exp)
            c.op("activation", cons[:, 9:10], cons[:, 8:9], AF.Abs)
            c.op("activation", cons[:, 10:11], cons[:, 9:10], AF.Exp, scale=-1.0)
            marks["sp"] = c.op("activation", cons[:, 11:12], cons[:, 10:11], AF.Ln, bias=cons[:, 0:1])

        @blk.vector
        def _(v):
            d = Eng(v, S["dve"])
            d.wait(S["act"], marks["sp"])
            d.op("tensor_scalar", cons[:, 4:5], cons[:, 4:5], -1.0, None, ALU.mult)
            d.op("tensor_scalar", cons[:, 12:13], cons[:, 8:9], 0.0, None, ALU.max)
            d.op("tensor_tensor", cons[:, 13:14], cons[:, 12:13], cons[:, 11:12], ALU.add)
            d.op("tensor_scalar", cons[:, 1:2], cons[:, 13:14], 0.5, None, ALU.mult)

    def ln_phase(src, xln, tag):
        stk = ExitStack()
        xf = [stk.enter_context(nc.sbuf_tensor(f"xf{tag}{i}", [128, NT], F32)) for i in range(2)]
        xbf = stk.enter_context(nc.sbuf_tensor(f"xb{tag}", [128, NT], BF16))
        sqb = stk.enter_context(nc.sbuf_tensor(f"sq{tag}", [128, NT], BF16))
        lnr = stk.enter_context(nc.sbuf_tensor(f"lnr{tag}", [1, 2 * NT], F32))
        lnrB = stk.enter_context(nc.sbuf_tensor(f"lnrB{tag}", [1, 2 * NT], BF16))
        Abc = stk.enter_context(nc.sbuf_tensor(f"Abc{tag}", [128, NT], BF16))
        Bbc = stk.enter_context(nc.sbuf_tensor(f"Bbc{tag}", [128, NT], BF16))
        ldx = [Sem(stk, nc, f"ldx{tag}0"), Sem(stk, nc, f"ldx{tag}1")]
        ldm = {}

        pstk = ExitStack()
        psM = pstk.enter_context(nc.psum_tensor(f"psM{tag}", [1, NT], F32))
        psS = pstk.enter_context(nc.psum_tensor(f"psS{tag}", [1, NT], F32))

        pe0, dve0, act0 = S["pe"].n, S["dve"].n, S["act"].n
        mmstm = lambda dc: pe0 + dc + 1
        castm = lambda dc: dve0 + dc + 1
        sqm = lambda dc: act0 + dc + 1

        with nc.Block() as blk:
            @blk.sync
            def _(sy):
                s = Eng(sy, None)
                for dc in range(0, NDC, 2):
                    if dc >= 2:
                        s.wait(S["pe"], mmstm(dc - 2))
                    ldm[dc] = s.dma(xf[0][:], src[dc], ldx[0])

            @blk.gpsimd
            def _(sy):
                s = Eng(sy, None)
                for dc in range(1, NDC, 2):
                    if dc >= 2:
                        s.wait(S["pe"], mmstm(dc - 2))
                    ldm[dc] = s.dma(xf[1][:], src[dc], ldx[1])

            @blk.vector
            def _(v):
                d = Eng(v, S["dve"])
                for dc in range(NDC):
                    d.wait(ldx[dc % 2], ldm[dc])
                    if dc >= 1:
                        d.wait(S["pe"], mmstm(dc - 1))
                    assert d.op("tensor_copy", xbf[:], xf[dc % 2][:]) == castm(dc)

            @blk.scalar
            def _(a):
                c = Eng(a, S["act"])
                for dc in range(NDC):
                    c.wait(ldx[dc % 2], ldm[dc])
                    if dc >= 1:
                        c.wait(S["pe"], mmstm(dc - 1))
                    assert c.op("activation", sqb[:], xf[dc % 2][:], AF.Square) == sqm(dc)

            @blk.tensor
            def _(t):
                p = MM(t, S["pe"])
                for dc in range(NDC):
                    p.wait(S["dve"], castm(dc))
                    p.wait(S["act"], sqm(dc))
                    for tt in range(NTT):
                        p.mm(psM[0:1, tt * 512:(tt + 1) * 512], onecolB[:],
                             xbf[:, tt * 512:(tt + 1) * 512], dc == 0, dc == NDC - 1)
                    for tt in range(NTT):
                        mk = p.mm(psS[0:1, tt * 512:(tt + 1) * 512], onecolB[:],
                                  sqb[:, tt * 512:(tt + 1) * 512], dc == 0, dc == NDC - 1,
                                  mark=(tt == NTT - 1))
                    assert mk == mmstm(dc)

            @blk.vector
            def _(v):
                d = Eng(v, S["dve"])
                d.wait(S["pe"], mmstm(NDC - 1))
                d.op("tensor_scalar", lnr[0:1, 0:NT], psM[0:1, :], 1.0 / DM, None, ALU.mult)
                d.op("tensor_scalar", lnr[0:1, NT:2 * NT], psS[0:1, :], 1.0 / DM, None, ALU.mult)
                d.op("tensor_tensor", sqb[0:1, 0:NT], lnr[0:1, 0:NT], lnr[0:1, 0:NT], ALU.mult)
                marks["var"] = d.op("tensor_tensor", lnr[0:1, NT:2 * NT], lnr[0:1, NT:2 * NT],
                                    sqb[0:1, 0:NT], ALU.subtract)

            @blk.scalar
            def _(a):
                c = Eng(a, S["act"])
                c.wait(S["dve"], marks["var"])
                c.op("activation", lnr[0:1, NT:2 * NT], lnr[0:1, NT:2 * NT], AF.Ln, bias=cons[0:1, 2:3])
                marks["rstd"] = c.op("activation", lnr[0:1, NT:2 * NT], lnr[0:1, NT:2 * NT], AF.Exp, scale=-0.5)

            @blk.vector
            def _(v):
                d = Eng(v, S["dve"])
                d.wait(S["act"], marks["rstd"])
                d.op("tensor_tensor", lnr[0:1, 0:NT], lnr[0:1, 0:NT], lnr[0:1, NT:2 * NT], ALU.mult)
                d.op("tensor_scalar", lnr[0:1, 0:NT], lnr[0:1, 0:NT], -1.0, None, ALU.mult)
                d.op("tensor_copy", lnrB[0:1, 0:NT], lnr[0:1, NT:2 * NT])
                marks["rows"] = d.op("tensor_copy", lnrB[0:1, NT:2 * NT], lnr[0:1, 0:NT])

        pstk.close()
        psBA = stk.enter_context(nc.psum_tensor(f"psBA{tag}", [128, NT], F32))
        psBB = stk.enter_context(nc.psum_tensor(f"psBB{tag}", [128, NT], F32))
        dve2 = S["dve"].n
        aplm = lambda dc: dve2 + 2 + 2 * (dc + 1)

        with nc.Block() as blk:
            @blk.sync
            def _(sy):
                s = Eng(sy, None)
                for dc in range(0, NDC, 2):
                    if dc >= 2:
                        s.wait(S["dve"], aplm(dc - 2))
                    ldm[(1, dc)] = s.dma(xf[0][:], src[dc], ldx[0])

            @blk.gpsimd
            def _(sy):
                s = Eng(sy, None)
                for dc in range(1, NDC, 2):
                    if dc >= 2:
                        s.wait(S["dve"], aplm(dc - 2))
                    ldm[(1, dc)] = s.dma(xf[1][:], src[dc], ldx[1])

            @blk.tensor
            def _(t):
                p = MM(t, S["pe"])
                p.wait(S["dve"], marks["rows"])
                for tt in range(NTT):
                    mka = p.mm(psBA[:, tt * 512:(tt + 1) * 512], onesrowB[0:1, :],
                               lnrB[0:1, tt * 512:(tt + 1) * 512], True, True, mark=(tt == NTT - 1))
                for tt in range(NTT):
                    mkb = p.mm(psBB[:, tt * 512:(tt + 1) * 512], onesrowB[0:1, :],
                               lnrB[0:1, NT + tt * 512:NT + (tt + 1) * 512], True, True, mark=(tt == NTT - 1))
                marks["bcA"], marks["bcB"] = mka, mkb

            @blk.vector
            def _(v):
                d = Eng(v, S["dve"])
                d.wait(S["pe"], marks["bcA"])
                d.op("tensor_copy", Abc[:, :], psBA[:, :])
                d.wait(S["pe"], marks["bcB"])
                d.op("tensor_copy", Bbc[:, :], psBB[:, :])
                for dc in range(NDC):
                    d.wait(ldx[dc % 2], ldm[(1, dc)])
                    d.op("tensor_tensor", xf[dc % 2][:], xf[dc % 2][:], Abc[:], ALU.mult)
                    assert d.op("tensor_tensor", xln[:, dc * NT:(dc + 1) * NT], xf[dc % 2][:],
                                Bbc[:], ALU.add) == aplm(dc)
        stk.close()

    def ffn_phase(xln, res, dst, tag, dbg_out=None):
        stk = ExitStack()
        sgb = [stk.enter_context(nc.sbuf_tensor(f"sg{tag}{i}", [128, NDC * 128], BF16)) for i in range(2)]
        xr = [stk.enter_context(nc.sbuf_tensor(f"xr{tag}{i}", [128, NT], F32)) for i in range(2)]
        x1t = [stk.enter_context(nc.sbuf_tensor(f"x1t{tag}{i}", [128, NT], F32)) for i in range(2)]
        psF = [stk.enter_context(nc.psum_tensor(f"psF{tag}{i}", [128, NT], F32)) for i in range(2)]
        ldw = [Sem(stk, nc, f"ldw{tag}0"), Sem(stk, nc, f"ldw{tag}1")]
        ldr = [Sem(stk, nc, f"ldr{tag}0"), Sem(stk, nc, f"ldr{tag}1")]
        wr = [Sem(stk, nc, f"wr{tag}0"), Sem(stk, nc, f"wr{tag}1")]
        lwm, lrm = {}, {}
        nst = 2 if dbg_out is not None else 1

        pe0, dve0 = S["pe"].n, S["dve"].n
        mmfc = lambda oc: pe0 + oc + 1
        evm = lambda oc: dve0 + oc + 1
        wrm = lambda oc: 16 * nst * (oc // 2 + 1)

        with nc.Block() as blk:
            @blk.gpsimd
            def _(g):
                p = Eng(g, S["pool"])
                for oc in range(NDC):
                    if oc >= 2:
                        p.wait(S["pe"], mmfc(oc - 2))
                    lwm[oc] = p.dma(sgb[oc % 2][:], SgB[oc].rearrange("p a b -> p (a b)"), ldw[oc % 2])

            @blk.sync
            def _(sy):
                s = Eng(sy, None)
                for oc in range(NDC):
                    if oc >= 2:
                        s.wait(S["dve"], evm(oc - 2))
                    lrm[oc] = s.dma(xr[oc % 2][:], res[oc], ldr[oc % 2])

            @blk.tensor
            def _(t):
                p = MM(t, S["pe"])
                for oc in range(NDC):
                    p.wait(ldw[oc % 2], lwm[oc])
                    if oc >= 2:
                        p.wait(S["dve"], evm(oc - 2))
                    for ic in range(NDC):
                        for tt in range(NTT):
                            mk = p.mm(psF[oc % 2][:, tt * 512:(tt + 1) * 512],
                                      sgb[oc % 2][:, ic * 128:(ic + 1) * 128],
                                      xln[:, ic * NT + tt * 512:ic * NT + (tt + 1) * 512],
                                      ic == 0, ic == NDC - 1,
                                      mark=(ic == NDC - 1 and tt == NTT - 1))
                    assert mk == mmfc(oc)

            @blk.vector
            def _(v):
                d = Eng(v, S["dve"])
                for oc in range(NDC):
                    d.wait(S["pe"], mmfc(oc))
                    d.wait(ldr[oc % 2], lrm[oc])
                    if oc >= 2:
                        d.wait(wr[oc % 2], wrm(oc - 2))
                    assert d.op("scalar_tensor_tensor", x1t[oc % 2][:], psF[oc % 2][:],
                                cons[:, 1:2], xr[oc % 2][:], ALU.mult, ALU.add) == evm(oc)

            @blk.scalar
            def _(a):
                c = Eng(a, S["act"])
                for oc in range(NDC):
                    c.wait(S["dve"], evm(oc))
                    c.dma(dst[oc], x1t[oc % 2][:], wr[oc % 2])
                    if dbg_out is not None:
                        c.dma(dbg_out[oc], x1t[oc % 2][:], wr[oc % 2])
                c.wait(wr[0], wr[0].n)
                c.wait(wr[1], wr[1].n)
        stk.close()


    def proj_phase(xln):
        stk = ExitStack()
        wb = [stk.enter_context(nc.sbuf_tensor(f"wbp{i}", [128, NDC * 128], BF16)) for i in range(2)]
        gevac = [stk.enter_context(nc.sbuf_tensor(f"gev{i}", [128, NT], BF16)) for i in range(2)]
        xevac = [stk.enter_context(nc.sbuf_tensor(f"xev{i}", [128, NT], BF16)) for i in range(2)]
        convraw = stk.enter_context(nc.sbuf_tensor("convraw", [128, NT + 3], F32))
        convacc = stk.enter_context(nc.sbuf_tensor("convacc", [128, NT], F32))
        dtraw = stk.enter_context(nc.sbuf_tensor("dtraw", [128, NT], F32))
        psW = [stk.enter_context(nc.psum_tensor(f"psW{i}", [128, NT], F32)) for i in range(2)]
        ldwB = [Sem(stk, nc, f"ldwB{i}") for i in range(2)]
        wrG = [Sem(stk, nc, f"wrG{i}") for i in range(2)]
        wrX = [Sem(stk, nc, f"wrX{i}") for i in range(2)]
        lwm = {}

        pe0, dve0, act0 = S["pe"].n, S["dve"].n, S["act"].n
        mmp = lambda rc: pe0 + rc + 1
        evbase = {}
        cum = dve0 + 1
        for rc in range(NRC):
            evbase[rc] = cum
            cum += 1 if (rc < 64 or rc == 144) else 5
        evW = lambda rc: evbase[rc] + 1
        convm = lambda rc: evbase[rc] + 5
        silm = lambda crc: act0 + crc + 1
        wrGm, wrXm = {}, {}
        cG, cX = [0, 0], [0, 0]
        for rc in range(64):
            cG[rc % 2] += 16 * NCH
            wrGm[rc] = cG[rc % 2]
        for crc in range(80):
            cX[crc % 2] += 16 * NCH
            wrXm[crc] = cX[crc % 2]

        with nc.Block() as blk:
            @blk.gpsimd
            def _(g):
                p = Eng(g, S["pool"])
                for rc in range(NRC):
                    if rc >= 2:
                        p.wait(S["pe"], mmp(rc - 2))
                    lwm[rc] = p.dma(wb[rc % 2][:], WinB[rc].rearrange("p a b -> p (a b)"), ldwB[rc % 2])

            @blk.tensor
            def _(t):
                p = MM(t, S["pe"])
                for rc in range(NRC):
                    p.wait(ldwB[rc % 2], lwm[rc])
                    if rc >= 2:
                        p.wait(S["dve"], evW(rc - 2))
                    for dc in range(NDC):
                        for tt in range(NTT):
                            mk = p.mm(psW[rc % 2][:, tt * 512:(tt + 1) * 512],
                                      wb[rc % 2][:, dc * 128:(dc + 1) * 128],
                                      xln[:, dc * NT + tt * 512:dc * NT + (tt + 1) * 512],
                                      dc == 0, dc == NDC - 1,
                                      mark=(dc == NDC - 1 and tt == NTT - 1))
                    assert mk == mmp(rc)

            @blk.vector
            def _(v):
                d = Eng(v, S["dve"])
                d.op("memset", convraw[:, 0:3], 0.0)
                for rc in range(NRC):
                    d.wait(S["pe"], mmp(rc))
                    if rc < 64:
                        if rc >= 2:
                            d.wait(wrG[rc % 2], wrGm[rc - 2])
                        assert d.op("tensor_copy", gevac[rc % 2][:], psW[rc % 2][:]) == evW(rc)
                    elif rc < 144:
                        crc = rc - 64
                        if crc >= 1:
                            d.wait(S["act"], silm(crc - 1))
                        assert d.op("tensor_copy", convraw[:, 3:3 + NT], psW[rc % 2][:]) == evW(rc)
                        d.op("tensor_scalar", convacc[:], convraw[:, 0:NT],
                             cwcol[:, crc * 4:crc * 4 + 1], None, ALU.mult)
                        for kk in range(1, 4):
                            mk = d.op("scalar_tensor_tensor", convacc[:], convraw[:, kk:kk + NT],
                                      cwcol[:, crc * 4 + kk:crc * 4 + kk + 1], convacc[:],
                                      ALU.mult, ALU.add)
                        assert mk == convm(rc)
                    else:
                        assert d.op("tensor_copy", dtraw[:], psW[rc % 2][:]) == evW(rc)
                d.op("tensor_scalar", dtraw[:], dtraw[:], cons[:, 3:4], None, ALU.add)
                marks["dt0"] = S["dve"].n + 1
                d.wait(S["act"], act0 + 80)
                assert d.op("tensor_scalar", convacc[:], dtraw[:], 0.0, None, ALU.max) == marks["dt0"]
                d.wait(S["act"], act0 + 80 + 3)
                d.op("tensor_tensor", dtv[:], convacc[:], convraw[:, 3:3 + NT], ALU.add)
                marks["adt"] = d.op("tensor_scalar", A_dt[:], dtv[:], cons[:, 4:5], None, ALU.mult)

            @blk.scalar
            def _(a):
                c = Eng(a, S["act"])
                for crc in range(80):
                    c.wait(S["dve"], convm(crc + 64))
                    if crc >= 2:
                        c.wait(wrX[crc % 2], wrXm[crc - 2])
                    assert c.op("activation", xevac[crc % 2][:], convacc[:], AF.Silu,
                                bias=cbcol[:, crc:crc + 1]) == silm(crc)
                    c.wait(S["act"], silm(crc))
                    for cb in range(NCH):
                        c.dma(xcs[cb, crc], xevac[crc % 2][:, cb * 128:(cb + 1) * 128], wrX[crc % 2])
                c.wait(S["dve"], marks["dt0"])
                c.op("activation", convraw[:, 3:3 + NT], dtraw[:], AF.Abs)
                c.op("activation", convraw[:, 3:3 + NT], convraw[:, 3:3 + NT], AF.Exp, scale=-1.0)
                c.op("activation", convraw[:, 3:3 + NT], convraw[:, 3:3 + NT], AF.Ln, bias=cons[:, 0:1])
                c.wait(wrX[0], wrX[0].n)
                c.wait(wrX[1], wrX[1].n)

            @blk.sync
            def _(sy):
                s = Eng(sy, None)
                for rc in range(64):
                    s.wait(S["dve"], evW(rc))
                    for cb in range(NCH):
                        s.dma(gates[cb, rc], gevac[rc % 2][:, cb * 128:(cb + 1) * 128], wrG[rc % 2])
                s.wait(S["dve"], marks["adt"])
                s.wait(wrG[0], wrG[0].n)
                s.wait(wrG[1], wrG[1].n)
        stk.close()


    def phase_d():
        stk = ExitStack()
        ygh = stk.enter_context(nc.sbuf_tensor("ygh", [128, NCC * (NT // 2)], BF16))
        wob = [stk.enter_context(nc.sbuf_tensor(f"wob{i}", [128, NCC * 128], BF16)) for i in range(2)]
        xr = [stk.enter_context(nc.sbuf_tensor(f"xrd{i}", [128, NT // 2], F32)) for i in range(2)]
        x2t = [stk.enter_context(nc.sbuf_tensor(f"x2t{i}", [128, NT // 2], F32)) for i in range(2)]
        psD = [stk.enter_context(nc.psum_tensor(f"psD{i}", [128, NT // 2], F32)) for i in range(2)]
        ldy = Sem(stk, nc, "ldy")
        ldwD = [Sem(stk, nc, f"ldwD{i}") for i in range(2)]
        ldxD = [Sem(stk, nc, f"ldxD{i}") for i in range(2)]
        wrD = [Sem(stk, nc, f"wrD{i}") for i in range(2)]
        HT = NT // 2
        NH2 = NCH // 2
        TS = min(512, HT)
        NT2 = HT // TS

        for hf in range(2):
            lym, lwm, lxm = {}, {}, {}
            pe0, dve0 = S["pe"].n, S["dve"].n
            wrb = (wrD[0].n, wrD[1].n)
            nstD = 2 if "dbg_x2" in dbg else 1
            mmdc = lambda dc: pe0 + dc + 1
            evdc = lambda dc: dve0 + dc + 1
            with nc.Block() as blk:
                @blk.sync
                def _(sy):
                    s = Eng(sy, None)
                    for cc in range(NCC):
                        k = s.dma(ygh[:, cc * HT:(cc + 1) * HT].rearrange("p (a t) -> p a t", a=NH2),
                                  ygs[hf * NH2:(hf + 1) * NH2, cc].rearrange("a p t -> p a t"), ldy)
                    lym[0] = k
                    for dc in range(NDC):
                        if dc >= 2:
                            s.wait(S["dve"], evdc(dc - 2))
                        lxm[dc] = s.dma(xr[dc % 2][:], x1s[dc][:, hf * HT:(hf + 1) * HT], ldxD[dc % 2])

                @blk.gpsimd
                def _(g):
                    p = Eng(g, S["pool"])
                    for dc in range(NDC):
                        if dc >= 2:
                            p.wait(S["pe"], mmdc(dc - 2))
                        lwm[dc] = p.dma(wob[dc % 2][:], WoutB[dc].rearrange("p a b -> p (a b)"), ldwD[dc % 2])

                @blk.tensor
                def _(t):
                    p = MM(t, S["pe"])
                    p.wait(ldy, lym[0])
                    for dc in range(NDC):
                        p.wait(ldwD[dc % 2], lwm[dc])
                        if dc >= 2:
                            p.wait(S["dve"], evdc(dc - 2))
                        for cc in range(NCC):
                            for tt in range(NT2):
                                mk = p.mm(psD[dc % 2][:, tt * TS:(tt + 1) * TS],
                                          wob[dc % 2][:, cc * 128:(cc + 1) * 128],
                                          ygh[:, cc * HT + tt * TS:cc * HT + (tt + 1) * TS],
                                          cc == 0, cc == NCC - 1,
                                          mark=(cc == NCC - 1 and tt == NT2 - 1))
                        assert mk == mmdc(dc)

                @blk.vector
                def _(v):
                    d = Eng(v, S["dve"])
                    for dc in range(NDC):
                        d.wait(S["pe"], mmdc(dc))
                        d.wait(ldxD[dc % 2], lxm[dc])
                        if dc >= 2:
                            d.wait(wrD[dc % 2], wrb[dc % 2] + 16 * nstD * ((dc - 2) // 2 + 1))
                        assert d.op("tensor_tensor", x2t[dc % 2][:], psD[dc % 2][:], xr[dc % 2][:],
                                    ALU.add) == evdc(dc)

                @blk.scalar
                def _(a):
                    c = Eng(a, S["act"])
                    for dc in range(NDC):
                        c.wait(S["dve"], evdc(dc))
                        c.dma(x2s[dc][:, hf * HT:(hf + 1) * HT], x2t[dc % 2][:], wrD[dc % 2])
                        if "dbg_x2" in dbg:
                            c.dma(dbg["dbg_x2"][dc][:, hf * HT:(hf + 1) * HT], x2t[dc % 2][:], wrD[dc % 2])
                    c.wait(wrD[0], wrD[0].n)
                    c.wait(wrD[1], wrD[1].n)
            for sm in (ldy, ldwD[0], ldwD[1], ldxD[0], ldxD[1], wrD[0], wrD[1]):
                pass
        stk.close()

    stkAB = ExitStack()
    xln = stkAB.enter_context(nc.sbuf_tensor("xlnAB", [128, NDC * NT], BF16))

    ln_phase(x0T, xln, "a")
    ffn_phase(xln, x0T, x1s, "fa", dbg_out=dbg.get("dbg_x1"))
    ln_phase(x1s, xln, "b")
    proj_phase(xln)
    stkAB.close()

    stkC = ExitStack()
    ctxC = dict(NT=NT, NCH=NCH, cons=cons, identF=identF, identB=identB, maskT=maskT,
                onesrowB=onesrowB, onecolB=onecolB, iot=iot, nwTc=nwTc, Dex=Dex, dtv=dtv, A_dt=A_dt,
                gates=gates, xcs=xcs, ygs=ygs, bncA=bncA, bncB=bncB, Eng=Eng, MM=MM)
    emit_phase_c(nc, S, Sem, ctxC, stkC)
    stkC.close()

    phase_d()

    stkE = ExitStack()
    xlnE = stkE.enter_context(nc.sbuf_tensor("xlnE", [128, NDC * NT], BF16))
    ln_phase(x2s, xlnE, "e")
    ffn_phase(xlnE, x2s, outT, "fe")
    stkE.close()

    if DEBUG and ("gates" in debug_names or "xcs" in debug_names or "dtv" in debug_names):
        with nc.Block() as blk:
            @blk.sync
            def _(sy):
                s = Eng(sy, None)
                if "gates" in debug_names:
                    s.dma(dbg["dbg_gates"].rearrange("a b p t -> (a b) (p t)"),
                          gates.rearrange("a b p t -> (a b) (p t)"), S["fin"])
                if "xcs" in debug_names:
                    s.dma(dbg["dbg_xcs"].rearrange("a b p t -> (a b) (p t)"),
                          xcs.rearrange("a b p t -> (a b) (p t)"), S["fin"])
                s.wait(S["fin"], S["fin"].n)
        if "dtv" in debug_names:
            with nc.Block() as blk:
                @blk.sync
                def _(sy):
                    s = Eng(sy, None)
                    s.dma(dbg["dbg_dtv"], dtv[:], S["fin"])
                    s.wait(S["fin"], S["fin"].n)

    if DEBUG and "yg" in debug_names:
        with nc.Block() as blk:
            @blk.sync
            def _(sy):
                s = Eng(sy, None)
                s.dma(dbg["dbg_yg"].rearrange("a b p t -> (a b) (p t)"),
                      ygs.rearrange("a b p t -> (a b) (p t)"), S["fin"])
                s.wait(S["fin"], S["fin"].n)

    st.close()
    return nc, dbg


def stage_inputs(hs_b, in_proj_w, out_proj_w, sign, conv_w, conv_b, dt_bias, A_log, Dv, norm_w, scale, NT=2048):
    m = {}
    m["x0T"] = np.ascontiguousarray(hs_b[:NT].T).reshape(NDC, 128, NT)
    m["WinB"] = np.ascontiguousarray(in_proj_w.reshape(NRC, 128, NDC, 128).transpose(0, 3, 2, 1))
    m["WoutB"] = np.ascontiguousarray(out_proj_w.reshape(NDC, 128, NCC, 128).transpose(0, 3, 2, 1))
    m["SgB"] = np.ascontiguousarray(sign.reshape(NDC, 128, NDC, 128).transpose(2, 1, 0, 3))
    m["convW"] = np.ascontiguousarray(conv_w.reshape(80, 128, 4).transpose(1, 0, 2))
    m["convBi"] = np.ascontiguousarray(conv_b.reshape(80, 128).T)
    m["dtBi"] = np.ascontiguousarray(dt_bias.reshape(128, 1))
    m["Alog"] = np.ascontiguousarray(A_log.reshape(128, 1))
    m["Dexp"] = np.ascontiguousarray(
        np.concatenate([np.broadcast_to(Dv[0::2], (64, 64)), np.broadcast_to(Dv[1::2], (64, 64))], axis=0))
    m["nwT"] = np.ascontiguousarray(norm_w.reshape(NCC, 128).T)
    m["scal"] = np.asarray(scale, np.float32).reshape(1, 1)
    return m



def kernel(L0_attn_hidden_states, L0_attn_in_proj_w, L0_attn_conv_w, L0_attn_conv_b,
           L0_attn_dt_bias, L0_attn_A_log, L0_attn_D, L0_attn_norm_w, L0_attn_out_proj_w,
           L0_ffn_sign, L0_ffn_scale):
    from concourse.bass_utils import run_bass_kernel_spmd
    hs = np.asarray(L0_attn_hidden_states, np.float32)
    NT = hs.shape[1]
    nc, _ = build(NT)
    base = stage_inputs(hs[0], np.asarray(L0_attn_in_proj_w, np.float32),
                        np.asarray(L0_attn_out_proj_w, np.float32),
                        np.asarray(L0_ffn_sign, np.float32),
                        np.asarray(L0_attn_conv_w, np.float32),
                        np.asarray(L0_attn_conv_b, np.float32),
                        np.asarray(L0_attn_dt_bias, np.float32),
                        np.asarray(L0_attn_A_log, np.float32),
                        np.asarray(L0_attn_D, np.float32),
                        np.asarray(L0_attn_norm_w, np.float32),
                        np.asarray(L0_ffn_scale, np.float32), NT=NT)
    in_maps = []
    for b in range(8):
        m = dict(base)
        m["x0T"] = np.ascontiguousarray(hs[b].T).reshape(NDC, 128, NT)
        in_maps.append(m)
    res = run_bass_kernel_spmd(nc, in_maps, core_ids=list(range(8)))
    out = np.stack([np.ascontiguousarray(
        res.results[b]["outT"].reshape(DM, NT).T) for b in range(8)])
    return out.astype(np.float32)


# version 11
# speedup vs baseline: 1.0392x; 1.0392x over previous
import numpy as np
from contextlib import ExitStack

import concourse.bass as bass
import concourse.mybir as mybir

F32 = mybir.dt.float32
BF16 = mybir.dt.bfloat16
ALU = mybir.AluOpType
AF = mybir.ActivationFunctionType

DM = 4096
NDC = 32
NRC = 145
NCC = 64
CS = 128
EPS = 1e-5

DEBUG = False


class Sem:
    def __init__(self, stack, nc, name):
        self.h = stack.enter_context(nc.semaphore(name))
        self.n = 0

    def add(self, k):
        self.n += k
        return self.n


class Eng:
    def __init__(self, eng, selfsem):
        self.e = eng
        self.s = selfsem

    def wait(self, sem, n):
        self.e.wait_ge(sem.h, n)

    def op(self, name, *args, dep=True, **kw):
        if dep is True:
            if self.s.n:
                self.e.wait_ge(self.s.h, self.s.n)
        elif dep is not None and dep > 0:
            self.e.wait_ge(self.s.h, dep)
        inst = getattr(self.e, name)(*args, **kw)
        inst.then_inc(self.s.h, 1)
        return self.s.add(1)

    def dma(self, out, in_, sem, **kw):
        inst = self.e.dma_start(out=out, in_=in_, **kw)
        inst.then_inc(sem.h, 16)
        return sem.add(16)


class MM:
    def __init__(self, eng, sem):
        self.e = eng
        self.s = sem

    def wait(self, sem, n):
        self.e.wait_ge(sem.h, n)

    def mm(self, out, lhsT, rhs, start, stop, mark=False, **kw):
        inst = self.e.matmul(out, lhsT, rhs, start=start, stop=stop, **kw)
        if mark:
            inst.then_inc(self.s.h, 1)
            return self.s.add(1)
        return None

    def tr(self, out, in_, ident, mark=False):
        inst = self.e.transpose(out, in_, ident)
        if mark:
            inst.then_inc(self.s.h, 1)
            return self.s.add(1)
        return None




def emit_phase_c(nc, S, Sem, ctx, stk):
    NT = ctx["NT"]; NCH = ctx["NCH"]
    cons = ctx["cons"]; identF = ctx["identF"]; identB = ctx["identB"]; maskT = ctx["maskT"]
    onesrowB = ctx["onesrowB"]; onecolB = ctx["onecolB"]; iot = ctx["iot"]
    nwTc = ctx["nwTc"]; Dex = ctx["Dex"]; dtv = ctx["dtv"]; Acol = ctx["Acol"]
    gates = ctx["gates"]; xcs = ctx["xcs"]; ygs = ctx["ygs"]; bncA = ctx["bncA"]; bncB = ctx["bncB"]

    sb = nc.sbuf_tensor
    xcb = [stk.enter_context(sb(f"xcb{i}", [128, 80 * 128], BF16)) for i in range(2)]
    gtb = stk.enter_context(sb("gtb", [128, 16 * 128], BF16))
    acs = [stk.enter_context(sb(f"acs{i}", [128, CS], F32)) for i in range(2)]
    decp = stk.enter_context(sb("decp", [128, CS], F32))
    adt_t = stk.enter_context(sb("adt_t", [128, CS], F32))
    sdf = stk.enter_context(sb("sdf", [128, CS], F32))
    decf = stk.enter_context(sb("decf", [128, CS], F32))
    dlc = [stk.enter_context(sb(f"dlc{i}", [128, 1], F32)) for i in range(2)]
    dl2 = [stk.enter_context(sb(f"dl2{i}", [2, 64], F32)) for i in range(2)]
    half01 = stk.enter_context(sb("half01", [2, 128], F32))
    i2 = stk.enter_context(sb("i2", [2, 128], mybir.dt.int32))
    h0t = stk.enter_context(sb("h0t", [2, 128], F32))
    sdT = stk.enter_context(sb("sdT", [128, 128], F32))
    decayT = stk.enter_context(sb("decayT", [128, 128], F32))
    dtT = stk.enter_context(sb("dtT", [128, 128], F32))
    ddtT = stk.enter_context(sb("ddtT", [128, 128], F32))
    dexpA = stk.enter_context(sb("dexpA", [128, 64 * NCH], F32))
    ahb = stk.enter_context(sb("ahb", [128, CS], BF16))
    ah32 = stk.enter_context(sb("ah32", [128, CS], F32))
    al32 = stk.enter_context(sb("al32", [128, CS], F32))
    albf = stk.enter_context(sb("albf", [128, CS], BF16))
    nahb = stk.enter_context(sb("nahb", [128, CS], BF16))
    nalb = stk.enter_context(sb("nalb", [128, CS], BF16))
    lhsT4 = stk.enter_context(sb("lhsT4", [4, 4096], BF16))
    rhs4 = stk.enter_context(sb("rhs4", [4, 4096], BF16))
    Lex = [stk.enter_context(sb(f"Lex{i}", [128, 1024], BF16)) for i in range(2)]
    Lt = stk.enter_context(sb("Lt", [128, 1024], F32))
    tril01 = stk.enter_context(sb("tril01", [128, 128], BF16))
    Msb = [stk.enter_context(sb(f"Msb{i}", [128, 1024], BF16)) for i in range(2)]
    xdt = [stk.enter_context(sb(f"xdt{i}", [128, 512], BF16)) for i in range(2)]
    xdd = [stk.enter_context(sb(f"xdd{i}", [128, 512], BF16)) for i in range(2)]
    Zs = [stk.enter_context(sb(f"Zsb{i}", [128, 512], F32)) for i in range(2)]
    BTsb = stk.enter_context(sb("BTsb", [128, 8 * 128], BF16))
    Gsb = stk.enter_context(sb("Gsb", [128, 8 * 128], BF16))
    Ssb = stk.enter_context(sb("Ssb", [128, 64 * 128], F32))
    Sbfst = [stk.enter_context(sb("Sbfs0", [128, 1024], BF16))] * 2
    STsb = stk.enter_context(sb("STsb", [128, 64 * 128], BF16))
    Ybf = stk.enter_context(sb("Ybf", [128, 64 * 128], BF16))
    rtmp = stk.enter_context(sb("rtmp", [128, 512], F32))
    dtmp = None
    y2st = [stk.enter_context(sb(f"y2st{i}", [128, 1024], BF16)) for i in range(2)]
    sgst = [stk.enter_context(sb("sgst0", [128, 1024], BF16))] * 2
    t1st = [stk.enter_context(sb(f"t1st{i}", [128, 1024], BF16)) for i in range(2)]
    ygst = [stk.enter_context(sb("ygst0", [128, 1024], BF16))] * 2
    rrow = stk.enter_context(sb("rrow", [1, 128], F32))
    rrowB = stk.enter_context(sb("rrowB", [1, 128], BF16))
    RSTD = stk.enter_context(sb("RSTD", [128, 128], F32))
    zeroB = stk.enter_context(sb("zeroB", [128, 128], F32))

    psL = stk.enter_context(nc.psum_tensor("psL", [128, 1024], F32))
    psY = [stk.enter_context(nc.psum_tensor(f"psYc{i}", [128, 512], F32)) for i in range(2)]
    psZ = stk.enter_context(nc.psum_tensor("psZ", [128, 512], F32))
    psSt = [stk.enter_context(nc.psum_tensor(f"psSt{i}", [128, 512], F32)) for i in range(2)]
    psT = [stk.enter_context(nc.psum_tensor("psTc0", [128, 1024], BF16))] * 2

    ldc = [Sem(stk, nc, f"ldc{i}") for i in range(2)]
    ldg = Sem(stk, nc, "ldg")
    bnc = Sem(stk, nc, "bnc")
    rla = Sem(stk, nc, "rla")
    rdl = [Sem(stk, nc, "rdl0"), Sem(stk, nc, "rdl1")]
    bncd = Sem(stk, nc, "bncd")
    wry = [Sem(stk, nc, f"wry{i}") for i in range(2)]

    class Marks(dict):
        strict = False
        def __missing__(self, key):
            if self.strict:
                raise KeyError(key)
            return 0
    M = Marks()

    def pool_stream(p, rec):
        k = p.op("iota", i2[:], pattern=[[1, 128]], base=0, channel_multiplier=-64)
        if rec: M["i2"] = k
        k = p.dma(xcb[0][:].rearrange("p (a t) -> p a t", a=80), xcs[0].rearrange("a p t -> p a t"), ldc[0])
        if rec: M["ldx", 0] = k
        for c in range(NCH):
            if c + 1 < NCH:
                if c >= 1:
                    p.wait(S["pe"], M["rnd_end", c - 1])
                    p.wait(S["dve"], M["yev", c - 1, 15])
                k = p.dma(xcb[(c + 1) % 2][:].rearrange("p (a t) -> p a t", a=80), xcs[c + 1].rearrange("a p t -> p a t"), ldc[(c + 1) % 2])
                if rec: M["ldx", c + 1] = k
            for h in range(4):
                if (c, h) != (0, 0):
                    pc, ph = (c, h * 2 - 1) if h >= 1 else (c - 1, 7)
                    p.wait(S["act"], M["silu", pc, ph])
                k = p.dma(gtb[:].rearrange("p (a t) -> p a t", a=16),
                          gates[c, h * 16:(h + 1) * 16].rearrange("a p t -> p a t"), ldg)
                if rec: M["ldg", c, h] = k

    def sync_stream(s, rec):
        s.wait(S["dve"], M["onesm"])
        k = s.dma(lhsT4[2:4, :], rhs4[0:2, :], rla)
        if rec: M["onesr"] = k
        s.wait(rla, M["onesr"])
        for c in range(NCH):
            s.wait(S["dve"], M["presplit", c])
            off = c * 128 * CS
            s.dma(bncB[0, off:off + 128 * CS].rearrange("(p l) -> p l", p=128), nahb[:], bnc)
            s.dma(bncB[1, off:off + 128 * CS].rearrange("(p l) -> p l", p=128), nalb[:], bnc)
            s.dma(bncB[2, off:off + 128 * CS].rearrange("(p l) -> p l", p=128), ahb[:], bnc)
            k = s.dma(bncB[3, off:off + 128 * CS].rearrange("(p l) -> p l", p=128), albf[:], bnc)
            if rec: M["bncw", c] = k
            s.wait(S["act"], M["predl", c])
            k = s.dma(bncA[2, c * 128:(c + 1) * 128].rearrange("(p o) -> p o", p=128), dlc[c % 2][:], bncd)
            if rec: M["bncdw", c] = k
            s.wait(bncd, M["bncdw", c])
            if c >= 2:
                s.wait(S["pe"], M["predmm", c - 2])
            k = s.dma(dl2[c % 2][:], bncA[2, c * 128:(c + 1) * 128].rearrange("(c q) -> q c", q=2),
                      rdl[c % 2], allow_slow_non_contiguous=True)
            if rec: M["rldl", c] = k
        for c in range(NCH):
            for q in range(4):
                s.wait(bnc, M["bncw", c])
                if (c, q) != (0, 0):
                    pc, pq = (c, q - 1) if q > 0 else (c - 1, 3)
                    s.wait(S["pe"], M["lmm", pc, pq * 4 + 3])
                off = c * 128 * CS + q * 4096
                s.dma(lhsT4[0:2, :], bncB[0:2, off:off + 4096], rla)
                k = s.dma(rhs4[2:4, :], bncB[2:4, off:off + 4096], rla)
                if rec: M["rlq", c, q] = k
            for gr in range(8):
                s.wait(S["dve"], M["yg", c, gr])
                for j in range(8):
                    k = s.dma(ygs[c, gr * 8 + j], ygst[0][:, j * 128:(j + 1) * 128], wry[gr % 2])
                if rec: M["wyg", c, gr] = k
        s.wait(wry[0], wry[0].n if not rec else M["wyg", NCH - 1, 6])
        s.wait(wry[1], wry[1].n if not rec else M["wyg", NCH - 1, 7])

    def pe_stream(p, rec):
        for c in range(NCH):
            p.wait(rdl[c % 2], M["rldl", c])
            k = p.mm(psZ[:, (c % 8) * 64:(c % 8) * 64 + 64], half01[:], dl2[c % 2][:], True, True,
                     mark=True)
            if rec: M["predmm", c] = k
            if rec and (c % 8 == 7 or c == NCH - 1):
                M["predexp", c // 8] = k
            if c % 8 == 7 and c + 1 < NCH:
                p.wait(S["dve"], M["dexpevA", c // 8])
        for c in range(NCH):
            p.wait(ldc[c % 2], M["ldx", c])
            if c >= 1:
                p.wait(S["dve"], M["yev", c - 1, 15])
            for g in range(8):
                k = p.mm(psY[g // 4][:, (g % 4) * 128:(g % 4 + 1) * 128],
                         xcb[c % 2][:, (64 + g) * 128:(65 + g) * 128],
                         xcb[c % 2][:, (72 + g) * 128:(73 + g) * 128],
                         True, True, mark=(g in (3, 7)))
                if rec and g == 3: M["gmm", c, 0] = k
                if rec and g == 7: M["gmm", c, 1] = k
            p.wait(S["act"], M["sd", c])
            if c >= 1:
                p.wait(S["dve"], M["rstdev", c - 1])
            p.tr(psSt[0][:, 0:128], sdf[:], identF[:])
            p.tr(psSt[0][:, 128:256], decf[:], identF[:])
            k = p.tr(psSt[0][:, 256:384], dtv[:, c * CS:(c + 1) * CS], identF[:], mark=True)
            if rec: M["ptr", c] = k
            if c >= 1:
                p.wait(S["dve"], M["stev", c - 1, 7])
            for g in range(8):
                k = p.tr(psT[0][:, g * 128:(g + 1) * 128],
                         xcb[c % 2][:, (64 + g) * 128:(65 + g) * 128], identB[:],
                         mark=(g == 7))
                if rec and g == 7: M["btr", c] = k
            for r in range(16):
                q = r // 4
                if c >= 1:
                    if r >= 1:
                        p.wait(S["dve"], M["zsev", c, r - 1])
                    elif c == 1:
                        p.wait(S["dve"], M["dexpfin"])
                    else:
                        p.wait(S["dve"], M["zsev", c - 1, 15])
                    g = r // 2
                    for j in range(4):
                        pp = r * 4 + j
                        k = p.mm(psZ[:, j * 128:(j + 1) * 128],
                                 xcb[c % 2][:, (72 + g) * 128:(73 + g) * 128],
                                 STsb[:, pp * 128:(pp + 1) * 128],
                                 True, True, mark=(j == 3))
                    if rec: M["zmm", c, r] = k
                p.wait(rla, M["rlq", c, q])
                if r == 0 and c == 0:
                    p.wait(rla, M["onesr"])
                if r >= 1:
                    p.wait(S["act"], M["lrelu", c, r - 1])
                elif c >= 1:
                    p.wait(S["act"], M["lrelu", c - 1, 15])
                for hh in range(8):
                    off = ((r % 4) * 8 + hh) * 128
                    col = slice(hh * 128, (hh + 1) * 128)
                    k = p.mm(psL[:, col], lhsT4[:, off:off + 128], rhs4[:, off:off + 128],
                             True, True, mark=(hh == 7))
                if rec: M["lmm", c, r] = k
                if r >= 1:
                    p.wait(S["dve"], M["xdtc", c, r - 1])
                else:
                    p.wait(S["dve"], M["btev", c])
                for j in range(4):
                    pp = r * 4 + j
                    k = p.tr(psT[0][:, j * 128:(j + 1) * 128],
                             xcb[c % 2][:, pp * 128:(pp + 1) * 128], identB[:], mark=(j == 3))
                if rec: M["xtr", c, r] = k
                if r >= 2:
                    p.wait(S["dve"], M["yev", c, r - 2])
                if r == 0:
                    p.wait(S["dve"], M["gev", c])
                if c >= 1:
                    p.wait(S["dve"], M["zsev", c, r])
                    for j in range(4):
                        k = p.mm(psY[r % 2][:, j * 128:(j + 1) * 128],
                                 Zs[r % 2][:, j * 128:(j + 1) * 128],
                                 identF[:], j == 0, False, is_transpose=True,
                                 skip_group_check=True, mark=(j == 3))
                    if rec: M["ztr", c, r] = k
                else:
                    for j in range(4):
                        k = p.mm(psY[r % 2][:, j * 128:(j + 1) * 128], zeroB[:],
                                 identF[:], j == 0, False, is_transpose=True,
                                 skip_group_check=True, mark=(j == 3))
                    if rec:
                        M["zmm", c, r] = None
                        M["ztr", c, r] = k
                p.wait(S["dve"], M["msb", c, r])
                for hh in range(8):
                    k = p.mm(psY[r % 2][(hh % 2) * 64:(hh % 2) * 64 + 64,
                                        (hh // 2) * 128:(hh // 2 + 1) * 128],
                             xdt[r % 2][:, hh * 64:(hh + 1) * 64],
                             Msb[r % 2][:, hh * 128:(hh + 1) * 128],
                             False, hh >= 6, skip_group_check=True, mark=(hh == 7))
                if rec: M["ydiag", c, r] = k
                p.wait(S["dve"], M["xdd", c, r])
                if r == 0:
                    p.wait(S["dve"], M["ptrev", c])
                    p.wait(S["dve"], M["btev", c])
                    p.wait(S["dve"], M["rstdev", c - 1] if c >= 1 else 0)
                elif r == 1:
                    p.wait(S["dve"], M["rec", c - 1, 15] if c >= 1 else 0)
                else:
                    p.wait(S["dve"], M["rec", c, r - 2])
                g = r // 2
                for j in range(4):
                    k = p.mm(psSt[r % 2][:, j * 128:(j + 1) * 128],
                             xdd[r % 2][:, j * 128:(j + 1) * 128],
                             BTsb[:, g * 128:(g + 1) * 128],
                             True, True, mark=(j == 3))
                if rec: M["smm", c, r] = k
                if rec and r == 15: M["rnd_end", c] = k
            p.wait(S["dve"], M["rec", c, 15])
            for gr in range(8):
                p.wait(S["dve"], M["y2", c, gr])
                for j in range(8):
                    k = p.mm(psSt[0][0:1, 0:128], onecolB[:],
                             y2st[0][:, j * 128:(j + 1) * 128],
                             gr == 0 and j == 0, gr == 7 and j == 7, mark=(j == 7))
                if rec: M["stat", c, gr] = k
            p.wait(S["dve"], M["rrowB", c])
            k = p.mm(psSt[0][:, 128:256], onesrowB[0:1, :], rrowB[0:1, :], True, True, mark=True)
            if rec: M["rbc", c] = k
            for sg in range(8):
                p.wait(S["act"], M["sbf", c, sg])
                if sg >= 1:
                    p.wait(S["dve"], M["stev", c, sg - 1])
                for j in range(8):
                    k = p.tr(psT[0][:, j * 128:(j + 1) * 128],
                             Sbfst[0][:, j * 128:(j + 1) * 128], identB[:], mark=(j == 7))
                if rec: M["str", c, sg] = k

    def dve_stream(d, rec):
        d.op("memset", Ssb[:], 0.0)
        d.op("memset", zeroB[:], 0.0)
        k = d.op("memset", rhs4[0:2, :], 1.0)
        M["onesm"] = k
        d.op("tensor_scalar", tril01[:], iot[:], 0.0, None, ALU.is_ge)
        d.wait(S["pool"], M["i2"])
        d.op("tensor_scalar", half01[:], i2[:], 0.0, None, ALU.is_ge)
        d.op("tensor_scalar", h0t[:], i2[:], 64.0, None, ALU.is_lt)
        d.op("tensor_tensor", half01[:], half01[:], h0t[:], ALU.mult)
        for c in range(NCH):
            if c >= 1:
                d.wait(bnc, M["bncw", c - 1])
            km = d.op("tensor_scalar", adt_t[:], dtv[:, c * CS:(c + 1) * CS], cons[:, 4:5], None,
                      ALU.mult, dep=M["prescan", c - 1] if c >= 1 else None)
            km = d.op("tensor_tensor_scan", acs[c % 2][:], cons[:, 0:1].to_broadcast((128, CS)),
                      adt_t[:], 0.0, ALU.mult, ALU.add, dep=km)
            if rec: M["prescan", c] = km
            km = d.op("tensor_copy", ahb[:], acs[c % 2][:], dep=km)
            km = d.op("tensor_copy", ah32[:], ahb[:], dep=km)
            km = d.op("tensor_tensor", al32[:], acs[c % 2][:], ah32[:], ALU.subtract, dep=km)
            km = d.op("tensor_copy", albf[:], al32[:], dep=km)
            km = d.op("tensor_scalar", nahb[:], ahb[:], -1.0, None, ALU.mult, dep=km)
            k = d.op("tensor_scalar", nalb[:], albf[:], -1.0, None, ALU.mult, dep=km)
            if rec: M["presplit", c] = k
            if c % 8 == 7 or c == NCH - 1:
                h = c // 8
                nn = min(8, NCH - 8 * h)
                d.wait(S["pe"], M["predexp", h])
                k = d.op("tensor_copy", dexpA[:, h * 512:h * 512 + nn * 64], psZ[:, 0:nn * 64])
                if rec: M["dexpevA", h] = k
                if rec and c == NCH - 1: M["dexpfin"] = k
        for c in range(NCH):
            if c >= 2:
                d.wait(S["act"], M["sd", c - 2])
            km = d.op("tensor_scalar", adt_t[:], dtv[:, c * CS:(c + 1) * CS], cons[:, 4:5], None,
                      ALU.mult, dep=M["dexpfin"] if c <= 1 else (M["scan", c - 1] if c >= 1 else None))
            km = d.op("tensor_tensor_scan", acs[c % 2][:], cons[:, 0:1].to_broadcast((128, CS)),
                      adt_t[:], 0.0, ALU.mult, ALU.add, dep=km)
            if rec: M["scan", c] = km
            d.op("tensor_scalar", decp[:], acs[c % 2][:], acs[c % 2][:, CS - 1:CS], None,
                 ALU.subtract, dep=km)
            if rec: M["decp", c] = d.s.n
            d.wait(S["pe"], M["ptr", c])
            d.op("tensor_copy", sdT[:], psSt[0][:, 0:128])
            d.op("tensor_copy", decayT[:], psSt[0][:, 128:256])
            d.op("tensor_copy", dtT[:], psSt[0][:, 256:384])
            k = d.op("tensor_tensor", ddtT[:], dtT[:], decayT[:], ALU.mult)
            if rec: M["ptrev", c] = k
            d.wait(S["pe"], M["gmm", c, 1])
            d.op("tensor_tensor", Gsb[:, 0:512].rearrange("p (a b) -> p a b", a=4),
                 psY[0][:, :].rearrange("p (a b) -> p a b", a=4),
                 tril01[:].unsqueeze(1).to_broadcast((128, 4, 128)), ALU.mult)
            k = d.op("tensor_tensor", Gsb[:, 512:1024].rearrange("p (a b) -> p a b", a=4),
                 psY[1][:, :].rearrange("p (a b) -> p a b", a=4),
                 tril01[:].unsqueeze(1).to_broadcast((128, 4, 128)), ALU.mult)
            if rec: M["gev", c] = k
            d.wait(S["pe"], M["btr", c])
            k = d.op("tensor_copy", BTsb[:, :], psT[0][:, :])
            if rec: M["btev", c] = k
            def rec_yev(r):
                nonlocal_marker = None
                d.wait(S["pe"], M["smm", c, r])
                km = d.op("tensor_tensor", rtmp[:].rearrange("p (a b) -> p a b", a=4),
                     Ssb[:, r * 512:(r + 1) * 512].rearrange("p (a b) -> p a b", a=4),
                     dexpA[:, c * 64 + r * 4:c * 64 + r * 4 + 4].unsqueeze(2).to_broadcast((128, 4, 128)), ALU.mult,
                     dep=M["yev", c, r - 1] if r >= 1 else (M["yev", c - 1, 15] if c >= 1 else None))
                k = d.op("tensor_tensor", Ssb[:, r * 512:(r + 1) * 512], rtmp[:], psSt[r % 2][:, :],
                         ALU.add, dep=km)
                if rec: M["rec", c, r] = k
                d.wait(S["pe"], M["ydiag", c, r])
                km = d.op("tensor_tensor", rtmp[:].rearrange("p (a b) -> p a b", a=4),
                     xcb[c % 2][:, r * 512:(r + 1) * 512].rearrange("p (a b) -> p a b", a=4),
                     Dex[:, r * 4:r * 4 + 4].unsqueeze(2).to_broadcast((128, 4, 128)), ALU.mult,
                     dep=M["rec", c, r])
                k = d.op("tensor_tensor", Ybf[:, r * 512:(r + 1) * 512], psY[r % 2][:, :],
                         rtmp[:], ALU.add, dep=km)
                if rec: M["yev", c, r] = k

            for r in range(16):
                if c >= 1:
                    d.wait(S["pe"], M["zmm", c, r])
                    if r >= 1:
                        d.wait(S["pe"], M["ztr", c, r - 1])
                    k = d.op("tensor_tensor", Zs[r % 2][:].rearrange("p (a b) -> p a b", a=8),
                             psZ[:].rearrange("p (a b) -> p a b", a=8),
                             sdT[:, r * 8:r * 8 + 8].unsqueeze(2).to_broadcast((128, 8, 64)), ALU.mult,
                             dep=M["zsev", c, r - 2] if r >= 2 else None)
                    if rec: M["zsev", c, r] = k
                elif rec:
                    M["zsev", c, r] = None
                d.wait(S["pe"], M["xtr", c, r])
                k = d.op("tensor_tensor", xdt[r % 2][:].rearrange("p (a b) -> p a b", a=8),
                         psT[0][:, 0:512].rearrange("p (a b) -> p a b", a=8),
                         dtT[:, r * 8:r * 8 + 8].unsqueeze(2).to_broadcast((128, 8, 64)), ALU.mult,
                         dep=M["xdd", c, r - 2] if r >= 2 else None)
                if rec:
                    M["xdt", c, r] = k
                    M["xdtc", c, r] = k
                k = d.op("tensor_tensor", xdd[r % 2][:].rearrange("p (a b) -> p a b", a=8),
                         xdt[r % 2][:].rearrange("p (a b) -> p a b", a=8),
                         decayT[:, r * 8:r * 8 + 8].unsqueeze(2).to_broadcast((128, 8, 64)), ALU.mult,
                         dep=M["xdt", c, r])
                if rec: M["xdd", c, r] = k
                d.wait(S["act"], M["lex", c, r])
                if r >= 2:
                    d.wait(S["pe"], M["ydiag", c, r - 2])
                g = r // 2
                k = d.op("tensor_tensor", Msb[r % 2][:].rearrange("p (a b) -> p a b", a=8),
                         Lex[r % 2][:].rearrange("p (a b) -> p a b", a=8),
                         Gsb[:, g * 128:(g + 1) * 128].unsqueeze(1).to_broadcast((128, 8, 128)),
                         ALU.mult,
                         dep=M["msb", c, r - 2] if r >= 2 else (M["gev", c] if r == 0 else None))
                if rec: M["msb", c, r] = k
                if r >= 1:
                    rec_yev(r - 1)
            rec_yev(15)
            for gr in range(8):
                if gr >= 2:
                    d.wait(S["pe"], M["stat", c, gr - 2])
                dd = M["yev", c, 2 * gr + 1]
                if gr >= 1:
                    dd = max(dd, M["y2", c, gr - 1])
                elif c >= 1:
                    dd = max(dd, M["y2", c - 1, 7])
                if gr >= 1:
                    d.wait(S["pe"], M["stat", c, gr - 1])
                k = d.op("tensor_tensor", y2st[0][:], Ybf[:, gr * 1024:(gr + 1) * 1024],
                         Ybf[:, gr * 1024:(gr + 1) * 1024], ALU.mult, dep=dd)
                if rec: M["y2", c, gr] = k
            d.wait(S["pe"], M["stat", c, 7])
            d.op("tensor_scalar", rrow[0:1, :], psSt[0][0:1, 0:128], 1.0 / 8192.0, None, ALU.mult)
            if rec: M["msq", c] = d.s.n
            d.wait(S["act"], M["rstd", c])
            k = d.op("tensor_copy", rrowB[0:1, :], rrow[0:1, :])
            if rec: M["rrowB", c] = k
            d.wait(S["pe"], M["rbc", c])
            k = d.op("tensor_copy", RSTD[:], psSt[0][:, 128:256])
            if rec: M["rstdev", c] = k
            for sg in range(8):
                d.wait(S["pe"], M["str", c, sg])
                k = d.op("tensor_copy", STsb[:, sg * 1024:(sg + 1) * 1024], psT[0][:, :],
                         dep=None)
                if rec: M["stev", c, sg] = k
            for gr in range(8):
                d.wait(S["act"], M["silu", c, gr])
                if (c, gr) >= (0, 1):
                    pc, pg = (c, gr - 1) if gr >= 1 else (c - 1, 7)
                    if pc >= 0:
                        d.wait(wry[pg % 2], M["wyg", pc, pg])
                dd = M["yg", c, gr - 1] if gr >= 1 else (M["yg", c - 1, 7] if c >= 1 else None)
                km = d.op("tensor_tensor", t1st[gr % 2][:].rearrange("p (a b) -> p a b", a=8),
                     Ybf[:, gr * 1024:(gr + 1) * 1024].rearrange("p (a b) -> p a b", a=8),
                     RSTD[:].unsqueeze(1).to_broadcast((128, 8, 128)), ALU.mult, dep=dd)
                km = d.op("tensor_tensor", t1st[gr % 2][:].rearrange("p (a b) -> p a b", a=8),
                     t1st[gr % 2][:].rearrange("p (a b) -> p a b", a=8),
                     nwTc[:, gr * 8:(gr + 1) * 8].unsqueeze(2).to_broadcast((128, 8, 128)), ALU.mult, dep=km)
                k = d.op("tensor_tensor", ygst[0][:], t1st[gr % 2][:], sgst[0][:], ALU.mult, dep=km)
                if rec: M["yg", c, gr] = k

    def act_stream(a, rec):
        for c in range(NCH):
            a.wait(S["dve"], M["prescan", c])
            if c >= 2:
                a.wait(bncd, M["bncdw", c - 2])
            k = a.op("activation", dlc[c % 2][:], acs[c % 2][:, CS - 1:CS], AF.Exp,
                     dep=M["predl", c - 2] if c >= 2 else None)
            if rec: M["predl", c] = k
        for c in range(NCH):
            a.wait(S["dve"], M["decp", c])
            a.op("activation", sdf[:], acs[c % 2][:], AF.Exp,
                 dep=M["sd", c - 1] if c >= 1 else None)
            k = a.op("activation", decf[:], decp[:], AF.Exp, scale=-1.0, dep=None)
            if rec: M["sd", c] = k
            for r in range(16):
                a.wait(S["pe"], M["lmm", c, r])
                km = a.op("activation", Lt[:], psL[:], AF.Relu, scale=-1.0,
                          dep=M["lex", c, r - 1] if r >= 1 else (M["lex", c - 1, 15] if c >= 1 else None))
                if rec: M["lrelu", c, r] = km
                if r >= 2:
                    a.wait(S["dve"], M["msb", c, r - 2])
                k = a.op("activation", Lex[r % 2][:], Lt[:], AF.Exp, scale=-1.0, dep=km)
                if rec: M["lex", c, r] = k
            a.wait(S["dve"], M["msq", c])
            a.op("activation", rrow[0:1, :], rrow[0:1, :], AF.Ln, bias=cons[0:1, 2:3])
            k = a.op("activation", rrow[0:1, :], rrow[0:1, :], AF.Exp, scale=-0.5)
            if rec: M["rstd", c] = k
            for sg in range(8):
                a.wait(S["dve"], M["rec", c, 2 * sg + 1])
                if sg >= 1:
                    a.wait(S["pe"], M["str", c, sg - 1])
                k = a.op("activation", Sbfst[0][:], Ssb[:, sg * 1024:(sg + 1) * 1024], AF.Copy,
                         dep=M["sbf", c, sg - 1] if sg >= 1 else (M["sbf", c - 1, 7] if c >= 1 else None))
                if rec: M["sbf", c, sg] = k
            for gr in range(8):
                a.wait(ldg, M["ldg", c, gr // 2])
                if gr >= 1:
                    a.wait(S["dve"], M["yg", c, gr - 1])
                elif c >= 1:
                    a.wait(S["dve"], M["yg", c - 1, 7])
                k = a.op("activation", sgst[0][:], gtb[:, (gr % 2) * 1024:(gr % 2 + 1) * 1024], AF.Silu,
                         dep=M["silu", c, gr - 1] if gr >= 1 else (M["silu", c - 1, 7] if c >= 1 else None))
                if rec: M["silu", c, gr] = k

    class CntEng:
        def __init__(self, key, cnt, dmac):
            self.key = key; self.c = cnt; self.d = dmac
            class _S:
                pass
            self.s = _S()
            self.s.n = cnt.get(key, 0)
        def op(self, *a, dep=None, **k):
            self.c[self.key] += 1
            self.s.n = self.c[self.key]
            return self.c[self.key]
        def dma(self, out, in_, sem, **kw):
            key = id(sem)
            self.d[key] = self.d.get(key, sem.n) + 16
            return self.d[key]
        def wait(self, *a, **k):
            pass
        def mm(self, *a, mark=False, **k):
            if mark:
                self.c["pe"] += 1
                return self.c["pe"]
            return None
        def tr(self, *a, mark=False, **k):
            if mark:
                self.c["pe"] += 1
                return self.c["pe"]
            return None

    cnt = {"pe": S["pe"].n, "dve": S["dve"].n, "act": S["act"].n, "pool": S["pool"].n}
    dmac = {}
    pool_stream(CntEng("pool", cnt, dmac), True)
    sync_stream(CntEng("sync", {"sync": 0, **cnt}, dmac), True)
    pe_stream(CntEng("pe", cnt, dmac), True)
    dve_stream(CntEng("dve", cnt, dmac), True)
    act_stream(CntEng("act", cnt, dmac), True)

    M.strict = True
    M["final"] = dict(cnt)
    Eng = ctx["Eng"]; MMc = ctx["MM"]
    with nc.Block() as blk:
        @blk.gpsimd
        def _(g):
            pool_stream(Eng(g, S["pool"]), False)

        @blk.sync
        def _(sy):
            sync_stream(Eng(sy, None), False)

        @blk.tensor
        def _(t):
            pe_stream(MMc(t, S["pe"]), False)

        @blk.vector
        def _(v):
            dve_stream(Eng(v, S["dve"]), False)

        @blk.scalar
        def _(a):
            act_stream(Eng(a, S["act"]), False)

    fin = M["final"]
    for e in ("pe", "dve", "act", "pool"):
        assert S[e].n == fin[e], f"count mismatch {e}: emit={S[e].n} count={fin[e]}"
    return M


def build(NT=2048, debug_names=()):
    assert NT % 512 == 0
    NTT = NT // 512
    NCH = NT // CS
    nc = bass.Bass("TRN2", target_bir_lowering=False, debug=False)

    x0T = nc.dram_tensor("x0T", [NDC, 128, NT], F32, kind="ExternalInput").ap()
    WinB = nc.dram_tensor("WinB", [NRC, 128, NDC, 128], F32, kind="ExternalInput").ap()
    WoutB = nc.dram_tensor("WoutB", [NDC, 128, NCC, 128], F32, kind="ExternalInput").ap()
    SgB = nc.dram_tensor("SgB", [NDC, 128, NDC, 128], F32, kind="ExternalInput").ap()
    convW = nc.dram_tensor("convW", [128, 80, 4], F32, kind="ExternalInput").ap()
    convBi = nc.dram_tensor("convBi", [128, 80], F32, kind="ExternalInput").ap()
    dtBi = nc.dram_tensor("dtBi", [128, 1], F32, kind="ExternalInput").ap()
    Alog = nc.dram_tensor("Alog", [128, 1], F32, kind="ExternalInput").ap()
    Dexp = nc.dram_tensor("Dexp", [128, 64], F32, kind="ExternalInput").ap()
    nwT = nc.dram_tensor("nwT", [128, NCC], F32, kind="ExternalInput").ap()
    scal = nc.dram_tensor("scal", [1, 1], F32, kind="ExternalInput").ap()

    outT = nc.dram_tensor("outT", [NDC, 128, NT], F32, kind="ExternalOutput").ap()

    x1s = nc.dram_tensor("x1s", [NDC, 128, NT], F32).ap()
    x2s = nc.dram_tensor("x2s", [NDC, 128, NT], F32).ap()
    gates = nc.dram_tensor("gates", [NCH, 64, 128, 128], BF16).ap()
    xcs = nc.dram_tensor("xcs", [NCH, 80, 128, 128], BF16).ap()
    ygs = nc.dram_tensor("ygs", [NCH, 64, 128, 128], BF16).ap()
    bncA = nc.dram_tensor("bncA", [4, 16384 * 2], F32).ap()
    bncB = nc.dram_tensor("bncB", [4, 16384 * (NT // 128)], BF16).ap()

    dbg = {}
    if DEBUG:
        def dbgt(nm, shape, dt=F32):
            dbg[nm] = nc.dram_tensor(nm, shape, dt, kind="ExternalOutput").ap()
        if "x1" in debug_names: dbgt("dbg_x1", [NDC, 128, NT])
        if "xcs" in debug_names: dbgt("dbg_xcs", [NCH, 80, 128, 128], BF16)
        if "gates" in debug_names: dbgt("dbg_gates", [NCH, 64, 128, 128], BF16)
        if "dtv" in debug_names: dbgt("dbg_dtv", [128, NT])
        if "yg" in debug_names: dbgt("dbg_yg", [NCH, 64, 128, 128], BF16)
        if "x2" in debug_names: dbgt("dbg_x2", [NDC, 128, NT])

    st = ExitStack()

    cons = st.enter_context(nc.sbuf_tensor("cons", [128, 16], F32))
    identB = st.enter_context(nc.sbuf_tensor("identB", [128, 128], BF16))
    identF = st.enter_context(nc.sbuf_tensor("identF", [128, 128], F32))
    maskT = st.enter_context(nc.sbuf_tensor("maskT", [128, 128], F32))
    onesrowB = st.enter_context(nc.sbuf_tensor("onesrowB", [1, 128], BF16))
    onecolB = st.enter_context(nc.sbuf_tensor("onecolB", [128, 1], BF16))
    iot = st.enter_context(nc.sbuf_tensor("iot", [128, 128], mybir.dt.int32))
    cwcol = st.enter_context(nc.sbuf_tensor("cwcol", [128, 320], F32))
    cbcol = st.enter_context(nc.sbuf_tensor("cbcol", [128, 80], F32))
    nwTc = st.enter_context(nc.sbuf_tensor("nwTc", [128, NCC], F32))
    Dex = st.enter_context(nc.sbuf_tensor("Dex", [128, 64], F32))
    dtv = st.enter_context(nc.sbuf_tensor("dtv", [128, NT], F32))

    S = {}
    for nm in ["pe", "dve", "act", "pool", "ldS", "fin"]:
        S[nm] = Sem(st, nc, nm)
    marks = {}

    with nc.Block() as blk:
        @blk.sync
        def _(sy):
            s = Eng(sy, None)
            s.dma(cwcol[:], convW.rearrange("p a b -> p (a b)"), S["ldS"])
            s.dma(cbcol[:], convBi[:], S["ldS"])
            s.dma(nwTc[:], nwT[:], S["ldS"])
            s.dma(cons[:, 3:4], dtBi[:], S["ldS"])
            s.dma(cons[:, 7:8], Alog[:], S["ldS"])
            s.dma(Dex[:], Dexp[:], S["ldS"])
            s.dma(cons[:, 8:9], scal[0:1, 0:1].to_broadcast((128, 1)), S["ldS"])

        @blk.gpsimd
        def _(g):
            p = Eng(g, S["pool"])
            marks["iota"] = p.op("iota", iot[:], pattern=[[1, 128]], base=0, channel_multiplier=-1)

        @blk.vector
        def _(v):
            d = Eng(v, S["dve"])
            d.wait(S["pool"], marks["iota"])
            d.op("memset", cons[:, 0:1], 1.0)
            d.op("memset", cons[:, 2:3], EPS)
            d.op("memset", cons[:, 6:7], 1.0 / DM)
            d.op("memset", onesrowB[:], 1.0)
            d.op("memset", onecolB[:], 1.0)
            d.op("tensor_scalar", identF[:], iot[:], 0.0, None, ALU.is_equal)
            d.op("tensor_scalar", identB[:], iot[:], 0.0, None, ALU.is_equal)
            d.op("tensor_scalar", maskT[:], iot[:], 0.0, 1.0, ALU.is_ge, ALU.subtract)
            marks["msk"] = d.op("tensor_scalar", maskT[:], maskT[:], 1e30, None, ALU.mult)

        @blk.scalar
        def _(a):
            c = Eng(a, S["act"])
            c.wait(S["ldS"], S["ldS"].n)
            c.wait(S["dve"], marks["msk"])
            c.op("activation", cons[:, 4:5], cons[:, 7:8], AF.Exp)
            c.op("activation", cons[:, 9:10], cons[:, 8:9], AF.Abs)
            c.op("activation", cons[:, 10:11], cons[:, 9:10], AF.Exp, scale=-1.0)
            marks["sp"] = c.op("activation", cons[:, 11:12], cons[:, 10:11], AF.Ln, bias=cons[:, 0:1])

        @blk.vector
        def _(v):
            d = Eng(v, S["dve"])
            d.wait(S["act"], marks["sp"])
            d.op("tensor_scalar", cons[:, 4:5], cons[:, 4:5], -1.0, None, ALU.mult)
            d.op("tensor_scalar", cons[:, 12:13], cons[:, 8:9], 0.0, None, ALU.max)
            d.op("tensor_tensor", cons[:, 13:14], cons[:, 12:13], cons[:, 11:12], ALU.add)
            d.op("tensor_scalar", cons[:, 1:2], cons[:, 13:14], 0.5, None, ALU.mult)

    def ln_phase(src, xln, tag):
        stk = ExitStack()
        tstk = ExitStack()
        xf = [stk.enter_context(nc.sbuf_tensor(f"xf{tag}{i}", [128, NT], F32)) for i in range(2)]
        Abc = stk.enter_context(nc.sbuf_tensor(f"Abc{tag}", [128, NT], BF16))
        Bbc = stk.enter_context(nc.sbuf_tensor(f"Bbc{tag}", [128, NT], BF16))
        xbf = tstk.enter_context(nc.sbuf_tensor(f"xb{tag}", [128, NT], BF16))
        sqb = tstk.enter_context(nc.sbuf_tensor(f"sq{tag}", [128, NT], BF16))
        lnr = tstk.enter_context(nc.sbuf_tensor(f"lnr{tag}", [1, 2 * NT], F32))
        lnrB = tstk.enter_context(nc.sbuf_tensor(f"lnrB{tag}", [1, 2 * NT], BF16))
        ldx = [Sem(stk, nc, f"ldx{tag}0"), Sem(stk, nc, f"ldx{tag}1")]
        ldm = {}

        pstk = ExitStack()
        psM = pstk.enter_context(nc.psum_tensor(f"psM{tag}", [1, NT], F32))
        psS = pstk.enter_context(nc.psum_tensor(f"psS{tag}", [1, NT], F32))

        pe0, dve0, act0 = S["pe"].n, S["dve"].n, S["act"].n
        mmstm = lambda dc: pe0 + dc + 1
        castm = lambda dc: dve0 + dc + 1
        sqm = lambda dc: act0 + dc + 1

        with nc.Block() as blk:
            @blk.sync
            def _(sy):
                s = Eng(sy, None)
                for dc in range(NDC):
                    if dc >= 2:
                        s.wait(S["pe"], mmstm(dc - 2))
                    ldm[dc] = s.dma(xf[dc % 2][:], src[dc], ldx[dc % 2])

            @blk.vector
            def _(v):
                d = Eng(v, S["dve"])
                for dc in range(NDC):
                    d.wait(ldx[dc % 2], ldm[dc])
                    if dc >= 1:
                        d.wait(S["pe"], mmstm(dc - 1))
                    assert d.op("tensor_copy", xbf[:], xf[dc % 2][:]) == castm(dc)

            @blk.scalar
            def _(a):
                c = Eng(a, S["act"])
                for dc in range(NDC):
                    c.wait(ldx[dc % 2], ldm[dc])
                    if dc >= 1:
                        c.wait(S["pe"], mmstm(dc - 1))
                    assert c.op("activation", sqb[:], xf[dc % 2][:], AF.Square) == sqm(dc)

            @blk.tensor
            def _(t):
                p = MM(t, S["pe"])
                for dc in range(NDC):
                    p.wait(S["dve"], castm(dc))
                    p.wait(S["act"], sqm(dc))
                    for tt in range(NTT):
                        p.mm(psM[0:1, tt * 512:(tt + 1) * 512], onecolB[:],
                             xbf[:, tt * 512:(tt + 1) * 512], dc == 0, dc == NDC - 1)
                    for tt in range(NTT):
                        mk = p.mm(psS[0:1, tt * 512:(tt + 1) * 512], onecolB[:],
                                  sqb[:, tt * 512:(tt + 1) * 512], dc == 0, dc == NDC - 1,
                                  mark=(tt == NTT - 1))
                    assert mk == mmstm(dc)

            @blk.vector
            def _(v):
                d = Eng(v, S["dve"])
                d.wait(S["pe"], mmstm(NDC - 1))
                d.op("tensor_scalar", lnr[0:1, 0:NT], psM[0:1, :], 1.0 / DM, None, ALU.mult)
                d.op("tensor_scalar", lnr[0:1, NT:2 * NT], psS[0:1, :], 1.0 / DM, None, ALU.mult)
                d.op("tensor_tensor", sqb[0:1, 0:NT], lnr[0:1, 0:NT], lnr[0:1, 0:NT], ALU.mult)
                marks["var"] = d.op("tensor_tensor", lnr[0:1, NT:2 * NT], lnr[0:1, NT:2 * NT],
                                    sqb[0:1, 0:NT], ALU.subtract)

            @blk.scalar
            def _(a):
                c = Eng(a, S["act"])
                c.wait(S["dve"], marks["var"])
                c.op("activation", lnr[0:1, NT:2 * NT], lnr[0:1, NT:2 * NT], AF.Ln, bias=cons[0:1, 2:3])
                marks["rstd"] = c.op("activation", lnr[0:1, NT:2 * NT], lnr[0:1, NT:2 * NT], AF.Exp, scale=-0.5)

            @blk.vector
            def _(v):
                d = Eng(v, S["dve"])
                d.wait(S["act"], marks["rstd"])
                d.op("tensor_tensor", lnr[0:1, 0:NT], lnr[0:1, 0:NT], lnr[0:1, NT:2 * NT], ALU.mult)
                d.op("tensor_scalar", lnr[0:1, 0:NT], lnr[0:1, 0:NT], -1.0, None, ALU.mult)
                d.op("tensor_copy", lnrB[0:1, 0:NT], lnr[0:1, NT:2 * NT])
                marks["rows"] = d.op("tensor_copy", lnrB[0:1, NT:2 * NT], lnr[0:1, 0:NT])

        pstk.close()
        psBA = tstk.enter_context(nc.psum_tensor(f"psBA{tag}", [128, 512], F32))
        psBB = tstk.enter_context(nc.psum_tensor(f"psBB{tag}", [128, 512], F32))
        dve0b = S["dve"].n
        bcev = lambda tt: dve0b + 2 * (tt + 1)
        with nc.Block() as blk:
            @blk.tensor
            def _(t):
                p = MM(t, S["pe"])
                p.wait(S["dve"], marks["rows"])
                for tt in range(NT // 512):
                    if tt >= 1:
                        p.wait(S["dve"], bcev(tt - 1))
                    mka = p.mm(psBA[:, :], onesrowB[0:1, :],
                               lnrB[0:1, tt * 512:(tt + 1) * 512], True, True, mark=True)
                    mkb = p.mm(psBB[:, :], onesrowB[0:1, :],
                               lnrB[0:1, NT + tt * 512:NT + (tt + 1) * 512], True, True, mark=True)
                    marks[f"bc{tag}{tt}"] = (mka, mkb)

            @blk.vector
            def _(v):
                d = Eng(v, S["dve"])
                for tt in range(NT // 512):
                    mka, mkb = marks[f"bc{tag}{tt}"]
                    d.wait(S["pe"], mka)
                    d.op("tensor_copy", Abc[:, tt * 512:(tt + 1) * 512], psBA[:, :], dep=True)
                    d.wait(S["pe"], mkb)
                    k = d.op("tensor_copy", Bbc[:, tt * 512:(tt + 1) * 512], psBB[:, :], dep=True)
                    assert k == bcev(tt)
        tstk.close()
        return dict(xf=xf, ldx=ldx, ldm=ldm, Abc=Abc, Bbc=Bbc, stk=stk, tag=tag, src=src)

    def ffn_phase(xln, res, dst, tag, lnd, dbg_out=None):
        stk = ExitStack()
        sgb = [stk.enter_context(nc.sbuf_tensor(f"sg{tag}{i}", [128, NDC * 128], BF16)) for i in range(2)]
        xr = [stk.enter_context(nc.sbuf_tensor(f"xr{tag}0", [128, NT], F32))] * 2
        x1t = [stk.enter_context(nc.sbuf_tensor(f"x1t{tag}0", [128, NT], F32))] * 2
        psF = [stk.enter_context(nc.psum_tensor(f"psF{tag}{i}", [128, NT], F32)) for i in range(2)]
        ldw = [Sem(stk, nc, f"ldw{tag}0"), Sem(stk, nc, f"ldw{tag}1")]
        ldr = [Sem(stk, nc, f"ldr{tag}0"), Sem(stk, nc, f"ldr{tag}1")]
        wr = [Sem(stk, nc, f"wr{tag}0"), Sem(stk, nc, f"wr{tag}1")]
        lwm, lrm = {}, {}
        nst = 2 if dbg_out is not None else 1
        xf, ldx, ldm, Abc, Bbc = lnd["xf"], lnd["ldx"], lnd["ldm"], lnd["Abc"], lnd["Bbc"]
        lsrc = lnd["src"]
        ldx2 = Sem(stk, nc, f"ldx2{tag}")

        pe0, dve0 = S["pe"].n, S["dve"].n
        aplm = lambda dc: dve0 + 2 * (dc + 1)
        mmfc = lambda oc: pe0 + oc + 1
        evm = lambda oc: dve0 + 2 * NDC + oc + 1
        wrm = lambda oc: 16 * nst * (oc // 2 + 1)

        with nc.Block() as blk:
            @blk.sync
            def _(sy):
                s = Eng(sy, None)
                for dc in range(0, NDC, 2):
                    if dc >= 2:
                        s.wait(S["dve"], aplm(dc - 2))
                    ldm[(1, dc)] = s.dma(xf[0][:], lsrc[dc], ldx[0])
                for oc in range(NDC):
                    if oc >= 1:
                        s.wait(S["dve"], evm(oc - 1))
                    lrm[oc] = s.dma(xr[0][:], res[oc], ldr[oc % 2])

            @blk.gpsimd
            def _(g):
                p = Eng(g, S["pool"])
                for oc in range(2):
                    lwm[oc] = p.dma(sgb[oc % 2][:], SgB[oc].rearrange("p a b -> p (a b)"), ldw[oc % 2])
                for dc in range(1, NDC, 2):
                    if dc >= 2:
                        p.wait(S["dve"], aplm(dc - 2))
                    ldm[(1, dc)] = p.dma(xf[1][:], lsrc[dc], ldx2)
                for oc in range(2, NDC):
                    p.wait(S["pe"], mmfc(oc - 2))
                    lwm[oc] = p.dma(sgb[oc % 2][:], SgB[oc].rearrange("p a b -> p (a b)"), ldw[oc % 2])

            @blk.vector
            def _(v):
                d = Eng(v, S["dve"])
                for dc in range(NDC):
                    d.wait(ldx[0] if dc % 2 == 0 else ldx2, ldm[(1, dc)])
                    km = d.op("tensor_tensor", xf[dc % 2][:], xf[dc % 2][:], Abc[:], ALU.mult,
                              dep=aplm(dc - 2) if dc >= 2 else None)
                    assert d.op("tensor_tensor", xln[:, dc * NT:(dc + 1) * NT], xf[dc % 2][:],
                                Bbc[:], ALU.add, dep=km) == aplm(dc)
                for oc in range(NDC):
                    d.wait(S["pe"], mmfc(oc))
                    d.wait(ldr[oc % 2], lrm[oc])
                    if oc >= 1:
                        d.wait(wr[(oc - 1) % 2], wrm(oc - 1))
                    assert d.op("scalar_tensor_tensor", x1t[0][:], psF[oc % 2][:],
                                cons[:, 1:2], xr[oc % 2][:], ALU.mult, ALU.add,
                                dep=None) == evm(oc)

            @blk.tensor
            def _(t):
                p = MM(t, S["pe"])
                for oc in range(NDC):
                    p.wait(ldw[oc % 2], lwm[oc])
                    if oc >= 2:
                        p.wait(S["dve"], evm(oc - 2))
                    for ic in range(NDC):
                        if oc == 0:
                            p.wait(S["dve"], aplm(ic))
                        for tt in range(NTT):
                            mk = p.mm(psF[oc % 2][:, tt * 512:(tt + 1) * 512],
                                      sgb[oc % 2][:, ic * 128:(ic + 1) * 128],
                                      xln[:, ic * NT + tt * 512:ic * NT + (tt + 1) * 512],
                                      ic == 0, ic == NDC - 1,
                                      mark=(ic == NDC - 1 and tt == NTT - 1))
                    assert mk == mmfc(oc)

            @blk.scalar
            def _(a):
                c = Eng(a, S["act"])
                for oc in range(NDC):
                    c.wait(S["dve"], evm(oc))
                    c.dma(dst[oc], x1t[0][:], wr[oc % 2])
                    if dbg_out is not None:
                        c.dma(dbg_out[oc], x1t[0][:], wr[oc % 2])
                c.wait(wr[0], wr[0].n)
                c.wait(wr[1], wr[1].n)
        stk.close()
        lnd["stk"].close()

    def proj_phase(xln, lnd):
        stk = ExitStack()
        wb = [stk.enter_context(nc.sbuf_tensor(f"wbp{i}", [128, NDC * 128], BF16)) for i in range(2)]
        gevac = [stk.enter_context(nc.sbuf_tensor("gev0", [128, NT], BF16))] * 2
        xevac = [stk.enter_context(nc.sbuf_tensor("xev0", [128, NT], BF16))] * 2
        convraw = stk.enter_context(nc.sbuf_tensor("convraw", [128, NT + 3], F32))
        convacc = stk.enter_context(nc.sbuf_tensor("convacc", [128, NT], F32))
        psW = [stk.enter_context(nc.psum_tensor(f"psW{i}", [128, NT], F32)) for i in range(2)]
        ldwB = [Sem(stk, nc, f"ldwB{i}") for i in range(2)]
        wrG = [Sem(stk, nc, f"wrG{i}") for i in range(2)]
        wrX = [Sem(stk, nc, f"wrX{i}") for i in range(2)]
        lwm = {}

        xfL, ldxL, ldmL, AbcL, BbcL = lnd["xf"], lnd["ldx"], lnd["ldm"], lnd["Abc"], lnd["Bbc"]
        lsrc = lnd["src"]
        ldx0b = ldxL[0].n
        ldme = lambda dc: ldx0b + 16 * (dc // 2 + 1)
        ldx2p = Sem(stk, nc, "ldx2p")
        pe0, dve0, act0 = S["pe"].n, S["dve"].n, S["act"].n
        aplm = lambda dc: dve0 + 2 * (dc + 1)
        mmp = lambda rc: pe0 + rc + 1
        evbase = {}
        cum = dve0 + 2 * NDC + 1
        for rc in range(NRC):
            evbase[rc] = cum
            cum += 1 if (rc < 64 or rc == 144) else 5
        evW = lambda rc: evbase[rc] + 1
        convm = lambda rc: evbase[rc] + 5
        silm = lambda crc: act0 + crc + 1
        wrGm, wrXm = {}, {}
        cG, cX = [0, 0], [0, 0]
        for rc in range(64):
            cG[rc % 2] += 16 * NCH
            wrGm[rc] = cG[rc % 2]
        for crc in range(80):
            cX[crc % 2] += 16 * NCH
            wrXm[crc] = cX[crc % 2]

        with nc.Block() as blk:
            @blk.gpsimd
            def _(g):
                p = Eng(g, S["pool"])
                for rc in range(2):
                    lwm[rc] = p.dma(wb[rc % 2][:], WinB[rc].rearrange("p a b -> p (a b)"), ldwB[rc % 2])
                for dc in range(1, NDC, 2):
                    if dc >= 2:
                        p.wait(S["dve"], aplm(dc - 2))
                    ldmL[(1, dc)] = p.dma(xfL[1][:], lsrc[dc], ldx2p)
                for rc in range(2, NRC):
                    p.wait(S["pe"], mmp(rc - 2))
                    lwm[rc] = p.dma(wb[rc % 2][:], WinB[rc].rearrange("p a b -> p (a b)"), ldwB[rc % 2])

            @blk.tensor
            def _(t):
                p = MM(t, S["pe"])
                for rc in range(NRC):
                    p.wait(ldwB[rc % 2], lwm[rc])
                    if rc >= 2:
                        p.wait(S["dve"], evW(rc - 2))
                    for dc in range(NDC):
                        if rc == 0:
                            p.wait(S["dve"], aplm(dc))
                        for tt in range(NTT):
                            mk = p.mm(psW[rc % 2][:, tt * 512:(tt + 1) * 512],
                                      wb[rc % 2][:, dc * 128:(dc + 1) * 128],
                                      xln[:, dc * NT + tt * 512:dc * NT + (tt + 1) * 512],
                                      dc == 0, dc == NDC - 1,
                                      mark=(dc == NDC - 1 and tt == NTT - 1))
                    assert mk == mmp(rc)

            @blk.vector
            def _(v):
                d = Eng(v, S["dve"])
                for dc in range(NDC):
                    d.wait(ldxL[0] if dc % 2 == 0 else ldx2p, ldme(dc) if dc % 2 == 0 else ldmL[(1, dc)])
                    km = d.op("tensor_tensor", xfL[dc % 2][:], xfL[dc % 2][:], AbcL[:], ALU.mult,
                              dep=aplm(dc - 2) if dc >= 2 else None)
                    assert d.op("tensor_tensor", xln[:, dc * NT:(dc + 1) * NT], xfL[dc % 2][:],
                                BbcL[:], ALU.add, dep=km) == aplm(dc)
                d.op("memset", convraw[:, 0:3], 0.0)
                for rc in range(NRC):
                    d.wait(S["pe"], mmp(rc))
                    if rc < 64:
                        if rc >= 1:
                            d.wait(wrG[(rc - 1) % 2], wrGm[rc - 1])
                        assert d.op("tensor_copy", gevac[rc % 2][:], psW[rc % 2][:]) == evW(rc)
                    elif rc < 144:
                        crc = rc - 64
                        if crc >= 1:
                            d.wait(S["act"], silm(crc - 1))
                        assert d.op("tensor_copy", convraw[:, 3:3 + NT], psW[rc % 2][:]) == evW(rc)
                        d.op("tensor_scalar", convacc[:], convraw[:, 0:NT],
                             cwcol[:, crc * 4:crc * 4 + 1], None, ALU.mult)
                        for kk in range(1, 4):
                            mk = d.op("scalar_tensor_tensor", convacc[:], convraw[:, kk:kk + NT],
                                      cwcol[:, crc * 4 + kk:crc * 4 + kk + 1], convacc[:],
                                      ALU.mult, ALU.add)
                        assert mk == convm(rc)
                    else:
                        d.wait(S["act"], act0 + 80)
                        assert d.op("tensor_copy", convacc[:], psW[rc % 2][:]) == evW(rc)
                d.op("tensor_scalar", convacc[:], convacc[:], cons[:, 3:4], None, ALU.add, dep=True)
                marks["dt0"] = S["dve"].n
                d.wait(S["act"], act0 + 80 + 1)
                d.op("tensor_scalar", convacc[:], convacc[:], 0.0, None, ALU.max, dep=True)
                marks["dt1"] = S["dve"].n
                d.wait(S["act"], act0 + 80 + 3)
                marks["adt"] = d.op("tensor_tensor", dtv[:], convacc[:], convraw[:, 3:3 + NT], ALU.add, dep=True)

            @blk.scalar
            def _(a):
                c = Eng(a, S["act"])
                for crc in range(80):
                    c.wait(S["dve"], convm(crc + 64))
                    if crc >= 1:
                        c.wait(wrX[(crc - 1) % 2], wrXm[crc - 1])
                    assert c.op("activation", xevac[crc % 2][:], convacc[:], AF.Silu,
                                bias=cbcol[:, crc:crc + 1]) == silm(crc)
                    c.wait(S["act"], silm(crc))
                    for cb in range(NCH):
                        c.dma(xcs[cb, crc], xevac[crc % 2][:, cb * 128:(cb + 1) * 128], wrX[crc % 2])
                c.wait(S["dve"], marks["dt0"])
                c.op("activation", convraw[:, 3:3 + NT], convacc[:], AF.Abs)
                c.op("activation", convraw[:, 3:3 + NT], convraw[:, 3:3 + NT], AF.Exp, scale=-1.0, dep=True)
                c.op("activation", convraw[:, 3:3 + NT], convraw[:, 3:3 + NT], AF.Ln, bias=cons[:, 0:1], dep=True)
                c.wait(wrX[0], wrX[0].n)
                c.wait(wrX[1], wrX[1].n)

            @blk.sync
            def _(sy):
                s = Eng(sy, None)
                for dc in range(0, NDC, 2):
                    if dc >= 2:
                        s.wait(S["dve"], aplm(dc - 2))
                    assert s.dma(xfL[0][:], lsrc[dc], ldxL[0]) == ldme(dc)
                for rc in range(64):
                    s.wait(S["dve"], evW(rc))
                    for cb in range(NCH):
                        s.dma(gates[cb, rc], gevac[rc % 2][:, cb * 128:(cb + 1) * 128], wrG[rc % 2])
                s.wait(S["dve"], marks["adt"])
                s.wait(wrG[0], wrG[0].n)
                s.wait(wrG[1], wrG[1].n)
        stk.close()
        lnd["stk"].close()


    def phase_d():
        stk = ExitStack()
        ygh = stk.enter_context(nc.sbuf_tensor("ygh", [128, NCC * (NT // 2)], BF16))
        wob = [stk.enter_context(nc.sbuf_tensor(f"wob{i}", [128, NCC * 128], BF16)) for i in range(2)]
        xr = [stk.enter_context(nc.sbuf_tensor(f"xrd{i}", [128, NT // 2], F32)) for i in range(2)]
        x2t = [stk.enter_context(nc.sbuf_tensor(f"x2t{i}", [128, NT // 2], F32)) for i in range(2)]
        psD = [stk.enter_context(nc.psum_tensor(f"psD{i}", [128, NT // 2], F32)) for i in range(2)]
        ldy = Sem(stk, nc, "ldy")
        ldwD = [Sem(stk, nc, f"ldwD{i}") for i in range(2)]
        ldxD = [Sem(stk, nc, f"ldxD{i}") for i in range(2)]
        wrD = [Sem(stk, nc, f"wrD{i}") for i in range(2)]
        HT = NT // 2
        NH2 = NCH // 2
        TS = min(512, HT)
        NT2 = HT // TS

        for hf in range(2):
            lym, lwm, lxm = {}, {}, {}
            pe0, dve0 = S["pe"].n, S["dve"].n
            wrb = (wrD[0].n, wrD[1].n)
            nstD = 2 if "dbg_x2" in dbg else 1
            mmdc = lambda dc: pe0 + dc + 1
            evdc = lambda dc: dve0 + dc + 1
            with nc.Block() as blk:
                @blk.sync
                def _(sy):
                    s = Eng(sy, None)
                    for cc in range(NCC):
                        k = s.dma(ygh[:, cc * HT:(cc + 1) * HT].rearrange("p (a t) -> p a t", a=NH2),
                                  ygs[hf * NH2:(hf + 1) * NH2, cc].rearrange("a p t -> p a t"), ldy)
                    lym[0] = k
                    for dc in range(NDC):
                        if dc >= 2:
                            s.wait(S["dve"], evdc(dc - 2))
                        lxm[dc] = s.dma(xr[dc % 2][:], x1s[dc][:, hf * HT:(hf + 1) * HT], ldxD[dc % 2])

                @blk.gpsimd
                def _(g):
                    p = Eng(g, S["pool"])
                    for dc in range(NDC):
                        if dc >= 2:
                            p.wait(S["pe"], mmdc(dc - 2))
                        lwm[dc] = p.dma(wob[dc % 2][:], WoutB[dc].rearrange("p a b -> p (a b)"), ldwD[dc % 2])

                @blk.tensor
                def _(t):
                    p = MM(t, S["pe"])
                    p.wait(ldy, lym[0])
                    for dc in range(NDC):
                        p.wait(ldwD[dc % 2], lwm[dc])
                        if dc >= 2:
                            p.wait(S["dve"], evdc(dc - 2))
                        for cc in range(NCC):
                            for tt in range(NT2):
                                mk = p.mm(psD[dc % 2][:, tt * TS:(tt + 1) * TS],
                                          wob[dc % 2][:, cc * 128:(cc + 1) * 128],
                                          ygh[:, cc * HT + tt * TS:cc * HT + (tt + 1) * TS],
                                          cc == 0, cc == NCC - 1,
                                          mark=(cc == NCC - 1 and tt == NT2 - 1))
                        assert mk == mmdc(dc)

                @blk.vector
                def _(v):
                    d = Eng(v, S["dve"])
                    for dc in range(NDC):
                        d.wait(S["pe"], mmdc(dc))
                        d.wait(ldxD[dc % 2], lxm[dc])
                        if dc >= 2:
                            d.wait(wrD[dc % 2], wrb[dc % 2] + 16 * nstD * ((dc - 2) // 2 + 1))
                        assert d.op("tensor_tensor", x2t[dc % 2][:], psD[dc % 2][:], xr[dc % 2][:],
                                    ALU.add) == evdc(dc)

                @blk.scalar
                def _(a):
                    c = Eng(a, S["act"])
                    for dc in range(NDC):
                        c.wait(S["dve"], evdc(dc))
                        c.dma(x2s[dc][:, hf * HT:(hf + 1) * HT], x2t[dc % 2][:], wrD[dc % 2])
                        if "dbg_x2" in dbg:
                            c.dma(dbg["dbg_x2"][dc][:, hf * HT:(hf + 1) * HT], x2t[dc % 2][:], wrD[dc % 2])
                    c.wait(wrD[0], wrD[0].n)
                    c.wait(wrD[1], wrD[1].n)
            for sm in (ldy, ldwD[0], ldwD[1], ldxD[0], ldxD[1], wrD[0], wrD[1]):
                pass
        stk.close()


    stkAB = ExitStack()
    xln = stkAB.enter_context(nc.sbuf_tensor("xlnAB", [128, NDC * NT], BF16))

    lnd = ln_phase(x0T, xln, "a")
    ffn_phase(xln, x0T, x1s, "fa", lnd, dbg_out=dbg.get("dbg_x1"))
    lnd = ln_phase(x1s, xln, "b")
    proj_phase(xln, lnd)
    stkAB.close()

    stkC = ExitStack()
    ctxC = dict(NT=NT, NCH=NCH, cons=cons, identF=identF, identB=identB, maskT=maskT,
                onesrowB=onesrowB, onecolB=onecolB, iot=iot, nwTc=nwTc, Dex=Dex, dtv=dtv, Acol=cons[:, 4:5],
                gates=gates, xcs=xcs, ygs=ygs, bncA=bncA, bncB=bncB, Eng=Eng, MM=MM)
    emit_phase_c(nc, S, Sem, ctxC, stkC)
    stkC.close()

    phase_d()

    stkE = ExitStack()
    xlnE = stkE.enter_context(nc.sbuf_tensor("xlnE", [128, NDC * NT], BF16))
    lnd = ln_phase(x2s, xlnE, "e")
    ffn_phase(xlnE, x2s, outT, "fe", lnd)
    stkE.close()

    if DEBUG and ("gates" in debug_names or "xcs" in debug_names or "dtv" in debug_names):
        with nc.Block() as blk:
            @blk.sync
            def _(sy):
                s = Eng(sy, None)
                if "gates" in debug_names:
                    s.dma(dbg["dbg_gates"].rearrange("a b p t -> (a b) (p t)"),
                          gates.rearrange("a b p t -> (a b) (p t)"), S["fin"])
                if "xcs" in debug_names:
                    s.dma(dbg["dbg_xcs"].rearrange("a b p t -> (a b) (p t)"),
                          xcs.rearrange("a b p t -> (a b) (p t)"), S["fin"])
                s.wait(S["fin"], S["fin"].n)
        if "dtv" in debug_names:
            with nc.Block() as blk:
                @blk.sync
                def _(sy):
                    s = Eng(sy, None)
                    s.dma(dbg["dbg_dtv"], dtv[:], S["fin"])
                    s.wait(S["fin"], S["fin"].n)

    if DEBUG and "yg" in debug_names:
        with nc.Block() as blk:
            @blk.sync
            def _(sy):
                s = Eng(sy, None)
                s.dma(dbg["dbg_yg"].rearrange("a b p t -> (a b) (p t)"),
                      ygs.rearrange("a b p t -> (a b) (p t)"), S["fin"])
                s.wait(S["fin"], S["fin"].n)

    st.close()
    return nc, dbg


def stage_inputs(hs_b, in_proj_w, out_proj_w, sign, conv_w, conv_b, dt_bias, A_log, Dv, norm_w, scale, NT=2048):
    m = {}
    m["x0T"] = np.ascontiguousarray(hs_b[:NT].T).reshape(NDC, 128, NT)
    m["WinB"] = np.ascontiguousarray(in_proj_w.reshape(NRC, 128, NDC, 128).transpose(0, 3, 2, 1))
    m["WoutB"] = np.ascontiguousarray(out_proj_w.reshape(NDC, 128, NCC, 128).transpose(0, 3, 2, 1))
    m["SgB"] = np.ascontiguousarray(sign.reshape(NDC, 128, NDC, 128).transpose(2, 1, 0, 3))
    m["convW"] = np.ascontiguousarray(conv_w.reshape(80, 128, 4).transpose(1, 0, 2))
    m["convBi"] = np.ascontiguousarray(conv_b.reshape(80, 128).T)
    m["dtBi"] = np.ascontiguousarray(dt_bias.reshape(128, 1))
    m["Alog"] = np.ascontiguousarray(A_log.reshape(128, 1))
    m["Dexp"] = np.ascontiguousarray(
        np.concatenate([np.broadcast_to(Dv[0::2], (64, 64)), np.broadcast_to(Dv[1::2], (64, 64))], axis=0))
    m["nwT"] = np.ascontiguousarray(norm_w.reshape(NCC, 128).T)
    m["scal"] = np.asarray(scale, np.float32).reshape(1, 1)
    return m



def kernel(L0_attn_hidden_states, L0_attn_in_proj_w, L0_attn_conv_w, L0_attn_conv_b,
           L0_attn_dt_bias, L0_attn_A_log, L0_attn_D, L0_attn_norm_w, L0_attn_out_proj_w,
           L0_ffn_sign, L0_ffn_scale):
    from concourse.bass_utils import run_bass_kernel_spmd
    hs = np.asarray(L0_attn_hidden_states, np.float32)
    NT = hs.shape[1]
    nc, _ = build(NT)
    base = stage_inputs(hs[0], np.asarray(L0_attn_in_proj_w, np.float32),
                        np.asarray(L0_attn_out_proj_w, np.float32),
                        np.asarray(L0_ffn_sign, np.float32),
                        np.asarray(L0_attn_conv_w, np.float32),
                        np.asarray(L0_attn_conv_b, np.float32),
                        np.asarray(L0_attn_dt_bias, np.float32),
                        np.asarray(L0_attn_A_log, np.float32),
                        np.asarray(L0_attn_D, np.float32),
                        np.asarray(L0_attn_norm_w, np.float32),
                        np.asarray(L0_ffn_scale, np.float32), NT=NT)
    in_maps = []
    for b in range(8):
        m = dict(base)
        m["x0T"] = np.ascontiguousarray(hs[b].T).reshape(NDC, 128, NT)
        in_maps.append(m)
    res = run_bass_kernel_spmd(nc, in_maps, core_ids=list(range(8)))
    out = np.stack([np.ascontiguousarray(
        res.results[b]["outT"].reshape(DM, NT).T) for b in range(8)])
    return out.astype(np.float32)


# version 12
# speedup vs baseline: 1.1111x; 1.0692x over previous
import numpy as np
from contextlib import ExitStack

import concourse.bass as bass
import concourse.mybir as mybir

F32 = mybir.dt.float32
BF16 = mybir.dt.bfloat16
ALU = mybir.AluOpType
AF = mybir.ActivationFunctionType

DM = 4096
NDC = 32
NRC = 145
NCC = 64
CS = 128
EPS = 1e-5

DEBUG = False


class Sem:
    def __init__(self, stack, nc, name):
        self.h = stack.enter_context(nc.semaphore(name))
        self.n = 0

    def add(self, k):
        self.n += k
        return self.n


class Eng:
    def __init__(self, eng, selfsem):
        self.e = eng
        self.s = selfsem

    def wait(self, sem, n):
        self.e.wait_ge(sem.h, n)

    def op(self, name, *args, dep=True, **kw):
        if dep is True:
            if self.s.n:
                self.e.wait_ge(self.s.h, self.s.n)
        elif dep is not None and dep > 0:
            self.e.wait_ge(self.s.h, dep)
        inst = getattr(self.e, name)(*args, **kw)
        inst.then_inc(self.s.h, 1)
        return self.s.add(1)

    def dma(self, out, in_, sem, **kw):
        inst = self.e.dma_start(out=out, in_=in_, **kw)
        inst.then_inc(sem.h, 16)
        return sem.add(16)


class MM:
    def __init__(self, eng, sem):
        self.e = eng
        self.s = sem

    def wait(self, sem, n):
        self.e.wait_ge(sem.h, n)

    def mm(self, out, lhsT, rhs, start, stop, mark=False, **kw):
        inst = self.e.matmul(out, lhsT, rhs, start=start, stop=stop, **kw)
        if mark:
            inst.then_inc(self.s.h, 1)
            return self.s.add(1)
        return None

    def tr(self, out, in_, ident, mark=False):
        inst = self.e.transpose(out, in_, ident)
        if mark:
            inst.then_inc(self.s.h, 1)
            return self.s.add(1)
        return None




def emit_phase_c(nc, S, Sem, ctx, stk):
    NT = ctx["NT"]; NCH = ctx["NCH"]
    cons = ctx["cons"]; identF = ctx["identF"]; identB = ctx["identB"]; maskT = ctx["maskT"]
    onesrowB = ctx["onesrowB"]; onecolB = ctx["onecolB"]; iot = ctx["iot"]
    nwTc = ctx["nwTc"]; Dex = ctx["Dex"]; dtv = ctx["dtv"]; Acol = ctx["Acol"]
    gates = ctx["gates"]; xcs = ctx["xcs"]; ygs = ctx["ygs"]; bncA = ctx["bncA"]; bncB = ctx["bncB"]

    sb = nc.sbuf_tensor
    xcb = [stk.enter_context(sb(f"xcb{i}", [128, 80 * 128], BF16)) for i in range(2)]
    gtb = stk.enter_context(sb("gtb", [128, 16 * 128], BF16))
    acs = [stk.enter_context(sb(f"acs{i}", [128, CS], F32)) for i in range(2)]
    decp = stk.enter_context(sb("decp", [128, CS], F32))
    adt_t = stk.enter_context(sb("adt_t", [128, CS], F32))
    sdf = stk.enter_context(sb("sdf", [128, CS], F32))
    decf = stk.enter_context(sb("decf", [128, CS], F32))
    dlc = [stk.enter_context(sb(f"dlc{i}", [128, 1], F32)) for i in range(2)]
    dl2 = [stk.enter_context(sb(f"dl2{i}", [2, 64], F32)) for i in range(2)]
    half01 = stk.enter_context(sb("half01", [2, 128], F32))
    i2 = stk.enter_context(sb("i2", [2, 128], mybir.dt.int32))
    h0t = stk.enter_context(sb("h0t", [2, 128], F32))
    sdT = stk.enter_context(sb("sdT", [128, 128], F32))
    decayT = stk.enter_context(sb("decayT", [128, 128], F32))
    dtT = stk.enter_context(sb("dtT", [128, 128], F32))
    ddtT = stk.enter_context(sb("ddtT", [128, 128], F32))
    dexpA = stk.enter_context(sb("dexpA", [128, 64 * NCH], F32))
    ahb = stk.enter_context(sb("ahb", [128, CS], BF16))
    ah32 = stk.enter_context(sb("ah32", [128, CS], F32))
    al32 = stk.enter_context(sb("al32", [128, CS], F32))
    albf = stk.enter_context(sb("albf", [128, CS], BF16))
    nahb = stk.enter_context(sb("nahb", [128, CS], BF16))
    nalb = stk.enter_context(sb("nalb", [128, CS], BF16))
    lhsT4 = stk.enter_context(sb("lhsT4", [4, 4096], BF16))
    rhs4 = stk.enter_context(sb("rhs4", [4, 4096], BF16))
    Lex = [stk.enter_context(sb(f"Lex{i}", [128, 1024], BF16)) for i in range(2)]
    Lt = stk.enter_context(sb("Lt", [128, 1024], F32))
    tril01 = stk.enter_context(sb("tril01", [128, 128], BF16))
    Msb = [stk.enter_context(sb(f"Msb{i}", [128, 1024], BF16)) for i in range(2)]
    xdt = [stk.enter_context(sb(f"xdt{i}", [128, 512], BF16)) for i in range(2)]
    xdd = [stk.enter_context(sb(f"xdd{i}", [128, 512], BF16)) for i in range(2)]
    Zs = [stk.enter_context(sb(f"Zsb{i}", [128, 512], F32)) for i in range(2)]
    BTsb = stk.enter_context(sb("BTsb", [128, 8 * 128], BF16))
    Gsb = stk.enter_context(sb("Gsb", [128, 8 * 128], BF16))
    Ssb = stk.enter_context(sb("Ssb", [128, 64 * 128], F32))
    Sbfst = [stk.enter_context(sb("Sbfs0", [128, 1024], BF16))] * 2
    STsb = stk.enter_context(sb("STsb", [128, 64 * 128], BF16))
    Ybf = stk.enter_context(sb("Ybf", [128, 64 * 128], BF16))
    rtmp = stk.enter_context(sb("rtmp", [128, 512], F32))
    dtmp = None
    y2st = [stk.enter_context(sb(f"y2st{i}", [128, 1024], BF16)) for i in range(2)]
    sgst = [stk.enter_context(sb("sgst0", [128, 1024], BF16))] * 2
    t1st = [stk.enter_context(sb(f"t1st{i}", [128, 1024], BF16)) for i in range(2)]
    ygst = [stk.enter_context(sb("ygst0", [128, 1024], BF16))] * 2
    rrow = stk.enter_context(sb("rrow", [1, 128], F32))
    rrowB = stk.enter_context(sb("rrowB", [1, 128], BF16))
    RSTD = stk.enter_context(sb("RSTD", [128, 128], F32))
    zeroB = stk.enter_context(sb("zeroB", [128, 128], F32))

    psL = stk.enter_context(nc.psum_tensor("psL", [128, 1024], F32))
    psY = [stk.enter_context(nc.psum_tensor(f"psYc{i}", [128, 512], F32)) for i in range(2)]
    psZ = stk.enter_context(nc.psum_tensor("psZ", [128, 512], F32))
    psSt = [stk.enter_context(nc.psum_tensor(f"psSt{i}", [128, 512], F32)) for i in range(2)]
    psT = [stk.enter_context(nc.psum_tensor("psTc0", [128, 1024], BF16))] * 2

    ldc = [Sem(stk, nc, f"ldc{i}") for i in range(2)]
    ldg = Sem(stk, nc, "ldg")
    bnc = Sem(stk, nc, "bnc")
    rla = Sem(stk, nc, "rla")
    rdl = [Sem(stk, nc, "rdl0"), Sem(stk, nc, "rdl1")]
    bncd = Sem(stk, nc, "bncd")
    wry = [Sem(stk, nc, f"wry{i}") for i in range(2)]

    class Marks(dict):
        strict = False
        def __missing__(self, key):
            if self.strict:
                raise KeyError(key)
            return 0
    M = Marks()

    def pool_stream(p, rec):
        k = p.op("iota", i2[:], pattern=[[1, 128]], base=0, channel_multiplier=-64)
        if rec: M["i2"] = k
        k = p.dma(xcb[0][:].rearrange("p (a t) -> p a t", a=80), xcs[0].rearrange("a p t -> p a t"), ldc[0])
        if rec: M["ldx", 0] = k
        for c in range(NCH):
            if c + 1 < NCH:
                if c >= 1:
                    p.wait(S["pe"], M["rnd_end", c - 1])
                    p.wait(S["dve"], M["yev", c - 1, 15])
                k = p.dma(xcb[(c + 1) % 2][:].rearrange("p (a t) -> p a t", a=80), xcs[c + 1].rearrange("a p t -> p a t"), ldc[(c + 1) % 2])
                if rec: M["ldx", c + 1] = k
            for h in range(4):
                if (c, h) != (0, 0):
                    pc, ph = (c, h * 2 - 1) if h >= 1 else (c - 1, 7)
                    p.wait(S["act"], M["silu", pc, ph])
                k = p.dma(gtb[:].rearrange("p (a t) -> p a t", a=16),
                          gates[c, h * 16:(h + 1) * 16].rearrange("a p t -> p a t"), ldg)
                if rec: M["ldg", c, h] = k

    def sync_stream(s, rec):
        s.wait(S["dve"], M["onesm"])
        k = s.dma(lhsT4[2:4, :], rhs4[0:2, :], rla)
        if rec: M["onesr"] = k
        s.wait(rla, M["onesr"])
        for c in range(NCH):
            s.wait(S["dve"], M["presplit", c])
            off = c * 128 * CS
            s.dma(bncB[0, off:off + 128 * CS].rearrange("(p l) -> p l", p=128), nahb[:], bnc)
            s.dma(bncB[1, off:off + 128 * CS].rearrange("(p l) -> p l", p=128), nalb[:], bnc)
            s.dma(bncB[2, off:off + 128 * CS].rearrange("(p l) -> p l", p=128), ahb[:], bnc)
            k = s.dma(bncB[3, off:off + 128 * CS].rearrange("(p l) -> p l", p=128), albf[:], bnc)
            if rec: M["bncw", c] = k
            s.wait(S["act"], M["predl", c])
            k = s.dma(bncA[2, c * 128:(c + 1) * 128].rearrange("(p o) -> p o", p=128), dlc[c % 2][:], bncd)
            if rec: M["bncdw", c] = k
            s.wait(bncd, M["bncdw", c])
            if c >= 2:
                s.wait(S["pe"], M["predmm", c - 2])
            k = s.dma(dl2[c % 2][:], bncA[2, c * 128:(c + 1) * 128].rearrange("(c q) -> q c", q=2),
                      rdl[c % 2], allow_slow_non_contiguous=True)
            if rec: M["rldl", c] = k
        for c in range(NCH):
            for q in range(4):
                s.wait(bnc, M["bncw", c])
                if (c, q) != (0, 0):
                    pc, pq = (c, q - 1) if q > 0 else (c - 1, 3)
                    s.wait(S["pe"], M["lmm", pc, pq * 4 + 3])
                off = c * 128 * CS + q * 4096
                s.dma(lhsT4[0:2, :], bncB[0:2, off:off + 4096], rla)
                k = s.dma(rhs4[2:4, :], bncB[2:4, off:off + 4096], rla)
                if rec: M["rlq", c, q] = k
            for gr in range(8):
                s.wait(S["dve"], M["yg", c, gr])
                for j in range(8):
                    k = s.dma(ygs[c, gr * 8 + j], ygst[0][:, j * 128:(j + 1) * 128], wry[gr % 2])
                if rec: M["wyg", c, gr] = k
        s.wait(wry[0], wry[0].n if not rec else M["wyg", NCH - 1, 6])
        s.wait(wry[1], wry[1].n if not rec else M["wyg", NCH - 1, 7])

    def pe_stream(p, rec):
        for c in range(NCH):
            p.wait(rdl[c % 2], M["rldl", c])
            k = p.mm(psZ[:, (c % 8) * 64:(c % 8) * 64 + 64], half01[:], dl2[c % 2][:], True, True,
                     mark=True)
            if rec: M["predmm", c] = k
            if rec and (c % 8 == 7 or c == NCH - 1):
                M["predexp", c // 8] = k
            if c % 8 == 7 and c + 1 < NCH:
                p.wait(S["dve"], M["dexpevA", c // 8])
        for c in range(NCH):
            p.wait(ldc[c % 2], M["ldx", c])
            if c >= 1:
                p.wait(S["dve"], M["yev", c - 1, 15])
            for g in range(8):
                k = p.mm(psY[g // 4][:, (g % 4) * 128:(g % 4 + 1) * 128],
                         xcb[c % 2][:, (64 + g) * 128:(65 + g) * 128],
                         xcb[c % 2][:, (72 + g) * 128:(73 + g) * 128],
                         True, True, mark=(g in (3, 7)))
                if rec and g == 3: M["gmm", c, 0] = k
                if rec and g == 7: M["gmm", c, 1] = k
            p.wait(S["act"], M["sd", c])
            if c >= 1:
                p.wait(S["dve"], M["rstdev", c - 1])
            p.tr(psSt[0][:, 0:128], sdf[:], identF[:])
            p.tr(psSt[0][:, 128:256], decf[:], identF[:])
            k = p.tr(psSt[0][:, 256:384], dtv[:, c * CS:(c + 1) * CS], identF[:], mark=True)
            if rec: M["ptr", c] = k
            if c >= 1:
                p.wait(S["dve"], M["stev", c - 1, 7])
            for g in range(8):
                k = p.tr(psT[0][:, g * 128:(g + 1) * 128],
                         xcb[c % 2][:, (64 + g) * 128:(65 + g) * 128], identB[:],
                         mark=(g == 7))
                if rec and g == 7: M["btr", c] = k
            for r in range(16):
                q = r // 4
                if c >= 1:
                    if r >= 1:
                        p.wait(S["dve"], M["zsev", c, r - 1])
                    elif c == 1:
                        p.wait(S["dve"], M["dexpfin"])
                    else:
                        p.wait(S["dve"], M["zsev", c - 1, 15])
                    g = r // 2
                    for j in range(4):
                        pp = r * 4 + j
                        k = p.mm(psZ[:, j * 128:(j + 1) * 128],
                                 xcb[c % 2][:, (72 + g) * 128:(73 + g) * 128],
                                 STsb[:, pp * 128:(pp + 1) * 128],
                                 True, True, mark=(j == 3))
                    if rec: M["zmm", c, r] = k
                p.wait(rla, M["rlq", c, q])
                if r == 0 and c == 0:
                    p.wait(rla, M["onesr"])
                if r >= 1:
                    p.wait(S["act"], M["lrelu", c, r - 1])
                elif c >= 1:
                    p.wait(S["act"], M["lrelu", c - 1, 15])
                for hh in range(8):
                    off = ((r % 4) * 8 + hh) * 128
                    col = slice(hh * 128, (hh + 1) * 128)
                    k = p.mm(psL[:, col], lhsT4[:, off:off + 128], rhs4[:, off:off + 128],
                             True, True, mark=(hh == 7))
                if rec: M["lmm", c, r] = k
                if r >= 1:
                    p.wait(S["dve"], M["xdtc", c, r - 1])
                else:
                    p.wait(S["dve"], M["btev", c])
                for j in range(4):
                    pp = r * 4 + j
                    k = p.tr(psT[0][:, j * 128:(j + 1) * 128],
                             xcb[c % 2][:, pp * 128:(pp + 1) * 128], identB[:], mark=(j == 3))
                if rec: M["xtr", c, r] = k
                if r >= 2:
                    p.wait(S["dve"], M["yev", c, r - 2])
                if r == 0:
                    p.wait(S["dve"], M["gev", c])
                if c >= 1:
                    p.wait(S["dve"], M["zsev", c, r])
                    for j in range(4):
                        k = p.mm(psY[r % 2][:, j * 128:(j + 1) * 128],
                                 Zs[r % 2][:, j * 128:(j + 1) * 128],
                                 identF[:], j == 0, False, is_transpose=True,
                                 skip_group_check=True, mark=(j == 3))
                    if rec: M["ztr", c, r] = k
                else:
                    for j in range(4):
                        k = p.mm(psY[r % 2][:, j * 128:(j + 1) * 128], zeroB[:],
                                 identF[:], j == 0, False, is_transpose=True,
                                 skip_group_check=True, mark=(j == 3))
                    if rec:
                        M["zmm", c, r] = None
                        M["ztr", c, r] = k
                p.wait(S["dve"], M["msb", c, r])
                for hh in range(8):
                    k = p.mm(psY[r % 2][(hh % 2) * 64:(hh % 2) * 64 + 64,
                                        (hh // 2) * 128:(hh // 2 + 1) * 128],
                             xdt[r % 2][:, hh * 64:(hh + 1) * 64],
                             Msb[r % 2][:, hh * 128:(hh + 1) * 128],
                             False, hh >= 6, skip_group_check=True, mark=(hh == 7))
                if rec: M["ydiag", c, r] = k
                p.wait(S["dve"], M["xdd", c, r])
                if r == 0:
                    p.wait(S["dve"], M["ptrev", c])
                    p.wait(S["dve"], M["btev", c])
                    p.wait(S["dve"], M["rstdev", c - 1] if c >= 1 else 0)
                elif r == 1:
                    p.wait(S["dve"], M["rec", c - 1, 15] if c >= 1 else 0)
                else:
                    p.wait(S["dve"], M["rec", c, r - 2])
                g = r // 2
                for j in range(4):
                    k = p.mm(psSt[r % 2][:, j * 128:(j + 1) * 128],
                             xdd[r % 2][:, j * 128:(j + 1) * 128],
                             BTsb[:, g * 128:(g + 1) * 128],
                             True, True, mark=(j == 3))
                if rec: M["smm", c, r] = k
                if rec and r == 15: M["rnd_end", c] = k
            p.wait(S["dve"], M["rec", c, 15])
            for gr in range(8):
                p.wait(S["dve"], M["y2", c, gr])
                for j in range(8):
                    k = p.mm(psSt[0][0:1, 0:128], onecolB[:],
                             y2st[0][:, j * 128:(j + 1) * 128],
                             gr == 0 and j == 0, gr == 7 and j == 7, mark=(j == 7))
                if rec: M["stat", c, gr] = k
            p.wait(S["dve"], M["rrowB", c])
            k = p.mm(psSt[0][:, 128:256], onesrowB[0:1, :], rrowB[0:1, :], True, True, mark=True)
            if rec: M["rbc", c] = k
            for sg in range(8):
                p.wait(S["act"], M["sbf", c, sg])
                if sg >= 1:
                    p.wait(S["dve"], M["stev", c, sg - 1])
                for j in range(8):
                    k = p.tr(psT[0][:, j * 128:(j + 1) * 128],
                             Sbfst[0][:, j * 128:(j + 1) * 128], identB[:], mark=(j == 7))
                if rec: M["str", c, sg] = k

    def dve_stream(d, rec):
        d.op("memset", Ssb[:], 0.0)
        d.op("memset", zeroB[:], 0.0)
        k = d.op("memset", rhs4[0:2, :], 1.0)
        M["onesm"] = k
        d.op("tensor_scalar", tril01[:], iot[:], 0.0, None, ALU.is_ge)
        d.wait(S["pool"], M["i2"])
        d.op("tensor_scalar", half01[:], i2[:], 0.0, None, ALU.is_ge)
        d.op("tensor_scalar", h0t[:], i2[:], 64.0, None, ALU.is_lt)
        d.op("tensor_tensor", half01[:], half01[:], h0t[:], ALU.mult)
        for c in range(NCH):
            if c >= 1:
                d.wait(bnc, M["bncw", c - 1])
            km = d.op("tensor_scalar", adt_t[:], dtv[:, c * CS:(c + 1) * CS], cons[:, 4:5], None,
                      ALU.mult, dep=M["prescan", c - 1] if c >= 1 else None)
            km = d.op("tensor_tensor_scan", acs[c % 2][:], cons[:, 0:1].to_broadcast((128, CS)),
                      adt_t[:], 0.0, ALU.mult, ALU.add, dep=km)
            if rec: M["prescan", c] = km
            km = d.op("tensor_copy", ahb[:], acs[c % 2][:], dep=km)
            km = d.op("tensor_copy", ah32[:], ahb[:], dep=km)
            km = d.op("tensor_tensor", al32[:], acs[c % 2][:], ah32[:], ALU.subtract, dep=km)
            km = d.op("tensor_copy", albf[:], al32[:], dep=km)
            km = d.op("tensor_scalar", nahb[:], ahb[:], -1.0, None, ALU.mult, dep=km)
            k = d.op("tensor_scalar", nalb[:], albf[:], -1.0, None, ALU.mult, dep=km)
            if rec: M["presplit", c] = k
            if c % 8 == 7 or c == NCH - 1:
                h = c // 8
                nn = min(8, NCH - 8 * h)
                d.wait(S["pe"], M["predexp", h])
                k = d.op("tensor_copy", dexpA[:, h * 512:h * 512 + nn * 64], psZ[:, 0:nn * 64])
                if rec: M["dexpevA", h] = k
                if rec and c == NCH - 1: M["dexpfin"] = k
        for c in range(NCH):
            if c >= 2:
                d.wait(S["act"], M["sd", c - 2])
            km = d.op("tensor_scalar", adt_t[:], dtv[:, c * CS:(c + 1) * CS], cons[:, 4:5], None,
                      ALU.mult, dep=M["dexpfin"] if c <= 1 else (M["scan", c - 1] if c >= 1 else None))
            km = d.op("tensor_tensor_scan", acs[c % 2][:], cons[:, 0:1].to_broadcast((128, CS)),
                      adt_t[:], 0.0, ALU.mult, ALU.add, dep=km)
            if rec: M["scan", c] = km
            d.op("tensor_scalar", decp[:], acs[c % 2][:], acs[c % 2][:, CS - 1:CS], None,
                 ALU.subtract, dep=km)
            if rec: M["decp", c] = d.s.n
            d.wait(S["pe"], M["ptr", c])
            d.op("tensor_copy", sdT[:], psSt[0][:, 0:128])
            d.op("tensor_copy", decayT[:], psSt[0][:, 128:256])
            d.op("tensor_copy", dtT[:], psSt[0][:, 256:384])
            k = d.op("tensor_tensor", ddtT[:], dtT[:], decayT[:], ALU.mult)
            if rec: M["ptrev", c] = k
            d.wait(S["pe"], M["gmm", c, 1])
            d.op("tensor_tensor", Gsb[:, 0:512].rearrange("p (a b) -> p a b", a=4),
                 psY[0][:, :].rearrange("p (a b) -> p a b", a=4),
                 tril01[:].unsqueeze(1).to_broadcast((128, 4, 128)), ALU.mult)
            k = d.op("tensor_tensor", Gsb[:, 512:1024].rearrange("p (a b) -> p a b", a=4),
                 psY[1][:, :].rearrange("p (a b) -> p a b", a=4),
                 tril01[:].unsqueeze(1).to_broadcast((128, 4, 128)), ALU.mult)
            if rec: M["gev", c] = k
            d.wait(S["pe"], M["btr", c])
            k = d.op("tensor_copy", BTsb[:, :], psT[0][:, :])
            if rec: M["btev", c] = k
            def rec_yev(r):
                nonlocal_marker = None
                d.wait(S["pe"], M["smm", c, r])
                km = d.op("tensor_tensor", rtmp[:].rearrange("p (a b) -> p a b", a=4),
                     Ssb[:, r * 512:(r + 1) * 512].rearrange("p (a b) -> p a b", a=4),
                     dexpA[:, c * 64 + r * 4:c * 64 + r * 4 + 4].unsqueeze(2).to_broadcast((128, 4, 128)), ALU.mult,
                     dep=M["yev", c, r - 1] if r >= 1 else (M["yev", c - 1, 15] if c >= 1 else None))
                k = d.op("tensor_tensor", Ssb[:, r * 512:(r + 1) * 512], rtmp[:], psSt[r % 2][:, :],
                         ALU.add, dep=km)
                if rec: M["rec", c, r] = k
                d.wait(S["pe"], M["ydiag", c, r])
                km = d.op("tensor_tensor", rtmp[:].rearrange("p (a b) -> p a b", a=4),
                     xcb[c % 2][:, r * 512:(r + 1) * 512].rearrange("p (a b) -> p a b", a=4),
                     Dex[:, r * 4:r * 4 + 4].unsqueeze(2).to_broadcast((128, 4, 128)), ALU.mult,
                     dep=M["rec", c, r])
                k = d.op("tensor_tensor", Ybf[:, r * 512:(r + 1) * 512], psY[r % 2][:, :],
                         rtmp[:], ALU.add, dep=km)
                if rec: M["yev", c, r] = k

            for r in range(16):
                if c >= 1:
                    d.wait(S["pe"], M["zmm", c, r])
                    if r >= 1:
                        d.wait(S["pe"], M["ztr", c, r - 1])
                    k = d.op("tensor_tensor", Zs[r % 2][:].rearrange("p (a b) -> p a b", a=8),
                             psZ[:].rearrange("p (a b) -> p a b", a=8),
                             sdT[:, r * 8:r * 8 + 8].unsqueeze(2).to_broadcast((128, 8, 64)), ALU.mult,
                             dep=M["zsev", c, r - 2] if r >= 2 else None)
                    if rec: M["zsev", c, r] = k
                elif rec:
                    M["zsev", c, r] = None
                d.wait(S["pe"], M["xtr", c, r])
                k = d.op("tensor_tensor", xdt[r % 2][:].rearrange("p (a b) -> p a b", a=8),
                         psT[0][:, 0:512].rearrange("p (a b) -> p a b", a=8),
                         dtT[:, r * 8:r * 8 + 8].unsqueeze(2).to_broadcast((128, 8, 64)), ALU.mult,
                         dep=M["xdd", c, r - 2] if r >= 2 else None)
                if rec:
                    M["xdt", c, r] = k
                    M["xdtc", c, r] = k
                k = d.op("tensor_tensor", xdd[r % 2][:].rearrange("p (a b) -> p a b", a=8),
                         xdt[r % 2][:].rearrange("p (a b) -> p a b", a=8),
                         decayT[:, r * 8:r * 8 + 8].unsqueeze(2).to_broadcast((128, 8, 64)), ALU.mult,
                         dep=M["xdt", c, r])
                if rec: M["xdd", c, r] = k
                d.wait(S["act"], M["lex", c, r])
                if r >= 2:
                    d.wait(S["pe"], M["ydiag", c, r - 2])
                g = r // 2
                k = d.op("tensor_tensor", Msb[r % 2][:].rearrange("p (a b) -> p a b", a=8),
                         Lex[r % 2][:].rearrange("p (a b) -> p a b", a=8),
                         Gsb[:, g * 128:(g + 1) * 128].unsqueeze(1).to_broadcast((128, 8, 128)),
                         ALU.mult,
                         dep=M["msb", c, r - 2] if r >= 2 else (M["gev", c] if r == 0 else None))
                if rec: M["msb", c, r] = k
                if r >= 1:
                    rec_yev(r - 1)
            rec_yev(15)
            for gr in range(8):
                if gr >= 2:
                    d.wait(S["pe"], M["stat", c, gr - 2])
                dd = M["yev", c, 2 * gr + 1]
                if gr >= 1:
                    dd = max(dd, M["y2", c, gr - 1])
                elif c >= 1:
                    dd = max(dd, M["y2", c - 1, 7])
                if gr >= 1:
                    d.wait(S["pe"], M["stat", c, gr - 1])
                k = d.op("tensor_tensor", y2st[0][:], Ybf[:, gr * 1024:(gr + 1) * 1024],
                         Ybf[:, gr * 1024:(gr + 1) * 1024], ALU.mult, dep=dd)
                if rec: M["y2", c, gr] = k
            d.wait(S["pe"], M["stat", c, 7])
            d.op("tensor_scalar", rrow[0:1, :], psSt[0][0:1, 0:128], 1.0 / 8192.0, None, ALU.mult)
            if rec: M["msq", c] = d.s.n
            d.wait(S["act"], M["rstd", c])
            k = d.op("tensor_copy", rrowB[0:1, :], rrow[0:1, :])
            if rec: M["rrowB", c] = k
            d.wait(S["pe"], M["rbc", c])
            k = d.op("tensor_copy", RSTD[:], psSt[0][:, 128:256])
            if rec: M["rstdev", c] = k
            for sg in range(8):
                d.wait(S["pe"], M["str", c, sg])
                k = d.op("tensor_copy", STsb[:, sg * 1024:(sg + 1) * 1024], psT[0][:, :],
                         dep=None)
                if rec: M["stev", c, sg] = k
            for gr in range(8):
                d.wait(S["act"], M["silu", c, gr])
                if (c, gr) >= (0, 1):
                    pc, pg = (c, gr - 1) if gr >= 1 else (c - 1, 7)
                    if pc >= 0:
                        d.wait(wry[pg % 2], M["wyg", pc, pg])
                dd = M["yg", c, gr - 1] if gr >= 1 else (M["yg", c - 1, 7] if c >= 1 else None)
                km = d.op("tensor_tensor", t1st[gr % 2][:].rearrange("p (a b) -> p a b", a=8),
                     Ybf[:, gr * 1024:(gr + 1) * 1024].rearrange("p (a b) -> p a b", a=8),
                     RSTD[:].unsqueeze(1).to_broadcast((128, 8, 128)), ALU.mult, dep=dd)
                km = d.op("tensor_tensor", t1st[gr % 2][:].rearrange("p (a b) -> p a b", a=8),
                     t1st[gr % 2][:].rearrange("p (a b) -> p a b", a=8),
                     nwTc[:, gr * 8:(gr + 1) * 8].unsqueeze(2).to_broadcast((128, 8, 128)), ALU.mult, dep=km)
                k = d.op("tensor_tensor", ygst[0][:], t1st[gr % 2][:], sgst[0][:], ALU.mult, dep=km)
                if rec: M["yg", c, gr] = k

    def act_stream(a, rec):
        for c in range(NCH):
            a.wait(S["dve"], M["prescan", c])
            if c >= 2:
                a.wait(bncd, M["bncdw", c - 2])
            k = a.op("activation", dlc[c % 2][:], acs[c % 2][:, CS - 1:CS], AF.Exp,
                     dep=M["predl", c - 2] if c >= 2 else None)
            if rec: M["predl", c] = k
        for c in range(NCH):
            a.wait(S["dve"], M["decp", c])
            a.op("activation", sdf[:], acs[c % 2][:], AF.Exp,
                 dep=M["sd", c - 1] if c >= 1 else None)
            k = a.op("activation", decf[:], decp[:], AF.Exp, scale=-1.0, dep=None)
            if rec: M["sd", c] = k
            for r in range(16):
                a.wait(S["pe"], M["lmm", c, r])
                km = a.op("activation", Lt[:], psL[:], AF.Relu, scale=-1.0,
                          dep=M["lex", c, r - 1] if r >= 1 else (M["lex", c - 1, 15] if c >= 1 else None))
                if rec: M["lrelu", c, r] = km
                if r >= 2:
                    a.wait(S["dve"], M["msb", c, r - 2])
                k = a.op("activation", Lex[r % 2][:], Lt[:], AF.Exp, scale=-1.0, dep=km)
                if rec: M["lex", c, r] = k
            a.wait(S["dve"], M["msq", c])
            a.op("activation", rrow[0:1, :], rrow[0:1, :], AF.Ln, bias=cons[0:1, 2:3])
            k = a.op("activation", rrow[0:1, :], rrow[0:1, :], AF.Exp, scale=-0.5)
            if rec: M["rstd", c] = k
            for sg in range(8):
                a.wait(S["dve"], M["rec", c, 2 * sg + 1])
                if sg >= 1:
                    a.wait(S["pe"], M["str", c, sg - 1])
                k = a.op("activation", Sbfst[0][:], Ssb[:, sg * 1024:(sg + 1) * 1024], AF.Copy,
                         dep=M["sbf", c, sg - 1] if sg >= 1 else (M["sbf", c - 1, 7] if c >= 1 else None))
                if rec: M["sbf", c, sg] = k
            for gr in range(8):
                a.wait(ldg, M["ldg", c, gr // 2])
                if gr >= 1:
                    a.wait(S["dve"], M["yg", c, gr - 1])
                elif c >= 1:
                    a.wait(S["dve"], M["yg", c - 1, 7])
                k = a.op("activation", sgst[0][:], gtb[:, (gr % 2) * 1024:(gr % 2 + 1) * 1024], AF.Silu,
                         dep=M["silu", c, gr - 1] if gr >= 1 else (M["silu", c - 1, 7] if c >= 1 else None))
                if rec: M["silu", c, gr] = k

    class CntEng:
        def __init__(self, key, cnt, dmac):
            self.key = key; self.c = cnt; self.d = dmac
            class _S:
                pass
            self.s = _S()
            self.s.n = cnt.get(key, 0)
        def op(self, *a, dep=None, **k):
            self.c[self.key] += 1
            self.s.n = self.c[self.key]
            return self.c[self.key]
        def dma(self, out, in_, sem, **kw):
            key = id(sem)
            self.d[key] = self.d.get(key, sem.n) + 16
            return self.d[key]
        def wait(self, *a, **k):
            pass
        def mm(self, *a, mark=False, **k):
            if mark:
                self.c["pe"] += 1
                return self.c["pe"]
            return None
        def tr(self, *a, mark=False, **k):
            if mark:
                self.c["pe"] += 1
                return self.c["pe"]
            return None

    cnt = {"pe": S["pe"].n, "dve": S["dve"].n, "act": S["act"].n, "pool": S["pool"].n}
    dmac = {}
    pool_stream(CntEng("pool", cnt, dmac), True)
    sync_stream(CntEng("sync", {"sync": 0, **cnt}, dmac), True)
    pe_stream(CntEng("pe", cnt, dmac), True)
    dve_stream(CntEng("dve", cnt, dmac), True)
    act_stream(CntEng("act", cnt, dmac), True)

    M.strict = True
    M["final"] = dict(cnt)
    Eng = ctx["Eng"]; MMc = ctx["MM"]
    with nc.Block() as blk:
        @blk.gpsimd
        def _(g):
            pool_stream(Eng(g, S["pool"]), False)

        @blk.sync
        def _(sy):
            sync_stream(Eng(sy, None), False)

        @blk.tensor
        def _(t):
            pe_stream(MMc(t, S["pe"]), False)

        @blk.vector
        def _(v):
            dve_stream(Eng(v, S["dve"]), False)

        @blk.scalar
        def _(a):
            act_stream(Eng(a, S["act"]), False)

    fin = M["final"]
    for e in ("pe", "dve", "act", "pool"):
        assert S[e].n == fin[e], f"count mismatch {e}: emit={S[e].n} count={fin[e]}"
    return M


def build(NT=2048, debug_names=()):
    assert NT % 512 == 0
    NTT = NT // 512
    NCH = NT // CS
    nc = bass.Bass("TRN2", target_bir_lowering=False, debug=False)

    x0T = nc.dram_tensor("x0T", [NDC, 128, NT], F32, kind="ExternalInput").ap()
    WinB = nc.dram_tensor("WinB", [NRC, 128, NDC, 128], F32, kind="ExternalInput").ap()
    WoutB = nc.dram_tensor("WoutB", [NDC, 128, NCC, 128], F32, kind="ExternalInput").ap()
    SgB = nc.dram_tensor("SgB", [NDC, 128, NDC, 128], F32, kind="ExternalInput").ap()
    convW = nc.dram_tensor("convW", [128, 80, 4], F32, kind="ExternalInput").ap()
    convBi = nc.dram_tensor("convBi", [128, 80], F32, kind="ExternalInput").ap()
    dtBi = nc.dram_tensor("dtBi", [128, 1], F32, kind="ExternalInput").ap()
    Alog = nc.dram_tensor("Alog", [128, 1], F32, kind="ExternalInput").ap()
    Dexp = nc.dram_tensor("Dexp", [128, 64], F32, kind="ExternalInput").ap()
    nwT = nc.dram_tensor("nwT", [128, NCC], F32, kind="ExternalInput").ap()
    scal = nc.dram_tensor("scal", [1, 1], F32, kind="ExternalInput").ap()

    outT = nc.dram_tensor("outT", [NDC, 128, NT], F32, kind="ExternalOutput").ap()

    x1s = nc.dram_tensor("x1s", [NDC, 128, NT], F32).ap()
    x2s = nc.dram_tensor("x2s", [NDC, 128, NT], F32).ap()
    gates = nc.dram_tensor("gates", [NCH, 64, 128, 128], BF16).ap()
    xcs = nc.dram_tensor("xcs", [NCH, 80, 128, 128], BF16).ap()
    ygs = nc.dram_tensor("ygs", [NCH, 64, 128, 128], BF16).ap()
    bncA = nc.dram_tensor("bncA", [4, 16384 * 2], F32).ap()
    bncB = nc.dram_tensor("bncB", [4, 16384 * (NT // 128)], BF16).ap()

    dbg = {}
    if DEBUG:
        def dbgt(nm, shape, dt=F32):
            dbg[nm] = nc.dram_tensor(nm, shape, dt, kind="ExternalOutput").ap()
        if "x1" in debug_names: dbgt("dbg_x1", [NDC, 128, NT])
        if "xcs" in debug_names: dbgt("dbg_xcs", [NCH, 80, 128, 128], BF16)
        if "gates" in debug_names: dbgt("dbg_gates", [NCH, 64, 128, 128], BF16)
        if "dtv" in debug_names: dbgt("dbg_dtv", [128, NT])
        if "yg" in debug_names: dbgt("dbg_yg", [NCH, 64, 128, 128], BF16)
        if "x2" in debug_names: dbgt("dbg_x2", [NDC, 128, NT])

    st = ExitStack()

    cons = st.enter_context(nc.sbuf_tensor("cons", [128, 16], F32))
    identB = st.enter_context(nc.sbuf_tensor("identB", [128, 128], BF16))
    identF = st.enter_context(nc.sbuf_tensor("identF", [128, 128], F32))
    maskT = st.enter_context(nc.sbuf_tensor("maskT", [128, 128], F32))
    onesrowB = st.enter_context(nc.sbuf_tensor("onesrowB", [1, 128], BF16))
    onecolB = st.enter_context(nc.sbuf_tensor("onecolB", [128, 1], BF16))
    iot = st.enter_context(nc.sbuf_tensor("iot", [128, 128], mybir.dt.int32))
    cwcol = st.enter_context(nc.sbuf_tensor("cwcol", [128, 320], F32))
    cbcol = st.enter_context(nc.sbuf_tensor("cbcol", [128, 80], F32))
    nwTc = st.enter_context(nc.sbuf_tensor("nwTc", [128, NCC], F32))
    Dex = st.enter_context(nc.sbuf_tensor("Dex", [128, 64], F32))
    dtv = st.enter_context(nc.sbuf_tensor("dtv", [128, NT], F32))

    S = {}
    for nm in ["pe", "dve", "act", "pool", "ldS", "fin"]:
        S[nm] = Sem(st, nc, nm)
    marks = {}

    with nc.Block() as blk:
        @blk.sync
        def _(sy):
            s = Eng(sy, None)
            s.dma(cwcol[:], convW.rearrange("p a b -> p (a b)"), S["ldS"])
            s.dma(cbcol[:], convBi[:], S["ldS"])
            s.dma(nwTc[:], nwT[:], S["ldS"])
            s.dma(cons[:, 3:4], dtBi[:], S["ldS"])
            s.dma(cons[:, 7:8], Alog[:], S["ldS"])
            s.dma(Dex[:], Dexp[:], S["ldS"])
            s.dma(cons[:, 8:9], scal[0:1, 0:1].to_broadcast((128, 1)), S["ldS"])

        @blk.gpsimd
        def _(g):
            p = Eng(g, S["pool"])
            marks["iota"] = p.op("iota", iot[:], pattern=[[1, 128]], base=0, channel_multiplier=-1)

        @blk.vector
        def _(v):
            d = Eng(v, S["dve"])
            d.wait(S["pool"], marks["iota"])
            d.op("memset", cons[:, 0:1], 1.0)
            d.op("memset", cons[:, 2:3], EPS)
            d.op("memset", cons[:, 6:7], 1.0 / DM)
            d.op("memset", onesrowB[:], 1.0)
            d.op("memset", onecolB[:], 1.0)
            d.op("tensor_scalar", identF[:], iot[:], 0.0, None, ALU.is_equal)
            d.op("tensor_scalar", identB[:], iot[:], 0.0, None, ALU.is_equal)
            d.op("tensor_scalar", maskT[:], iot[:], 0.0, 1.0, ALU.is_ge, ALU.subtract)
            marks["msk"] = d.op("tensor_scalar", maskT[:], maskT[:], 1e30, None, ALU.mult)

        @blk.scalar
        def _(a):
            c = Eng(a, S["act"])
            c.wait(S["ldS"], S["ldS"].n)
            c.wait(S["dve"], marks["msk"])
            c.op("activation", cons[:, 4:5], cons[:, 7:8], AF.Exp)
            c.op("activation", cons[:, 9:10], cons[:, 8:9], AF.Abs)
            c.op("activation", cons[:, 10:11], cons[:, 9:10], AF.Exp, scale=-1.0)
            marks["sp"] = c.op("activation", cons[:, 11:12], cons[:, 10:11], AF.Ln, bias=cons[:, 0:1])

        @blk.vector
        def _(v):
            d = Eng(v, S["dve"])
            d.wait(S["act"], marks["sp"])
            d.op("tensor_scalar", cons[:, 4:5], cons[:, 4:5], -1.0, None, ALU.mult)
            d.op("tensor_scalar", cons[:, 12:13], cons[:, 8:9], 0.0, None, ALU.max)
            d.op("tensor_tensor", cons[:, 13:14], cons[:, 12:13], cons[:, 11:12], ALU.add)
            d.op("tensor_scalar", cons[:, 1:2], cons[:, 13:14], 0.5, None, ALU.mult)

    def ln_phase(src, xln, tag):
        stk = ExitStack()
        tstk = ExitStack()
        xf = [stk.enter_context(nc.sbuf_tensor(f"xf{tag}{i}", [128, NT], F32)) for i in range(2)]
        Abc = stk.enter_context(nc.sbuf_tensor(f"Abc{tag}", [128, NT], BF16))
        Bbc = stk.enter_context(nc.sbuf_tensor(f"Bbc{tag}", [128, NT], BF16))
        xbf = tstk.enter_context(nc.sbuf_tensor(f"xb{tag}", [128, NT], BF16))
        sqb = tstk.enter_context(nc.sbuf_tensor(f"sq{tag}", [128, NT], BF16))
        lnr = tstk.enter_context(nc.sbuf_tensor(f"lnr{tag}", [1, 2 * NT], F32))
        lnrB = tstk.enter_context(nc.sbuf_tensor(f"lnrB{tag}", [1, 2 * NT], BF16))
        ldx = [Sem(stk, nc, f"ldx{tag}0"), Sem(stk, nc, f"ldx{tag}1")]
        ldm = {}

        pstk = ExitStack()
        psM = pstk.enter_context(nc.psum_tensor(f"psM{tag}", [1, NT], F32))
        psS = pstk.enter_context(nc.psum_tensor(f"psS{tag}", [1, NT], F32))

        pe0, dve0, act0 = S["pe"].n, S["dve"].n, S["act"].n
        mmstm = lambda dc: pe0 + dc + 1
        castm = lambda dc: dve0 + dc + 1
        sqm = lambda dc: act0 + dc + 1

        with nc.Block() as blk:
            @blk.sync
            def _(sy):
                s = Eng(sy, None)
                for dc in range(NDC):
                    if dc >= 2:
                        s.wait(S["pe"], mmstm(dc - 2))
                    ldm[dc] = s.dma(xf[dc % 2][:], src[dc], ldx[dc % 2])

            @blk.vector
            def _(v):
                d = Eng(v, S["dve"])
                for dc in range(NDC):
                    d.wait(ldx[dc % 2], ldm[dc])
                    if dc >= 1:
                        d.wait(S["pe"], mmstm(dc - 1))
                    assert d.op("tensor_copy", xbf[:], xf[dc % 2][:]) == castm(dc)

            @blk.scalar
            def _(a):
                c = Eng(a, S["act"])
                for dc in range(NDC):
                    c.wait(ldx[dc % 2], ldm[dc])
                    if dc >= 1:
                        c.wait(S["pe"], mmstm(dc - 1))
                    assert c.op("activation", sqb[:], xf[dc % 2][:], AF.Square) == sqm(dc)

            @blk.tensor
            def _(t):
                p = MM(t, S["pe"])
                for dc in range(NDC):
                    p.wait(S["dve"], castm(dc))
                    p.wait(S["act"], sqm(dc))
                    for tt in range(NTT):
                        p.mm(psM[0:1, tt * 512:(tt + 1) * 512], onecolB[:],
                             xbf[:, tt * 512:(tt + 1) * 512], dc == 0, dc == NDC - 1)
                    for tt in range(NTT):
                        mk = p.mm(psS[0:1, tt * 512:(tt + 1) * 512], onecolB[:],
                                  sqb[:, tt * 512:(tt + 1) * 512], dc == 0, dc == NDC - 1,
                                  mark=(tt == NTT - 1))
                    assert mk == mmstm(dc)

            @blk.vector
            def _(v):
                d = Eng(v, S["dve"])
                d.wait(S["pe"], mmstm(NDC - 1))
                d.op("tensor_scalar", lnr[0:1, 0:NT], psM[0:1, :], 1.0 / DM, None, ALU.mult)
                d.op("tensor_scalar", lnr[0:1, NT:2 * NT], psS[0:1, :], 1.0 / DM, None, ALU.mult)
                d.op("tensor_tensor", sqb[0:1, 0:NT], lnr[0:1, 0:NT], lnr[0:1, 0:NT], ALU.mult)
                marks["var"] = d.op("tensor_tensor", lnr[0:1, NT:2 * NT], lnr[0:1, NT:2 * NT],
                                    sqb[0:1, 0:NT], ALU.subtract)

            @blk.scalar
            def _(a):
                c = Eng(a, S["act"])
                c.wait(S["dve"], marks["var"])
                c.op("activation", lnr[0:1, NT:2 * NT], lnr[0:1, NT:2 * NT], AF.Ln, bias=cons[0:1, 2:3])
                marks["rstd"] = c.op("activation", lnr[0:1, NT:2 * NT], lnr[0:1, NT:2 * NT], AF.Exp, scale=-0.5)

            @blk.vector
            def _(v):
                d = Eng(v, S["dve"])
                d.wait(S["act"], marks["rstd"])
                d.op("tensor_tensor", lnr[0:1, 0:NT], lnr[0:1, 0:NT], lnr[0:1, NT:2 * NT], ALU.mult)
                d.op("tensor_scalar", lnr[0:1, 0:NT], lnr[0:1, 0:NT], -1.0, None, ALU.mult)
                d.op("tensor_copy", lnrB[0:1, 0:NT], lnr[0:1, NT:2 * NT])
                marks["rows"] = d.op("tensor_copy", lnrB[0:1, NT:2 * NT], lnr[0:1, 0:NT])

        pstk.close()
        psBA = tstk.enter_context(nc.psum_tensor(f"psBA{tag}", [128, 512], F32))
        psBB = tstk.enter_context(nc.psum_tensor(f"psBB{tag}", [128, 512], F32))
        dve0b = S["dve"].n
        bcev = lambda tt: dve0b + 2 * (tt + 1)
        with nc.Block() as blk:
            @blk.tensor
            def _(t):
                p = MM(t, S["pe"])
                p.wait(S["dve"], marks["rows"])
                for tt in range(NT // 512):
                    if tt >= 1:
                        p.wait(S["dve"], bcev(tt - 1))
                    mka = p.mm(psBA[:, :], onesrowB[0:1, :],
                               lnrB[0:1, tt * 512:(tt + 1) * 512], True, True, mark=True)
                    mkb = p.mm(psBB[:, :], onesrowB[0:1, :],
                               lnrB[0:1, NT + tt * 512:NT + (tt + 1) * 512], True, True, mark=True)
                    marks[f"bc{tag}{tt}"] = (mka, mkb)

            @blk.vector
            def _(v):
                d = Eng(v, S["dve"])
                for tt in range(NT // 512):
                    mka, mkb = marks[f"bc{tag}{tt}"]
                    d.wait(S["pe"], mka)
                    d.op("tensor_copy", Abc[:, tt * 512:(tt + 1) * 512], psBA[:, :], dep=True)
                    d.wait(S["pe"], mkb)
                    k = d.op("tensor_copy", Bbc[:, tt * 512:(tt + 1) * 512], psBB[:, :], dep=True)
                    assert k == bcev(tt)
        tstk.close()
        return dict(xf=xf, ldx=ldx, ldm=ldm, Abc=Abc, Bbc=Bbc, stk=stk, tag=tag, src=src)

    def ffn_phase(xln, res, dst, tag, lnd, dbg_out=None):
        stk = ExitStack()
        sgb = [stk.enter_context(nc.sbuf_tensor(f"sg{tag}{i}", [128, NDC * 128], BF16)) for i in range(2)]
        xr = [stk.enter_context(nc.sbuf_tensor(f"xr{tag}0", [128, NT], F32))] * 2
        x1t = [stk.enter_context(nc.sbuf_tensor(f"x1t{tag}0", [128, NT], F32))] * 2
        psF = [stk.enter_context(nc.psum_tensor(f"psF{tag}{i}", [128, NT], F32)) for i in range(2)]
        ldw = [Sem(stk, nc, f"ldw{tag}0"), Sem(stk, nc, f"ldw{tag}1")]
        ldr = [Sem(stk, nc, f"ldr{tag}0"), Sem(stk, nc, f"ldr{tag}1")]
        wr = [Sem(stk, nc, f"wr{tag}0"), Sem(stk, nc, f"wr{tag}1")]
        lwm, lrm = {}, {}
        nst = 2 if dbg_out is not None else 1
        xf, ldx, ldm, Abc, Bbc = lnd["xf"], lnd["ldx"], lnd["ldm"], lnd["Abc"], lnd["Bbc"]
        lsrc = lnd["src"]
        ldx2 = Sem(stk, nc, f"ldx2{tag}")

        pe0, dve0 = S["pe"].n, S["dve"].n
        aplm = lambda dc: dve0 + 2 * (dc + 1)
        mmfc = lambda oc: pe0 + oc + 1
        evm = lambda oc: dve0 + 2 * NDC + oc + 1
        wrm = lambda oc: 16 * nst * (oc // 2 + 1)

        with nc.Block() as blk:
            @blk.sync
            def _(sy):
                s = Eng(sy, None)
                for dc in range(NDC):
                    if dc >= 2:
                        s.wait(S["dve"], aplm(dc - 2))
                    ldm[(1, dc)] = s.dma(xf[dc % 2][:], lsrc[dc], ldx[0] if dc % 2 == 0 else ldx2)
                for oc in range(NDC):
                    if oc >= 1:
                        s.wait(S["dve"], evm(oc - 1))
                    lrm[oc] = s.dma(xr[0][:], res[oc], ldr[oc % 2])

            @blk.gpsimd
            def _(g):
                p = Eng(g, S["pool"])
                for oc in range(NDC):
                    if oc >= 2:
                        p.wait(S["pe"], mmfc(oc - 2))
                    lwm[oc] = p.dma(sgb[oc % 2][:], SgB[oc].rearrange("p a b -> p (a b)"), ldw[oc % 2])

            @blk.vector
            def _(v):
                d = Eng(v, S["dve"])
                for dc in range(NDC):
                    d.wait(ldx[0] if dc % 2 == 0 else ldx2, ldm[(1, dc)])
                    km = d.op("tensor_tensor", xf[dc % 2][:], xf[dc % 2][:], Abc[:], ALU.mult,
                              dep=aplm(dc - 2) if dc >= 2 else None)
                    assert d.op("tensor_tensor", xln[:, dc * NT:(dc + 1) * NT], xf[dc % 2][:],
                                Bbc[:], ALU.add, dep=km) == aplm(dc)
                for oc in range(NDC):
                    d.wait(S["pe"], mmfc(oc))
                    d.wait(ldr[oc % 2], lrm[oc])
                    if oc >= 1:
                        d.wait(wr[(oc - 1) % 2], wrm(oc - 1))
                    assert d.op("scalar_tensor_tensor", x1t[0][:], psF[oc % 2][:],
                                cons[:, 1:2], xr[oc % 2][:], ALU.mult, ALU.add,
                                dep=None) == evm(oc)

            @blk.tensor
            def _(t):
                p = MM(t, S["pe"])
                for oc in range(NDC):
                    p.wait(ldw[oc % 2], lwm[oc])
                    if oc >= 2:
                        p.wait(S["dve"], evm(oc - 2))
                    for ic in range(NDC):
                        if oc == 0:
                            p.wait(S["dve"], aplm(ic))
                        for tt in range(NTT):
                            mk = p.mm(psF[oc % 2][:, tt * 512:(tt + 1) * 512],
                                      sgb[oc % 2][:, ic * 128:(ic + 1) * 128],
                                      xln[:, ic * NT + tt * 512:ic * NT + (tt + 1) * 512],
                                      ic == 0, ic == NDC - 1,
                                      mark=(ic == NDC - 1 and tt == NTT - 1))
                    assert mk == mmfc(oc)

            @blk.scalar
            def _(a):
                c = Eng(a, S["act"])
                for oc in range(NDC):
                    c.wait(S["dve"], evm(oc))
                    c.dma(dst[oc], x1t[0][:], wr[oc % 2])
                    if dbg_out is not None:
                        c.dma(dbg_out[oc], x1t[0][:], wr[oc % 2])
                c.wait(wr[0], wr[0].n)
                c.wait(wr[1], wr[1].n)
        stk.close()
        lnd["stk"].close()

    def proj_phase(xln, lnd):
        stk = ExitStack()
        wb = [stk.enter_context(nc.sbuf_tensor(f"wbp{i}", [128, NDC * 128], BF16)) for i in range(2)]
        gevac = [stk.enter_context(nc.sbuf_tensor("gev0", [128, NT], BF16))] * 2
        xevac = [stk.enter_context(nc.sbuf_tensor("xev0", [128, NT], BF16))] * 2
        convraw = stk.enter_context(nc.sbuf_tensor("convraw", [128, NT + 3], F32))
        convacc = stk.enter_context(nc.sbuf_tensor("convacc", [128, NT], F32))
        psW = [stk.enter_context(nc.psum_tensor(f"psW{i}", [128, NT], F32)) for i in range(2)]
        ldwB = [Sem(stk, nc, f"ldwB{i}") for i in range(2)]
        wrG = [Sem(stk, nc, f"wrG{i}") for i in range(2)]
        wrX = [Sem(stk, nc, f"wrX{i}") for i in range(2)]
        lwm = {}

        xfL, ldxL, ldmL, AbcL, BbcL = lnd["xf"], lnd["ldx"], lnd["ldm"], lnd["Abc"], lnd["Bbc"]
        lsrc = lnd["src"]
        ldx0b = ldxL[0].n
        ldme = lambda dc: ldx0b + 16 * (dc // 2 + 1)
        ldx2p = Sem(stk, nc, "ldx2p")
        ldmo = lambda dc: 16 * ((dc + 1) // 2)
        pe0, dve0, act0 = S["pe"].n, S["dve"].n, S["act"].n
        mmp = lambda rc: pe0 + rc + 1
        evbase = {}
        cum = dve0 + 1 + 2 * NDC
        for rc in range(NRC):
            evbase[rc] = cum
            cum += 1 if (rc < 64 or rc == 144) else 5
        evW = lambda rc: evbase[rc] + 1
        convm = lambda rc: evbase[rc] + 5
        aplm = lambda dc: dve0 + 1 + 2 * (dc + 1)
        silm = lambda crc: act0 + crc + 1
        wrGm, wrXm = {}, {}
        cG, cX = [0, 0], [0, 0]
        for rc in range(64):
            cG[rc % 2] += 16 * NCH
            wrGm[rc] = cG[rc % 2]
        for crc in range(80):
            cX[crc % 2] += 16 * NCH
            wrXm[crc] = cX[crc % 2]

        with nc.Block() as blk:
            @blk.gpsimd
            def _(g):
                p = Eng(g, S["pool"])
                for rc in range(NRC):
                    if rc >= 2:
                        p.wait(S["pe"], mmp(rc - 2))
                    lwm[rc] = p.dma(wb[rc % 2][:], WinB[rc].rearrange("p a b -> p (a b)"), ldwB[rc % 2])

            @blk.tensor
            def _(t):
                p = MM(t, S["pe"])
                for rc in range(NRC):
                    p.wait(ldwB[rc % 2], lwm[rc])
                    if rc >= 2:
                        p.wait(S["dve"], evW(rc - 2))
                    for dc in range(NDC):
                        if rc == 0:
                            p.wait(S["dve"], aplm(dc))
                        for tt in range(NTT):
                            mk = p.mm(psW[rc % 2][:, tt * 512:(tt + 1) * 512],
                                      wb[rc % 2][:, dc * 128:(dc + 1) * 128],
                                      xln[:, dc * NT + tt * 512:dc * NT + (tt + 1) * 512],
                                      dc == 0, dc == NDC - 1,
                                      mark=(dc == NDC - 1 and tt == NTT - 1))
                    assert mk == mmp(rc)

            @blk.vector
            def _(v):
                d = Eng(v, S["dve"])
                d.op("memset", convraw[:, 0:3], 0.0)
                for dc in range(NDC):
                    d.wait(ldxL[0] if dc % 2 == 0 else ldx2p,
                           ldme(dc) if dc % 2 == 0 else ldmo(dc))
                    km = d.op("tensor_tensor", xfL[dc % 2][:], xfL[dc % 2][:], AbcL[:], ALU.mult,
                              dep=aplm(dc - 2) if dc >= 2 else None)
                    assert d.op("tensor_tensor", xln[:, dc * NT:(dc + 1) * NT], xfL[dc % 2][:],
                                BbcL[:], ALU.add, dep=km) == aplm(dc)
                for rc in range(NRC):
                    d.wait(S["pe"], mmp(rc))
                    if rc < 64:
                        if rc >= 1:
                            d.wait(wrG[(rc - 1) % 2], wrGm[rc - 1])
                        assert d.op("tensor_copy", gevac[rc % 2][:], psW[rc % 2][:]) == evW(rc)
                    elif rc < 144:
                        crc = rc - 64
                        if crc >= 1:
                            d.wait(S["act"], silm(crc - 1))
                        assert d.op("tensor_copy", convraw[:, 3:3 + NT], psW[rc % 2][:]) == evW(rc)
                        d.op("tensor_scalar", convacc[:], convraw[:, 0:NT],
                             cwcol[:, crc * 4:crc * 4 + 1], None, ALU.mult)
                        for kk in range(1, 4):
                            mk = d.op("scalar_tensor_tensor", convacc[:], convraw[:, kk:kk + NT],
                                      cwcol[:, crc * 4 + kk:crc * 4 + kk + 1], convacc[:],
                                      ALU.mult, ALU.add)
                        assert mk == convm(rc)
                    else:
                        d.wait(S["act"], act0 + 80)
                        assert d.op("tensor_copy", convacc[:], psW[rc % 2][:]) == evW(rc)
                d.op("tensor_scalar", convacc[:], convacc[:], cons[:, 3:4], None, ALU.add, dep=True)
                marks["dt0"] = S["dve"].n
                d.wait(S["act"], act0 + 80 + 1)
                d.op("tensor_scalar", convacc[:], convacc[:], 0.0, None, ALU.max, dep=True)
                marks["dt1"] = S["dve"].n
                d.wait(S["act"], act0 + 80 + 3)
                marks["adt"] = d.op("tensor_tensor", dtv[:], convacc[:], convraw[:, 3:3 + NT], ALU.add, dep=True)

            @blk.scalar
            def _(a):
                c = Eng(a, S["act"])
                for crc in range(80):
                    c.wait(S["dve"], convm(crc + 64))
                    if crc >= 1:
                        c.wait(wrX[(crc - 1) % 2], wrXm[crc - 1])
                    assert c.op("activation", xevac[crc % 2][:], convacc[:], AF.Silu,
                                bias=cbcol[:, crc:crc + 1]) == silm(crc)
                    c.wait(S["act"], silm(crc))
                    for cb in range(NCH):
                        c.dma(xcs[cb, crc], xevac[crc % 2][:, cb * 128:(cb + 1) * 128], wrX[crc % 2])
                c.wait(S["dve"], marks["dt0"])
                c.op("activation", convraw[:, 3:3 + NT], convacc[:], AF.Abs)
                c.op("activation", convraw[:, 3:3 + NT], convraw[:, 3:3 + NT], AF.Exp, scale=-1.0, dep=True)
                c.op("activation", convraw[:, 3:3 + NT], convraw[:, 3:3 + NT], AF.Ln, bias=cons[:, 0:1], dep=True)
                c.wait(wrX[0], wrX[0].n)
                c.wait(wrX[1], wrX[1].n)

            @blk.sync
            def _(sy):
                s = Eng(sy, None)
                for dc in range(NDC):
                    if dc >= 2:
                        s.wait(S["dve"], aplm(dc - 2))
                    if dc % 2 == 0:
                        assert s.dma(xfL[0][:], lsrc[dc], ldxL[0]) == ldme(dc)
                    else:
                        assert s.dma(xfL[1][:], lsrc[dc], ldx2p) == ldmo(dc)
                for rc in range(64):
                    s.wait(S["dve"], evW(rc))
                    for cb in range(NCH):
                        s.dma(gates[cb, rc], gevac[rc % 2][:, cb * 128:(cb + 1) * 128], wrG[rc % 2])
                s.wait(S["dve"], marks["adt"])
                s.wait(wrG[0], wrG[0].n)
                s.wait(wrG[1], wrG[1].n)
        stk.close()
        lnd["stk"].close()


    def phase_d():
        stk = ExitStack()
        ygh = stk.enter_context(nc.sbuf_tensor("ygh", [128, NCC * (NT // 2)], BF16))
        wob = [stk.enter_context(nc.sbuf_tensor(f"wob{i}", [128, NCC * 128], BF16)) for i in range(2)]
        xr = [stk.enter_context(nc.sbuf_tensor(f"xrd{i}", [128, NT // 2], F32)) for i in range(2)]
        x2t = [stk.enter_context(nc.sbuf_tensor(f"x2t{i}", [128, NT // 2], F32)) for i in range(2)]
        psD = [stk.enter_context(nc.psum_tensor(f"psD{i}", [128, NT // 2], F32)) for i in range(2)]
        ldy = Sem(stk, nc, "ldy")
        ldwD = [Sem(stk, nc, f"ldwD{i}") for i in range(2)]
        ldxD = [Sem(stk, nc, f"ldxD{i}") for i in range(2)]
        wrD = [Sem(stk, nc, f"wrD{i}") for i in range(2)]
        HT = NT // 2
        NH2 = NCH // 2
        TS = min(512, HT)
        NT2 = HT // TS

        for hf in range(2):
            lym, lwm, lxm = {}, {}, {}
            pe0, dve0 = S["pe"].n, S["dve"].n
            wrb = (wrD[0].n, wrD[1].n)
            nstD = 2 if "dbg_x2" in dbg else 1
            mmdc = lambda dc: pe0 + dc + 1
            evdc = lambda dc: dve0 + dc + 1
            with nc.Block() as blk:
                @blk.sync
                def _(sy):
                    s = Eng(sy, None)
                    for cc in range(NCC):
                        k = s.dma(ygh[:, cc * HT:(cc + 1) * HT].rearrange("p (a t) -> p a t", a=NH2),
                                  ygs[hf * NH2:(hf + 1) * NH2, cc].rearrange("a p t -> p a t"), ldy)
                    lym[0] = k
                    for dc in range(NDC):
                        if dc >= 2:
                            s.wait(S["dve"], evdc(dc - 2))
                        lxm[dc] = s.dma(xr[dc % 2][:], x1s[dc][:, hf * HT:(hf + 1) * HT], ldxD[dc % 2])

                @blk.gpsimd
                def _(g):
                    p = Eng(g, S["pool"])
                    for dc in range(NDC):
                        if dc >= 2:
                            p.wait(S["pe"], mmdc(dc - 2))
                        lwm[dc] = p.dma(wob[dc % 2][:], WoutB[dc].rearrange("p a b -> p (a b)"), ldwD[dc % 2])

                @blk.tensor
                def _(t):
                    p = MM(t, S["pe"])
                    p.wait(ldy, lym[0])
                    for dc in range(NDC):
                        p.wait(ldwD[dc % 2], lwm[dc])
                        if dc >= 2:
                            p.wait(S["dve"], evdc(dc - 2))
                        for cc in range(NCC):
                            for tt in range(NT2):
                                mk = p.mm(psD[dc % 2][:, tt * TS:(tt + 1) * TS],
                                          wob[dc % 2][:, cc * 128:(cc + 1) * 128],
                                          ygh[:, cc * HT + tt * TS:cc * HT + (tt + 1) * TS],
                                          cc == 0, cc == NCC - 1,
                                          mark=(cc == NCC - 1 and tt == NT2 - 1))
                        assert mk == mmdc(dc)

                @blk.vector
                def _(v):
                    d = Eng(v, S["dve"])
                    for dc in range(NDC):
                        d.wait(S["pe"], mmdc(dc))
                        d.wait(ldxD[dc % 2], lxm[dc])
                        if dc >= 2:
                            d.wait(wrD[dc % 2], wrb[dc % 2] + 16 * nstD * ((dc - 2) // 2 + 1))
                        assert d.op("tensor_tensor", x2t[dc % 2][:], psD[dc % 2][:], xr[dc % 2][:],
                                    ALU.add) == evdc(dc)

                @blk.scalar
                def _(a):
                    c = Eng(a, S["act"])
                    for dc in range(NDC):
                        c.wait(S["dve"], evdc(dc))
                        c.dma(x2s[dc][:, hf * HT:(hf + 1) * HT], x2t[dc % 2][:], wrD[dc % 2])
                        if "dbg_x2" in dbg:
                            c.dma(dbg["dbg_x2"][dc][:, hf * HT:(hf + 1) * HT], x2t[dc % 2][:], wrD[dc % 2])
                    c.wait(wrD[0], wrD[0].n)
                    c.wait(wrD[1], wrD[1].n)
            for sm in (ldy, ldwD[0], ldwD[1], ldxD[0], ldxD[1], wrD[0], wrD[1]):
                pass
        stk.close()


    stkAB = ExitStack()
    xln = stkAB.enter_context(nc.sbuf_tensor("xlnAB", [128, NDC * NT], BF16))

    lnd = ln_phase(x0T, xln, "a")
    ffn_phase(xln, x0T, x1s, "fa", lnd, dbg_out=dbg.get("dbg_x1"))
    lnd = ln_phase(x1s, xln, "b")
    proj_phase(xln, lnd)
    stkAB.close()

    stkC = ExitStack()
    ctxC = dict(NT=NT, NCH=NCH, cons=cons, identF=identF, identB=identB, maskT=maskT,
                onesrowB=onesrowB, onecolB=onecolB, iot=iot, nwTc=nwTc, Dex=Dex, dtv=dtv, Acol=cons[:, 4:5],
                gates=gates, xcs=xcs, ygs=ygs, bncA=bncA, bncB=bncB, Eng=Eng, MM=MM)
    emit_phase_c(nc, S, Sem, ctxC, stkC)
    stkC.close()

    phase_d()

    stkE = ExitStack()
    xlnE = stkE.enter_context(nc.sbuf_tensor("xlnE", [128, NDC * NT], BF16))
    lnd = ln_phase(x2s, xlnE, "e")
    ffn_phase(xlnE, x2s, outT, "fe", lnd)
    stkE.close()

    if DEBUG and ("gates" in debug_names or "xcs" in debug_names or "dtv" in debug_names):
        with nc.Block() as blk:
            @blk.sync
            def _(sy):
                s = Eng(sy, None)
                if "gates" in debug_names:
                    s.dma(dbg["dbg_gates"].rearrange("a b p t -> (a b) (p t)"),
                          gates.rearrange("a b p t -> (a b) (p t)"), S["fin"])
                if "xcs" in debug_names:
                    s.dma(dbg["dbg_xcs"].rearrange("a b p t -> (a b) (p t)"),
                          xcs.rearrange("a b p t -> (a b) (p t)"), S["fin"])
                s.wait(S["fin"], S["fin"].n)
        if "dtv" in debug_names:
            with nc.Block() as blk:
                @blk.sync
                def _(sy):
                    s = Eng(sy, None)
                    s.dma(dbg["dbg_dtv"], dtv[:], S["fin"])
                    s.wait(S["fin"], S["fin"].n)

    if DEBUG and "yg" in debug_names:
        with nc.Block() as blk:
            @blk.sync
            def _(sy):
                s = Eng(sy, None)
                s.dma(dbg["dbg_yg"].rearrange("a b p t -> (a b) (p t)"),
                      ygs.rearrange("a b p t -> (a b) (p t)"), S["fin"])
                s.wait(S["fin"], S["fin"].n)

    st.close()
    return nc, dbg


def stage_inputs(hs_b, in_proj_w, out_proj_w, sign, conv_w, conv_b, dt_bias, A_log, Dv, norm_w, scale, NT=2048):
    m = {}
    m["x0T"] = np.ascontiguousarray(hs_b[:NT].T).reshape(NDC, 128, NT)
    m["WinB"] = np.ascontiguousarray(in_proj_w.reshape(NRC, 128, NDC, 128).transpose(0, 3, 2, 1))
    m["WoutB"] = np.ascontiguousarray(out_proj_w.reshape(NDC, 128, NCC, 128).transpose(0, 3, 2, 1))
    m["SgB"] = np.ascontiguousarray(sign.reshape(NDC, 128, NDC, 128).transpose(2, 1, 0, 3))
    m["convW"] = np.ascontiguousarray(conv_w.reshape(80, 128, 4).transpose(1, 0, 2))
    m["convBi"] = np.ascontiguousarray(conv_b.reshape(80, 128).T)
    m["dtBi"] = np.ascontiguousarray(dt_bias.reshape(128, 1))
    m["Alog"] = np.ascontiguousarray(A_log.reshape(128, 1))
    m["Dexp"] = np.ascontiguousarray(
        np.concatenate([np.broadcast_to(Dv[0::2], (64, 64)), np.broadcast_to(Dv[1::2], (64, 64))], axis=0))
    m["nwT"] = np.ascontiguousarray(norm_w.reshape(NCC, 128).T)
    m["scal"] = np.asarray(scale, np.float32).reshape(1, 1)
    return m



def kernel(L0_attn_hidden_states, L0_attn_in_proj_w, L0_attn_conv_w, L0_attn_conv_b,
           L0_attn_dt_bias, L0_attn_A_log, L0_attn_D, L0_attn_norm_w, L0_attn_out_proj_w,
           L0_ffn_sign, L0_ffn_scale):
    from concourse.bass_utils import run_bass_kernel_spmd
    hs = np.asarray(L0_attn_hidden_states, np.float32)
    NT = hs.shape[1]
    nc, _ = build(NT)
    base = stage_inputs(hs[0], np.asarray(L0_attn_in_proj_w, np.float32),
                        np.asarray(L0_attn_out_proj_w, np.float32),
                        np.asarray(L0_ffn_sign, np.float32),
                        np.asarray(L0_attn_conv_w, np.float32),
                        np.asarray(L0_attn_conv_b, np.float32),
                        np.asarray(L0_attn_dt_bias, np.float32),
                        np.asarray(L0_attn_A_log, np.float32),
                        np.asarray(L0_attn_D, np.float32),
                        np.asarray(L0_attn_norm_w, np.float32),
                        np.asarray(L0_ffn_scale, np.float32), NT=NT)
    in_maps = []
    for b in range(8):
        m = dict(base)
        m["x0T"] = np.ascontiguousarray(hs[b].T).reshape(NDC, 128, NT)
        in_maps.append(m)
    res = run_bass_kernel_spmd(nc, in_maps, core_ids=list(range(8)))
    out = np.stack([np.ascontiguousarray(
        res.results[b]["outT"].reshape(DM, NT).T) for b in range(8)])
    return out.astype(np.float32)


# version 13
# speedup vs baseline: 1.1131x; 1.0018x over previous
import numpy as np
from contextlib import ExitStack

import concourse.bass as bass
import concourse.mybir as mybir

F32 = mybir.dt.float32
BF16 = mybir.dt.bfloat16
ALU = mybir.AluOpType
AF = mybir.ActivationFunctionType

DM = 4096
NDC = 32
NRC = 145
NCC = 64
CS = 128
EPS = 1e-5

DEBUG = False


class Sem:
    def __init__(self, stack, nc, name):
        self.h = stack.enter_context(nc.semaphore(name))
        self.n = 0

    def add(self, k):
        self.n += k
        return self.n


class Eng:
    def __init__(self, eng, selfsem):
        self.e = eng
        self.s = selfsem

    def wait(self, sem, n):
        self.e.wait_ge(sem.h, n)

    def op(self, name, *args, dep=True, **kw):
        if dep is True:
            if self.s.n:
                self.e.wait_ge(self.s.h, self.s.n)
        elif dep is not None and dep > 0:
            self.e.wait_ge(self.s.h, dep)
        inst = getattr(self.e, name)(*args, **kw)
        inst.then_inc(self.s.h, 1)
        return self.s.add(1)

    def dma(self, out, in_, sem, **kw):
        inst = self.e.dma_start(out=out, in_=in_, **kw)
        inst.then_inc(sem.h, 16)
        return sem.add(16)


class MM:
    def __init__(self, eng, sem):
        self.e = eng
        self.s = sem

    def wait(self, sem, n):
        self.e.wait_ge(sem.h, n)

    def mm(self, out, lhsT, rhs, start, stop, mark=False, **kw):
        inst = self.e.matmul(out, lhsT, rhs, start=start, stop=stop, **kw)
        if mark:
            inst.then_inc(self.s.h, 1)
            return self.s.add(1)
        return None

    def tr(self, out, in_, ident, mark=False):
        inst = self.e.transpose(out, in_, ident)
        if mark:
            inst.then_inc(self.s.h, 1)
            return self.s.add(1)
        return None




def emit_phase_c(nc, S, Sem, ctx, stk):
    NT = ctx["NT"]; NCH = ctx["NCH"]
    cons = ctx["cons"]; identF = ctx["identF"]; identB = ctx["identB"]; maskT = ctx["maskT"]
    onesrowB = ctx["onesrowB"]; onecolB = ctx["onecolB"]; iot = ctx["iot"]
    nwTc = ctx["nwTc"]; Dex = ctx["Dex"]; dtv = ctx["dtv"]; Acol = ctx["Acol"]
    gates = ctx["gates"]; xcs = ctx["xcs"]; ygs = ctx["ygs"]; bncA = ctx["bncA"]; bncB = ctx["bncB"]

    sb = nc.sbuf_tensor
    xcb = [stk.enter_context(sb(f"xcb{i}", [128, 80 * 128], BF16)) for i in range(2)]
    gtb = stk.enter_context(sb("gtb", [128, 16 * 128], BF16))
    acs = [stk.enter_context(sb(f"acs{i}", [128, CS], F32)) for i in range(2)]
    decp = stk.enter_context(sb("decp", [128, CS], F32))
    adt_t = stk.enter_context(sb("adt_t", [128, CS], F32))
    sdf = stk.enter_context(sb("sdf", [128, CS], F32))
    decf = stk.enter_context(sb("decf", [128, CS], F32))
    dlc = [stk.enter_context(sb(f"dlc{i}", [128, 1], F32)) for i in range(2)]
    dl2 = [stk.enter_context(sb(f"dl2{i}", [2, 64], F32)) for i in range(2)]
    half01 = stk.enter_context(sb("half01", [2, 128], F32))
    i2 = stk.enter_context(sb("i2", [2, 128], mybir.dt.int32))
    h0t = stk.enter_context(sb("h0t", [2, 128], F32))
    sdT = stk.enter_context(sb("sdT", [128, 128], F32))
    decayT = stk.enter_context(sb("decayT", [128, 128], F32))
    dtT = stk.enter_context(sb("dtT", [128, 128], F32))
    ddtT = stk.enter_context(sb("ddtT", [128, 128], F32))
    dexpA = stk.enter_context(sb("dexpA", [128, 64 * NCH], F32))
    ahb = stk.enter_context(sb("ahb", [128, CS], BF16))
    ah32 = stk.enter_context(sb("ah32", [128, CS], F32))
    al32 = stk.enter_context(sb("al32", [128, CS], F32))
    albf = stk.enter_context(sb("albf", [128, CS], BF16))
    nahb = stk.enter_context(sb("nahb", [128, CS], BF16))
    nalb = stk.enter_context(sb("nalb", [128, CS], BF16))
    lhsT4 = stk.enter_context(sb("lhsT4", [4, 4096], BF16))
    rhs4 = stk.enter_context(sb("rhs4", [4, 4096], BF16))
    Lex = [stk.enter_context(sb(f"Lex{i}", [128, 1024], BF16)) for i in range(2)]
    Lt = stk.enter_context(sb("Lt", [128, 1024], F32))
    tril01 = stk.enter_context(sb("tril01", [128, 128], BF16))
    Msb = [stk.enter_context(sb(f"Msb{i}", [128, 1024], BF16)) for i in range(2)]
    xdt = [stk.enter_context(sb(f"xdt{i}", [128, 512], BF16)) for i in range(2)]
    xdd = [stk.enter_context(sb(f"xdd{i}", [128, 512], BF16)) for i in range(2)]
    Zs = [stk.enter_context(sb(f"Zsb{i}", [128, 512], F32)) for i in range(2)]
    BTsb = stk.enter_context(sb("BTsb", [128, 8 * 128], BF16))
    Gsb = stk.enter_context(sb("Gsb", [128, 8 * 128], BF16))
    Ssb = stk.enter_context(sb("Ssb", [128, 64 * 128], F32))
    Sbfst = [stk.enter_context(sb("Sbfs0", [128, 1024], BF16))] * 2
    STsb = stk.enter_context(sb("STsb", [128, 64 * 128], BF16))
    Ybf = stk.enter_context(sb("Ybf", [128, 64 * 128], BF16))
    rtmp = stk.enter_context(sb("rtmp", [128, 512], F32))
    dtmp = None
    y2st = [stk.enter_context(sb(f"y2st{i}", [128, 1024], BF16)) for i in range(2)]
    sgst = [stk.enter_context(sb("sgst0", [128, 1024], BF16))] * 2
    t1st = [stk.enter_context(sb(f"t1st{i}", [128, 1024], BF16)) for i in range(2)]
    ygst = [stk.enter_context(sb("ygst0", [128, 1024], BF16))] * 2
    rrow = stk.enter_context(sb("rrow", [1, 128], F32))
    rrowB = stk.enter_context(sb("rrowB", [1, 128], BF16))
    RSTD = stk.enter_context(sb("RSTD", [128, 128], F32))
    zeroB = stk.enter_context(sb("zeroB", [128, 128], F32))

    psL = stk.enter_context(nc.psum_tensor("psL", [128, 1024], F32))
    psY = [stk.enter_context(nc.psum_tensor(f"psYc{i}", [128, 512], F32)) for i in range(2)]
    psZ = stk.enter_context(nc.psum_tensor("psZ", [128, 512], F32))
    psSt = [stk.enter_context(nc.psum_tensor(f"psSt{i}", [128, 512], F32)) for i in range(2)]
    psT = [stk.enter_context(nc.psum_tensor("psTc0", [128, 1024], BF16))] * 2

    ldc = [Sem(stk, nc, f"ldc{i}") for i in range(2)]
    ldg = Sem(stk, nc, "ldg")
    bnc = Sem(stk, nc, "bnc")
    rla = Sem(stk, nc, "rla")
    rdl = [Sem(stk, nc, "rdl0"), Sem(stk, nc, "rdl1")]
    bncd = Sem(stk, nc, "bncd")
    wry = [Sem(stk, nc, f"wry{i}") for i in range(2)]

    class Marks(dict):
        strict = False
        def __missing__(self, key):
            if self.strict:
                raise KeyError(key)
            return 0
    M = Marks()

    def pool_stream(p, rec):
        k = p.op("iota", i2[:], pattern=[[1, 128]], base=0, channel_multiplier=-64)
        if rec: M["i2"] = k
        k = p.dma(xcb[0][:].rearrange("p (a t) -> p a t", a=80), xcs[0].rearrange("a p t -> p a t"), ldc[0])
        if rec: M["ldx", 0] = k
        for c in range(NCH):
            if c + 1 < NCH:
                if c >= 1:
                    p.wait(S["pe"], M["rnd_end", c - 1])
                    p.wait(S["dve"], M["yev", c - 1, 15])
                k = p.dma(xcb[(c + 1) % 2][:].rearrange("p (a t) -> p a t", a=80), xcs[c + 1].rearrange("a p t -> p a t"), ldc[(c + 1) % 2])
                if rec: M["ldx", c + 1] = k
            for h in range(4):
                if (c, h) != (0, 0):
                    pc, ph = (c, h * 2 - 1) if h >= 1 else (c - 1, 7)
                    p.wait(S["act"], M["silu", pc, ph])
                k = p.dma(gtb[:].rearrange("p (a t) -> p a t", a=16),
                          gates[c, h * 16:(h + 1) * 16].rearrange("a p t -> p a t"), ldg)
                if rec: M["ldg", c, h] = k

    def sync_stream(s, rec):
        s.wait(S["dve"], M["onesm"])
        k = s.dma(lhsT4[2:4, :], rhs4[0:2, :], rla)
        if rec: M["onesr"] = k
        s.wait(rla, M["onesr"])
        for c in range(NCH):
            s.wait(S["dve"], M["presplit", c])
            off = c * 128 * CS
            s.dma(bncB[0, off:off + 128 * CS].rearrange("(p l) -> p l", p=128), nahb[:], bnc)
            s.dma(bncB[1, off:off + 128 * CS].rearrange("(p l) -> p l", p=128), nalb[:], bnc)
            s.dma(bncB[2, off:off + 128 * CS].rearrange("(p l) -> p l", p=128), ahb[:], bnc)
            k = s.dma(bncB[3, off:off + 128 * CS].rearrange("(p l) -> p l", p=128), albf[:], bnc)
            if rec: M["bncw", c] = k
            s.wait(S["act"], M["predl", c])
            k = s.dma(bncA[2, c * 128:(c + 1) * 128].rearrange("(p o) -> p o", p=128), dlc[c % 2][:], bncd)
            if rec: M["bncdw", c] = k
            s.wait(bncd, M["bncdw", c])
            if c >= 2:
                s.wait(S["pe"], M["predmm", c - 2])
            k = s.dma(dl2[c % 2][:], bncA[2, c * 128:(c + 1) * 128].rearrange("(c q) -> q c", q=2),
                      rdl[c % 2], allow_slow_non_contiguous=True)
            if rec: M["rldl", c] = k
        for c in range(NCH):
            for q in range(4):
                s.wait(bnc, M["bncw", c])
                if (c, q) != (0, 0):
                    pc, pq = (c, q - 1) if q > 0 else (c - 1, 3)
                    s.wait(S["pe"], M["lmm", pc, pq * 4 + 3])
                off = c * 128 * CS + q * 4096
                s.dma(lhsT4[0:2, :], bncB[0:2, off:off + 4096], rla)
                k = s.dma(rhs4[2:4, :], bncB[2:4, off:off + 4096], rla)
                if rec: M["rlq", c, q] = k
            for gr in range(8):
                s.wait(S["dve"], M["yg", c, gr])
                for j in range(8):
                    k = s.dma(ygs[c, gr * 8 + j], ygst[0][:, j * 128:(j + 1) * 128], wry[gr % 2])
                if rec: M["wyg", c, gr] = k
        s.wait(wry[0], wry[0].n if not rec else M["wyg", NCH - 1, 6])
        s.wait(wry[1], wry[1].n if not rec else M["wyg", NCH - 1, 7])

    def pe_stream(p, rec):
        for c in range(NCH):
            p.wait(rdl[c % 2], M["rldl", c])
            k = p.mm(psZ[:, (c % 8) * 64:(c % 8) * 64 + 64], half01[:], dl2[c % 2][:], True, True,
                     mark=True)
            if rec: M["predmm", c] = k
            if rec and (c % 8 == 7 or c == NCH - 1):
                M["predexp", c // 8] = k
            if c % 8 == 7 and c + 1 < NCH:
                p.wait(S["dve"], M["dexpevA", c // 8])
        for c in range(NCH):
            p.wait(ldc[c % 2], M["ldx", c])
            if c >= 1:
                p.wait(S["dve"], M["yev", c - 1, 15])
            for g in range(8):
                k = p.mm(psY[g // 4][:, (g % 4) * 128:(g % 4 + 1) * 128],
                         xcb[c % 2][:, (64 + g) * 128:(65 + g) * 128],
                         xcb[c % 2][:, (72 + g) * 128:(73 + g) * 128],
                         True, True, mark=(g in (3, 7)))
                if rec and g == 3: M["gmm", c, 0] = k
                if rec and g == 7: M["gmm", c, 1] = k
            p.wait(S["act"], M["sd", c])
            if c >= 1:
                p.wait(S["dve"], M["rstdev", c - 1])
            p.tr(psSt[0][:, 0:128], sdf[:], identF[:])
            p.tr(psSt[0][:, 128:256], decf[:], identF[:])
            k = p.tr(psSt[0][:, 256:384], dtv[:, c * CS:(c + 1) * CS], identF[:], mark=True)
            if rec: M["ptr", c] = k
            if c >= 1:
                p.wait(S["dve"], M["stev", c - 1, 7])
            for g in range(8):
                k = p.tr(psT[0][:, g * 128:(g + 1) * 128],
                         xcb[c % 2][:, (64 + g) * 128:(65 + g) * 128], identB[:],
                         mark=(g == 7))
                if rec and g == 7: M["btr", c] = k
            for r in range(16):
                q = r // 4
                if c >= 1:
                    if r >= 1:
                        p.wait(S["dve"], M["zsev", c, r - 1])
                    elif c == 1:
                        p.wait(S["dve"], M["dexpfin"])
                    else:
                        p.wait(S["dve"], M["zsev", c - 1, 15])
                    g = r // 2
                    for j in range(4):
                        pp = r * 4 + j
                        k = p.mm(psZ[:, j * 128:(j + 1) * 128],
                                 xcb[c % 2][:, (72 + g) * 128:(73 + g) * 128],
                                 STsb[:, pp * 128:(pp + 1) * 128],
                                 True, True, mark=(j == 3))
                    if rec: M["zmm", c, r] = k
                p.wait(rla, M["rlq", c, q])
                if r == 0 and c == 0:
                    p.wait(rla, M["onesr"])
                if r >= 1:
                    p.wait(S["act"], M["lrelu", c, r - 1])
                elif c >= 1:
                    p.wait(S["act"], M["lrelu", c - 1, 15])
                for hh in range(8):
                    off = ((r % 4) * 8 + hh) * 128
                    col = slice(hh * 128, (hh + 1) * 128)
                    k = p.mm(psL[:, col], lhsT4[:, off:off + 128], rhs4[:, off:off + 128],
                             True, True, mark=(hh == 7))
                if rec: M["lmm", c, r] = k
                if r >= 1:
                    p.wait(S["dve"], M["xdtc", c, r - 1])
                else:
                    p.wait(S["dve"], M["btev", c])
                for j in range(4):
                    pp = r * 4 + j
                    k = p.tr(psT[0][:, j * 128:(j + 1) * 128],
                             xcb[c % 2][:, pp * 128:(pp + 1) * 128], identB[:], mark=(j == 3))
                if rec: M["xtr", c, r] = k
                if r >= 2:
                    p.wait(S["dve"], M["yev", c, r - 2])
                if r == 0:
                    p.wait(S["dve"], M["gev", c])
                if c >= 1:
                    p.wait(S["dve"], M["zsev", c, r])
                    for j in range(4):
                        k = p.mm(psY[r % 2][:, j * 128:(j + 1) * 128],
                                 Zs[r % 2][:, j * 128:(j + 1) * 128],
                                 identF[:], j == 0, False, is_transpose=True,
                                 skip_group_check=True, mark=(j == 3))
                    if rec: M["ztr", c, r] = k
                else:
                    for j in range(4):
                        k = p.mm(psY[r % 2][:, j * 128:(j + 1) * 128], zeroB[:],
                                 identF[:], j == 0, False, is_transpose=True,
                                 skip_group_check=True, mark=(j == 3))
                    if rec:
                        M["zmm", c, r] = None
                        M["ztr", c, r] = k
                p.wait(S["dve"], M["msb", c, r])
                for hh in range(8):
                    k = p.mm(psY[r % 2][(hh % 2) * 64:(hh % 2) * 64 + 64,
                                        (hh // 2) * 128:(hh // 2 + 1) * 128],
                             xdt[r % 2][:, hh * 64:(hh + 1) * 64],
                             Msb[r % 2][:, hh * 128:(hh + 1) * 128],
                             False, hh >= 6, skip_group_check=True, mark=(hh == 7))
                if rec: M["ydiag", c, r] = k
                p.wait(S["dve"], M["xdd", c, r])
                if r == 0:
                    p.wait(S["dve"], M["ptrev", c])
                    p.wait(S["dve"], M["btev", c])
                    p.wait(S["dve"], M["rstdev", c - 1] if c >= 1 else 0)
                elif r == 1:
                    p.wait(S["dve"], M["rec", c - 1, 15] if c >= 1 else 0)
                else:
                    p.wait(S["dve"], M["rec", c, r - 2])
                g = r // 2
                for j in range(4):
                    k = p.mm(psSt[r % 2][:, j * 128:(j + 1) * 128],
                             xdd[r % 2][:, j * 128:(j + 1) * 128],
                             BTsb[:, g * 128:(g + 1) * 128],
                             True, True, mark=(j == 3))
                if rec: M["smm", c, r] = k
                if rec and r == 15: M["rnd_end", c] = k
            p.wait(S["dve"], M["rec", c, 15])
            for gr in range(8):
                p.wait(S["dve"], M["y2", c, gr])
                for j in range(8):
                    k = p.mm(psSt[0][0:1, 0:128], onecolB[:],
                             y2st[0][:, j * 128:(j + 1) * 128],
                             gr == 0 and j == 0, gr == 7 and j == 7, mark=(j == 7))
                if rec: M["stat", c, gr] = k
            p.wait(S["dve"], M["rrowB", c])
            k = p.mm(psSt[0][:, 128:256], onesrowB[0:1, :], rrowB[0:1, :], True, True, mark=True)
            if rec: M["rbc", c] = k
            for sg in range(8):
                p.wait(S["act"], M["sbf", c, sg])
                if sg >= 1:
                    p.wait(S["dve"], M["stev", c, sg - 1])
                for j in range(8):
                    k = p.tr(psT[0][:, j * 128:(j + 1) * 128],
                             Sbfst[0][:, j * 128:(j + 1) * 128], identB[:], mark=(j == 7))
                if rec: M["str", c, sg] = k

    def dve_stream(d, rec):
        d.op("memset", Ssb[:], 0.0)
        d.op("memset", zeroB[:], 0.0)
        k = d.op("memset", rhs4[0:2, :], 1.0)
        M["onesm"] = k
        d.op("tensor_scalar", tril01[:], iot[:], 0.0, None, ALU.is_ge)
        d.wait(S["pool"], M["i2"])
        d.op("tensor_scalar", half01[:], i2[:], 0.0, None, ALU.is_ge)
        d.op("tensor_scalar", h0t[:], i2[:], 64.0, None, ALU.is_lt)
        d.op("tensor_tensor", half01[:], half01[:], h0t[:], ALU.mult)
        for c in range(NCH):
            if c >= 1:
                d.wait(bnc, M["bncw", c - 1])
            km = d.op("tensor_scalar", adt_t[:], dtv[:, c * CS:(c + 1) * CS], cons[:, 4:5], None,
                      ALU.mult, dep=M["prescan", c - 1] if c >= 1 else None)
            km = d.op("tensor_tensor_scan", acs[c % 2][:], cons[:, 0:1].to_broadcast((128, CS)),
                      adt_t[:], 0.0, ALU.mult, ALU.add, dep=km)
            if rec: M["prescan", c] = km
            km = d.op("tensor_copy", ahb[:], acs[c % 2][:], dep=km)
            km = d.op("tensor_copy", ah32[:], ahb[:], dep=km)
            km = d.op("tensor_tensor", al32[:], acs[c % 2][:], ah32[:], ALU.subtract, dep=km)
            km = d.op("tensor_copy", albf[:], al32[:], dep=km)
            km = d.op("tensor_scalar", nahb[:], ahb[:], -1.0, None, ALU.mult, dep=km)
            k = d.op("tensor_scalar", nalb[:], albf[:], -1.0, None, ALU.mult, dep=km)
            if rec: M["presplit", c] = k
            if c % 8 == 7 or c == NCH - 1:
                h = c // 8
                nn = min(8, NCH - 8 * h)
                d.wait(S["pe"], M["predexp", h])
                k = d.op("tensor_copy", dexpA[:, h * 512:h * 512 + nn * 64], psZ[:, 0:nn * 64])
                if rec: M["dexpevA", h] = k
                if rec and c == NCH - 1: M["dexpfin"] = k
        for c in range(NCH):
            if c >= 2:
                d.wait(S["act"], M["sd", c - 2])
            km = d.op("tensor_scalar", adt_t[:], dtv[:, c * CS:(c + 1) * CS], cons[:, 4:5], None,
                      ALU.mult, dep=M["dexpfin"] if c <= 1 else (M["scan", c - 1] if c >= 1 else None))
            km = d.op("tensor_tensor_scan", acs[c % 2][:], cons[:, 0:1].to_broadcast((128, CS)),
                      adt_t[:], 0.0, ALU.mult, ALU.add, dep=km)
            if rec: M["scan", c] = km
            d.op("tensor_scalar", decp[:], acs[c % 2][:], acs[c % 2][:, CS - 1:CS], None,
                 ALU.subtract, dep=km)
            if rec: M["decp", c] = d.s.n
            d.wait(S["pe"], M["ptr", c])
            d.op("tensor_copy", sdT[:], psSt[0][:, 0:128])
            d.op("tensor_copy", decayT[:], psSt[0][:, 128:256])
            d.op("tensor_copy", dtT[:], psSt[0][:, 256:384])
            k = d.op("tensor_tensor", ddtT[:], dtT[:], decayT[:], ALU.mult)
            if rec: M["ptrev", c] = k
            d.wait(S["pe"], M["gmm", c, 1])
            d.op("tensor_tensor", Gsb[:, 0:512].rearrange("p (a b) -> p a b", a=4),
                 psY[0][:, :].rearrange("p (a b) -> p a b", a=4),
                 tril01[:].unsqueeze(1).to_broadcast((128, 4, 128)), ALU.mult)
            k = d.op("tensor_tensor", Gsb[:, 512:1024].rearrange("p (a b) -> p a b", a=4),
                 psY[1][:, :].rearrange("p (a b) -> p a b", a=4),
                 tril01[:].unsqueeze(1).to_broadcast((128, 4, 128)), ALU.mult)
            if rec: M["gev", c] = k
            d.wait(S["pe"], M["btr", c])
            k = d.op("tensor_copy", BTsb[:, :], psT[0][:, :])
            if rec: M["btev", c] = k
            def rec_yev(r):
                nonlocal_marker = None
                d.wait(S["pe"], M["smm", c, r])
                km = d.op("tensor_tensor", rtmp[:].rearrange("p (a b) -> p a b", a=4),
                     Ssb[:, r * 512:(r + 1) * 512].rearrange("p (a b) -> p a b", a=4),
                     dexpA[:, c * 64 + r * 4:c * 64 + r * 4 + 4].unsqueeze(2).to_broadcast((128, 4, 128)), ALU.mult,
                     dep=M["yev", c, r - 1] if r >= 1 else (M["yev", c - 1, 15] if c >= 1 else None))
                k = d.op("tensor_tensor", Ssb[:, r * 512:(r + 1) * 512], rtmp[:], psSt[r % 2][:, :],
                         ALU.add, dep=km)
                if rec: M["rec", c, r] = k
                d.wait(S["pe"], M["ydiag", c, r])
                km = d.op("tensor_tensor", rtmp[:].rearrange("p (a b) -> p a b", a=4),
                     xcb[c % 2][:, r * 512:(r + 1) * 512].rearrange("p (a b) -> p a b", a=4),
                     Dex[:, r * 4:r * 4 + 4].unsqueeze(2).to_broadcast((128, 4, 128)), ALU.mult,
                     dep=M["rec", c, r])
                k = d.op("tensor_tensor", Ybf[:, r * 512:(r + 1) * 512], psY[r % 2][:, :],
                         rtmp[:], ALU.add, dep=km)
                if rec: M["yev", c, r] = k

            for r in range(16):
                if c >= 1:
                    d.wait(S["pe"], M["zmm", c, r])
                    if r >= 1:
                        d.wait(S["pe"], M["ztr", c, r - 1])
                    k = d.op("tensor_tensor", Zs[r % 2][:].rearrange("p (a b) -> p a b", a=8),
                             psZ[:].rearrange("p (a b) -> p a b", a=8),
                             sdT[:, r * 8:r * 8 + 8].unsqueeze(2).to_broadcast((128, 8, 64)), ALU.mult,
                             dep=M["zsev", c, r - 2] if r >= 2 else None)
                    if rec: M["zsev", c, r] = k
                elif rec:
                    M["zsev", c, r] = None
                d.wait(S["pe"], M["xtr", c, r])
                k = d.op("tensor_tensor", xdt[r % 2][:].rearrange("p (a b) -> p a b", a=8),
                         psT[0][:, 0:512].rearrange("p (a b) -> p a b", a=8),
                         dtT[:, r * 8:r * 8 + 8].unsqueeze(2).to_broadcast((128, 8, 64)), ALU.mult,
                         dep=M["xdd", c, r - 2] if r >= 2 else None)
                if rec:
                    M["xdt", c, r] = k
                    M["xdtc", c, r] = k
                k = d.op("tensor_tensor", xdd[r % 2][:].rearrange("p (a b) -> p a b", a=8),
                         xdt[r % 2][:].rearrange("p (a b) -> p a b", a=8),
                         decayT[:, r * 8:r * 8 + 8].unsqueeze(2).to_broadcast((128, 8, 64)), ALU.mult,
                         dep=M["xdt", c, r])
                if rec: M["xdd", c, r] = k
                d.wait(S["act"], M["lex", c, r])
                if r >= 2:
                    d.wait(S["pe"], M["ydiag", c, r - 2])
                g = r // 2
                k = d.op("tensor_tensor", Msb[r % 2][:].rearrange("p (a b) -> p a b", a=8),
                         Lex[r % 2][:].rearrange("p (a b) -> p a b", a=8),
                         Gsb[:, g * 128:(g + 1) * 128].unsqueeze(1).to_broadcast((128, 8, 128)),
                         ALU.mult,
                         dep=M["msb", c, r - 2] if r >= 2 else (M["gev", c] if r == 0 else None))
                if rec: M["msb", c, r] = k
                if r >= 1:
                    rec_yev(r - 1)
            rec_yev(15)
            for gr in range(8):
                if gr >= 2:
                    d.wait(S["pe"], M["stat", c, gr - 2])
                dd = M["yev", c, 2 * gr + 1]
                if gr >= 1:
                    dd = max(dd, M["y2", c, gr - 1])
                elif c >= 1:
                    dd = max(dd, M["y2", c - 1, 7])
                if gr >= 1:
                    d.wait(S["pe"], M["stat", c, gr - 1])
                k = d.op("tensor_tensor", y2st[0][:], Ybf[:, gr * 1024:(gr + 1) * 1024],
                         Ybf[:, gr * 1024:(gr + 1) * 1024], ALU.mult, dep=dd)
                if rec: M["y2", c, gr] = k
            d.wait(S["pe"], M["stat", c, 7])
            d.op("tensor_scalar", rrow[0:1, :], psSt[0][0:1, 0:128], 1.0 / 8192.0, None, ALU.mult)
            if rec: M["msq", c] = d.s.n
            d.wait(S["act"], M["rstd", c])
            k = d.op("tensor_copy", rrowB[0:1, :], rrow[0:1, :])
            if rec: M["rrowB", c] = k
            d.wait(S["pe"], M["rbc", c])
            k = d.op("tensor_copy", RSTD[:], psSt[0][:, 128:256])
            if rec: M["rstdev", c] = k
            for sg in range(8):
                d.wait(S["pe"], M["str", c, sg])
                k = d.op("tensor_copy", STsb[:, sg * 1024:(sg + 1) * 1024], psT[0][:, :],
                         dep=None)
                if rec: M["stev", c, sg] = k
            for gr in range(8):
                d.wait(S["act"], M["silu", c, gr])
                if (c, gr) >= (0, 1):
                    pc, pg = (c, gr - 1) if gr >= 1 else (c - 1, 7)
                    if pc >= 0:
                        d.wait(wry[pg % 2], M["wyg", pc, pg])
                dd = M["yg", c, gr - 1] if gr >= 1 else (M["yg", c - 1, 7] if c >= 1 else None)
                km = d.op("tensor_tensor", t1st[gr % 2][:].rearrange("p (a b) -> p a b", a=8),
                     Ybf[:, gr * 1024:(gr + 1) * 1024].rearrange("p (a b) -> p a b", a=8),
                     RSTD[:].unsqueeze(1).to_broadcast((128, 8, 128)), ALU.mult, dep=dd)
                km = d.op("tensor_tensor", t1st[gr % 2][:].rearrange("p (a b) -> p a b", a=8),
                     t1st[gr % 2][:].rearrange("p (a b) -> p a b", a=8),
                     nwTc[:, gr * 8:(gr + 1) * 8].unsqueeze(2).to_broadcast((128, 8, 128)), ALU.mult, dep=km)
                k = d.op("tensor_tensor", ygst[0][:], t1st[gr % 2][:], sgst[0][:], ALU.mult, dep=km)
                if rec: M["yg", c, gr] = k

    def act_stream(a, rec):
        for c in range(NCH):
            a.wait(S["dve"], M["prescan", c])
            if c >= 2:
                a.wait(bncd, M["bncdw", c - 2])
            k = a.op("activation", dlc[c % 2][:], acs[c % 2][:, CS - 1:CS], AF.Exp,
                     dep=M["predl", c - 2] if c >= 2 else None)
            if rec: M["predl", c] = k
        for c in range(NCH):
            a.wait(S["dve"], M["decp", c])
            a.op("activation", sdf[:], acs[c % 2][:], AF.Exp,
                 dep=M["sd", c - 1] if c >= 1 else None)
            k = a.op("activation", decf[:], decp[:], AF.Exp, scale=-1.0, dep=None)
            if rec: M["sd", c] = k
            for r in range(16):
                a.wait(S["pe"], M["lmm", c, r])
                km = a.op("activation", Lt[:], psL[:], AF.Relu, scale=-1.0,
                          dep=M["lex", c, r - 1] if r >= 1 else (M["lex", c - 1, 15] if c >= 1 else None))
                if rec: M["lrelu", c, r] = km
                if r >= 2:
                    a.wait(S["dve"], M["msb", c, r - 2])
                k = a.op("activation", Lex[r % 2][:], Lt[:], AF.Exp, scale=-1.0, dep=km)
                if rec: M["lex", c, r] = k
            a.wait(S["dve"], M["msq", c])
            a.op("activation", rrow[0:1, :], rrow[0:1, :], AF.Ln, bias=cons[0:1, 2:3])
            k = a.op("activation", rrow[0:1, :], rrow[0:1, :], AF.Exp, scale=-0.5)
            if rec: M["rstd", c] = k
            for sg in range(8):
                a.wait(S["dve"], M["rec", c, 2 * sg + 1])
                if sg >= 1:
                    a.wait(S["pe"], M["str", c, sg - 1])
                k = a.op("activation", Sbfst[0][:], Ssb[:, sg * 1024:(sg + 1) * 1024], AF.Copy,
                         dep=M["sbf", c, sg - 1] if sg >= 1 else (M["sbf", c - 1, 7] if c >= 1 else None))
                if rec: M["sbf", c, sg] = k
            for gr in range(8):
                a.wait(ldg, M["ldg", c, gr // 2])
                if gr >= 1:
                    a.wait(S["dve"], M["yg", c, gr - 1])
                elif c >= 1:
                    a.wait(S["dve"], M["yg", c - 1, 7])
                k = a.op("activation", sgst[0][:], gtb[:, (gr % 2) * 1024:(gr % 2 + 1) * 1024], AF.Silu,
                         dep=M["silu", c, gr - 1] if gr >= 1 else (M["silu", c - 1, 7] if c >= 1 else None))
                if rec: M["silu", c, gr] = k

    class CntEng:
        def __init__(self, key, cnt, dmac):
            self.key = key; self.c = cnt; self.d = dmac
            class _S:
                pass
            self.s = _S()
            self.s.n = cnt.get(key, 0)
        def op(self, *a, dep=None, **k):
            self.c[self.key] += 1
            self.s.n = self.c[self.key]
            return self.c[self.key]
        def dma(self, out, in_, sem, **kw):
            key = id(sem)
            self.d[key] = self.d.get(key, sem.n) + 16
            return self.d[key]
        def wait(self, *a, **k):
            pass
        def mm(self, *a, mark=False, **k):
            if mark:
                self.c["pe"] += 1
                return self.c["pe"]
            return None
        def tr(self, *a, mark=False, **k):
            if mark:
                self.c["pe"] += 1
                return self.c["pe"]
            return None

    cnt = {"pe": S["pe"].n, "dve": S["dve"].n, "act": S["act"].n, "pool": S["pool"].n}
    dmac = {}
    pool_stream(CntEng("pool", cnt, dmac), True)
    sync_stream(CntEng("sync", {"sync": 0, **cnt}, dmac), True)
    pe_stream(CntEng("pe", cnt, dmac), True)
    dve_stream(CntEng("dve", cnt, dmac), True)
    act_stream(CntEng("act", cnt, dmac), True)

    M.strict = True
    M["final"] = dict(cnt)
    Eng = ctx["Eng"]; MMc = ctx["MM"]
    with nc.Block() as blk:
        @blk.gpsimd
        def _(g):
            pool_stream(Eng(g, S["pool"]), False)

        @blk.sync
        def _(sy):
            sync_stream(Eng(sy, None), False)

        @blk.tensor
        def _(t):
            pe_stream(MMc(t, S["pe"]), False)

        @blk.vector
        def _(v):
            dve_stream(Eng(v, S["dve"]), False)

        @blk.scalar
        def _(a):
            act_stream(Eng(a, S["act"]), False)

    fin = M["final"]
    for e in ("pe", "dve", "act", "pool"):
        assert S[e].n == fin[e], f"count mismatch {e}: emit={S[e].n} count={fin[e]}"
    return M


def build(NT=2048, debug_names=()):
    assert NT % 512 == 0
    NTT = NT // 512
    NCH = NT // CS
    nc = bass.Bass("TRN2", target_bir_lowering=False, debug=False)

    x0T = nc.dram_tensor("x0T", [NDC, 128, NT], F32, kind="ExternalInput").ap()
    WinB = nc.dram_tensor("WinB", [NRC, 128, NDC, 128], F32, kind="ExternalInput").ap()
    WoutB = nc.dram_tensor("WoutB", [NDC, 128, NCC, 128], F32, kind="ExternalInput").ap()
    SgB = nc.dram_tensor("SgB", [NDC, 128, NDC, 128], F32, kind="ExternalInput").ap()
    convW = nc.dram_tensor("convW", [128, 80, 4], F32, kind="ExternalInput").ap()
    convBi = nc.dram_tensor("convBi", [128, 80], F32, kind="ExternalInput").ap()
    dtBi = nc.dram_tensor("dtBi", [128, 1], F32, kind="ExternalInput").ap()
    Alog = nc.dram_tensor("Alog", [128, 1], F32, kind="ExternalInput").ap()
    Dexp = nc.dram_tensor("Dexp", [128, 64], F32, kind="ExternalInput").ap()
    nwT = nc.dram_tensor("nwT", [128, NCC], F32, kind="ExternalInput").ap()
    scal = nc.dram_tensor("scal", [1, 1], F32, kind="ExternalInput").ap()

    outT = nc.dram_tensor("outT", [NDC, 128, NT], F32, kind="ExternalOutput").ap()

    x1s = nc.dram_tensor("x1s", [NDC, 128, NT], F32).ap()
    x2s = nc.dram_tensor("x2s", [NDC, 128, NT], F32).ap()
    gates = nc.dram_tensor("gates", [NCH, 64, 128, 128], BF16).ap()
    xcs = nc.dram_tensor("xcs", [NCH, 80, 128, 128], BF16).ap()
    ygs = nc.dram_tensor("ygs", [NCH, 64, 128, 128], BF16).ap()
    bncA = nc.dram_tensor("bncA", [4, 16384 * 2], F32).ap()
    bncB = nc.dram_tensor("bncB", [4, 16384 * (NT // 128)], BF16).ap()

    dbg = {}
    if DEBUG:
        def dbgt(nm, shape, dt=F32):
            dbg[nm] = nc.dram_tensor(nm, shape, dt, kind="ExternalOutput").ap()
        if "x1" in debug_names: dbgt("dbg_x1", [NDC, 128, NT])
        if "xcs" in debug_names: dbgt("dbg_xcs", [NCH, 80, 128, 128], BF16)
        if "gates" in debug_names: dbgt("dbg_gates", [NCH, 64, 128, 128], BF16)
        if "dtv" in debug_names: dbgt("dbg_dtv", [128, NT])
        if "yg" in debug_names: dbgt("dbg_yg", [NCH, 64, 128, 128], BF16)
        if "x2" in debug_names: dbgt("dbg_x2", [NDC, 128, NT])

    st = ExitStack()

    cons = st.enter_context(nc.sbuf_tensor("cons", [128, 16], F32))
    identB = st.enter_context(nc.sbuf_tensor("identB", [128, 128], BF16))
    identF = st.enter_context(nc.sbuf_tensor("identF", [128, 128], F32))
    maskT = st.enter_context(nc.sbuf_tensor("maskT", [128, 128], F32))
    onesrowB = st.enter_context(nc.sbuf_tensor("onesrowB", [1, 128], BF16))
    onecolB = st.enter_context(nc.sbuf_tensor("onecolB", [128, 1], BF16))
    iot = st.enter_context(nc.sbuf_tensor("iot", [128, 128], mybir.dt.int32))
    cwcol = st.enter_context(nc.sbuf_tensor("cwcol", [128, 320], F32))
    cbcol = st.enter_context(nc.sbuf_tensor("cbcol", [128, 80], F32))
    nwTc = st.enter_context(nc.sbuf_tensor("nwTc", [128, NCC], F32))
    Dex = st.enter_context(nc.sbuf_tensor("Dex", [128, 64], F32))
    dtv = st.enter_context(nc.sbuf_tensor("dtv", [128, NT], F32))

    S = {}
    for nm in ["pe", "dve", "act", "pool", "ldS", "fin"]:
        S[nm] = Sem(st, nc, nm)
    marks = {}

    with nc.Block() as blk:
        @blk.sync
        def _(sy):
            s = Eng(sy, None)
            s.dma(cwcol[:], convW.rearrange("p a b -> p (a b)"), S["ldS"])
            s.dma(cbcol[:], convBi[:], S["ldS"])
            s.dma(nwTc[:], nwT[:], S["ldS"])
            s.dma(cons[:, 3:4], dtBi[:], S["ldS"])
            s.dma(cons[:, 7:8], Alog[:], S["ldS"])
            s.dma(Dex[:], Dexp[:], S["ldS"])
            s.dma(cons[:, 8:9], scal[0:1, 0:1].to_broadcast((128, 1)), S["ldS"])

        @blk.gpsimd
        def _(g):
            p = Eng(g, S["pool"])
            marks["iota"] = p.op("iota", iot[:], pattern=[[1, 128]], base=0, channel_multiplier=-1)

        @blk.vector
        def _(v):
            d = Eng(v, S["dve"])
            d.wait(S["pool"], marks["iota"])
            d.op("memset", cons[:, 0:1], 1.0)
            d.op("memset", cons[:, 2:3], EPS)
            d.op("memset", cons[:, 6:7], 1.0 / DM)
            d.op("memset", onesrowB[:], 1.0)
            d.op("memset", onecolB[:], 1.0)
            d.op("tensor_scalar", identF[:], iot[:], 0.0, None, ALU.is_equal)
            d.op("tensor_scalar", identB[:], iot[:], 0.0, None, ALU.is_equal)
            d.op("tensor_scalar", maskT[:], iot[:], 0.0, 1.0, ALU.is_ge, ALU.subtract)
            marks["msk"] = d.op("tensor_scalar", maskT[:], maskT[:], 1e30, None, ALU.mult)

        @blk.scalar
        def _(a):
            c = Eng(a, S["act"])
            c.wait(S["ldS"], S["ldS"].n)
            c.wait(S["dve"], marks["msk"])
            c.op("activation", cons[:, 4:5], cons[:, 7:8], AF.Exp)
            c.op("activation", cons[:, 9:10], cons[:, 8:9], AF.Abs)
            c.op("activation", cons[:, 10:11], cons[:, 9:10], AF.Exp, scale=-1.0)
            marks["sp"] = c.op("activation", cons[:, 11:12], cons[:, 10:11], AF.Ln, bias=cons[:, 0:1])

        @blk.vector
        def _(v):
            d = Eng(v, S["dve"])
            d.wait(S["act"], marks["sp"])
            d.op("tensor_scalar", cons[:, 4:5], cons[:, 4:5], -1.0, None, ALU.mult)
            d.op("tensor_scalar", cons[:, 12:13], cons[:, 8:9], 0.0, None, ALU.max)
            d.op("tensor_tensor", cons[:, 13:14], cons[:, 12:13], cons[:, 11:12], ALU.add)
            d.op("tensor_scalar", cons[:, 1:2], cons[:, 13:14], 0.5, None, ALU.mult)

    def ln_phase(src, xln, tag):
        stk = ExitStack()
        tstk = ExitStack()
        xf = [stk.enter_context(nc.sbuf_tensor(f"xf{tag}{i}", [128, NT], F32)) for i in range(2)]
        Abc = stk.enter_context(nc.sbuf_tensor(f"Abc{tag}", [128, NT], BF16))
        Bbc = stk.enter_context(nc.sbuf_tensor(f"Bbc{tag}", [128, NT], BF16))
        xbf = tstk.enter_context(nc.sbuf_tensor(f"xb{tag}", [128, NT], BF16))
        sqb = tstk.enter_context(nc.sbuf_tensor(f"sq{tag}", [128, NT], BF16))
        lnr = tstk.enter_context(nc.sbuf_tensor(f"lnr{tag}", [1, 2 * NT], F32))
        lnrB = tstk.enter_context(nc.sbuf_tensor(f"lnrB{tag}", [1, 2 * NT], BF16))
        ldx = [Sem(stk, nc, f"ldx{tag}0"), Sem(stk, nc, f"ldx{tag}1")]
        ldm = {}

        pstk = ExitStack()
        psM = pstk.enter_context(nc.psum_tensor(f"psM{tag}", [1, NT], F32))
        psS = pstk.enter_context(nc.psum_tensor(f"psS{tag}", [1, NT], F32))

        pe0, dve0, act0 = S["pe"].n, S["dve"].n, S["act"].n
        mmstm = lambda dc: pe0 + dc + 1
        castm = lambda dc: dve0 + dc + 1
        sqm = lambda dc: act0 + dc + 1

        with nc.Block() as blk:
            @blk.sync
            def _(sy):
                s = Eng(sy, None)
                for dc in range(NDC):
                    if dc >= 2:
                        s.wait(S["pe"], mmstm(dc - 2))
                    ldm[dc] = s.dma(xf[dc % 2][:], src[dc], ldx[dc % 2])

            @blk.vector
            def _(v):
                d = Eng(v, S["dve"])
                for dc in range(NDC):
                    d.wait(ldx[dc % 2], ldm[dc])
                    if dc >= 1:
                        d.wait(S["pe"], mmstm(dc - 1))
                    assert d.op("tensor_copy", xbf[:], xf[dc % 2][:]) == castm(dc)

            @blk.scalar
            def _(a):
                c = Eng(a, S["act"])
                for dc in range(NDC):
                    c.wait(ldx[dc % 2], ldm[dc])
                    if dc >= 1:
                        c.wait(S["pe"], mmstm(dc - 1))
                    assert c.op("activation", sqb[:], xf[dc % 2][:], AF.Square) == sqm(dc)

            @blk.tensor
            def _(t):
                p = MM(t, S["pe"])
                for dc in range(NDC):
                    p.wait(S["dve"], castm(dc))
                    p.wait(S["act"], sqm(dc))
                    for tt in range(NTT):
                        p.mm(psM[0:1, tt * 512:(tt + 1) * 512], onecolB[:],
                             xbf[:, tt * 512:(tt + 1) * 512], dc == 0, dc == NDC - 1)
                    for tt in range(NTT):
                        mk = p.mm(psS[0:1, tt * 512:(tt + 1) * 512], onecolB[:],
                                  sqb[:, tt * 512:(tt + 1) * 512], dc == 0, dc == NDC - 1,
                                  mark=(tt == NTT - 1))
                    assert mk == mmstm(dc)

            @blk.vector
            def _(v):
                d = Eng(v, S["dve"])
                d.wait(S["pe"], mmstm(NDC - 1))
                d.op("tensor_scalar", lnr[0:1, 0:NT], psM[0:1, :], 1.0 / DM, None, ALU.mult)
                d.op("tensor_scalar", lnr[0:1, NT:2 * NT], psS[0:1, :], 1.0 / DM, None, ALU.mult)
                d.op("tensor_tensor", sqb[0:1, 0:NT], lnr[0:1, 0:NT], lnr[0:1, 0:NT], ALU.mult)
                marks["var"] = d.op("tensor_tensor", lnr[0:1, NT:2 * NT], lnr[0:1, NT:2 * NT],
                                    sqb[0:1, 0:NT], ALU.subtract)

            @blk.scalar
            def _(a):
                c = Eng(a, S["act"])
                c.wait(S["dve"], marks["var"])
                c.op("activation", lnr[0:1, NT:2 * NT], lnr[0:1, NT:2 * NT], AF.Ln, bias=cons[0:1, 2:3])
                marks["rstd"] = c.op("activation", lnr[0:1, NT:2 * NT], lnr[0:1, NT:2 * NT], AF.Exp, scale=-0.5)

            @blk.vector
            def _(v):
                d = Eng(v, S["dve"])
                d.wait(S["act"], marks["rstd"])
                d.op("tensor_tensor", lnr[0:1, 0:NT], lnr[0:1, 0:NT], lnr[0:1, NT:2 * NT], ALU.mult)
                d.op("tensor_scalar", lnr[0:1, 0:NT], lnr[0:1, 0:NT], -1.0, None, ALU.mult)
                d.op("tensor_copy", lnrB[0:1, 0:NT], lnr[0:1, NT:2 * NT])
                marks["rows"] = d.op("tensor_copy", lnrB[0:1, NT:2 * NT], lnr[0:1, 0:NT])

        pstk.close()
        psBA = tstk.enter_context(nc.psum_tensor(f"psBA{tag}", [128, 512], F32))
        psBB = tstk.enter_context(nc.psum_tensor(f"psBB{tag}", [128, 512], F32))
        dve0b = S["dve"].n
        bcev = lambda tt: dve0b + 2 * (tt + 1)
        with nc.Block() as blk:
            @blk.tensor
            def _(t):
                p = MM(t, S["pe"])
                p.wait(S["dve"], marks["rows"])
                for tt in range(NT // 512):
                    if tt >= 1:
                        p.wait(S["dve"], bcev(tt - 1))
                    mka = p.mm(psBA[:, :], onesrowB[0:1, :],
                               lnrB[0:1, tt * 512:(tt + 1) * 512], True, True, mark=True)
                    mkb = p.mm(psBB[:, :], onesrowB[0:1, :],
                               lnrB[0:1, NT + tt * 512:NT + (tt + 1) * 512], True, True, mark=True)
                    marks[f"bc{tag}{tt}"] = (mka, mkb)

            @blk.vector
            def _(v):
                d = Eng(v, S["dve"])
                for tt in range(NT // 512):
                    mka, mkb = marks[f"bc{tag}{tt}"]
                    d.wait(S["pe"], mka)
                    d.op("tensor_copy", Abc[:, tt * 512:(tt + 1) * 512], psBA[:, :], dep=True)
                    d.wait(S["pe"], mkb)
                    k = d.op("tensor_copy", Bbc[:, tt * 512:(tt + 1) * 512], psBB[:, :], dep=True)
                    assert k == bcev(tt)
        tstk.close()
        return dict(xf=xf, ldx=ldx, ldm=ldm, Abc=Abc, Bbc=Bbc, stk=stk, tag=tag, src=src)

    def ffn_phase(xln, res, dst, tag, lnd, dbg_out=None):
        stk = ExitStack()
        sgb = [stk.enter_context(nc.sbuf_tensor(f"sg{tag}{i}", [128, NDC * 128], BF16)) for i in range(2)]
        xr = [stk.enter_context(nc.sbuf_tensor(f"xr{tag}0", [128, NT], F32))] * 2
        x1t = [stk.enter_context(nc.sbuf_tensor(f"x1t{tag}0", [128, NT], F32))] * 2
        psF = [stk.enter_context(nc.psum_tensor(f"psF{tag}{i}", [128, NT], F32)) for i in range(2)]
        ldw = [Sem(stk, nc, f"ldw{tag}0"), Sem(stk, nc, f"ldw{tag}1")]
        ldr = [Sem(stk, nc, f"ldr{tag}0"), Sem(stk, nc, f"ldr{tag}1")]
        wr = [Sem(stk, nc, f"wr{tag}0"), Sem(stk, nc, f"wr{tag}1")]
        lwm, lrm = {}, {}
        nst = 2 if dbg_out is not None else 1
        xf, ldx, ldm, Abc, Bbc = lnd["xf"], lnd["ldx"], lnd["ldm"], lnd["Abc"], lnd["Bbc"]
        lsrc = lnd["src"]
        ldx2 = Sem(stk, nc, f"ldx2{tag}")

        pe0, dve0 = S["pe"].n, S["dve"].n
        aplm = lambda dc: dve0 + 2 * (dc + 1)
        mmfc = lambda oc: pe0 + oc + 1
        evm = lambda oc: dve0 + 2 * NDC + oc + 1
        wrm = lambda oc: 16 * nst * (oc // 2 + 1)

        with nc.Block() as blk:
            @blk.sync
            def _(sy):
                s = Eng(sy, None)
                for dc in range(NDC):
                    if dc >= 2:
                        s.wait(S["dve"], aplm(dc - 2))
                    ldm[(1, dc)] = s.dma(xf[dc % 2][:], lsrc[dc], ldx[0] if dc % 2 == 0 else ldx2)
                for oc in range(NDC):
                    if oc >= 1:
                        s.wait(S["dve"], evm(oc - 1))
                    lrm[oc] = s.dma(xr[0][:], res[oc], ldr[oc % 2])

            @blk.gpsimd
            def _(g):
                p = Eng(g, S["pool"])
                for oc in range(NDC):
                    if oc >= 2:
                        p.wait(S["pe"], mmfc(oc - 2))
                    lwm[oc] = p.dma(sgb[oc % 2][:], SgB[oc].rearrange("p a b -> p (a b)"), ldw[oc % 2])

            @blk.vector
            def _(v):
                d = Eng(v, S["dve"])
                for dc in range(NDC):
                    d.wait(ldx[0] if dc % 2 == 0 else ldx2, ldm[(1, dc)])
                    km = d.op("tensor_tensor", xf[dc % 2][:], xf[dc % 2][:], Abc[:], ALU.mult,
                              dep=aplm(dc - 2) if dc >= 2 else None)
                    assert d.op("tensor_tensor", xln[:, dc * NT:(dc + 1) * NT], xf[dc % 2][:],
                                Bbc[:], ALU.add, dep=km) == aplm(dc)
                for oc in range(NDC):
                    d.wait(S["pe"], mmfc(oc))
                    d.wait(ldr[oc % 2], lrm[oc])
                    if oc >= 1:
                        d.wait(wr[(oc - 1) % 2], wrm(oc - 1))
                    assert d.op("scalar_tensor_tensor", x1t[0][:], psF[oc % 2][:],
                                cons[:, 1:2], xr[oc % 2][:], ALU.mult, ALU.add,
                                dep=None) == evm(oc)

            @blk.tensor
            def _(t):
                p = MM(t, S["pe"])
                for oc in range(NDC):
                    p.wait(ldw[oc % 2], lwm[oc])
                    if oc >= 2:
                        p.wait(S["dve"], evm(oc - 2))
                    for ic in range(NDC):
                        if oc == 0:
                            p.wait(S["dve"], aplm(ic))
                        for tt in range(NTT):
                            mk = p.mm(psF[oc % 2][:, tt * 512:(tt + 1) * 512],
                                      sgb[oc % 2][:, ic * 128:(ic + 1) * 128],
                                      xln[:, ic * NT + tt * 512:ic * NT + (tt + 1) * 512],
                                      ic == 0, ic == NDC - 1,
                                      mark=(ic == NDC - 1 and tt == NTT - 1))
                    assert mk == mmfc(oc)

            @blk.scalar
            def _(a):
                c = Eng(a, S["act"])
                for oc in range(NDC):
                    c.wait(S["dve"], evm(oc))
                    c.dma(dst[oc], x1t[0][:], wr[oc % 2])
                    if dbg_out is not None:
                        c.dma(dbg_out[oc], x1t[0][:], wr[oc % 2])
                c.wait(wr[0], wr[0].n)
                c.wait(wr[1], wr[1].n)
        stk.close()
        lnd["stk"].close()


    def ln_apply_block(lnd, xln):
        xf, ldx, Abc, Bbc = lnd["xf"], lnd["ldx"], lnd["Abc"], lnd["Bbc"]
        lsrc = lnd["src"]
        stk2 = ExitStack()
        ldx2 = Sem(stk2, nc, "ldxap" + lnd["tag"])
        dve0 = S["dve"].n
        aplm = lambda dc: dve0 + 2 * (dc + 1)
        ldx0b = ldx[0].n
        ldme = lambda dc: ldx0b + 16 * (dc // 2 + 1)
        ldmo = lambda dc: 16 * ((dc + 1) // 2)
        with nc.Block() as blk:
            @blk.sync
            def _(sy):
                s = Eng(sy, None)
                for dc in range(0, NDC, 2):
                    if dc >= 2:
                        s.wait(S["dve"], aplm(dc - 2))
                    assert s.dma(xf[0][:], lsrc[dc], ldx[0]) == ldme(dc)

            @blk.gpsimd
            def _(g):
                p = Eng(g, S["pool"])
                for dc in range(1, NDC, 2):
                    if dc >= 2:
                        p.wait(S["dve"], aplm(dc - 2))
                    assert p.dma(xf[1][:], lsrc[dc], ldx2) == ldmo(dc)

            @blk.vector
            def _(v):
                d = Eng(v, S["dve"])
                for dc in range(NDC):
                    d.wait(ldx[0] if dc % 2 == 0 else ldx2, ldme(dc) if dc % 2 == 0 else ldmo(dc))
                    km = d.op("tensor_tensor", xf[dc % 2][:], xf[dc % 2][:], Abc[:], ALU.mult,
                              dep=aplm(dc - 2) if dc >= 2 else None)
                    assert d.op("tensor_tensor", xln[:, dc * NT:(dc + 1) * NT], xf[dc % 2][:],
                                Bbc[:], ALU.add, dep=km) == aplm(dc)
        stk2.close()
        lnd["stk"].close()

    def proj_phase(xln):
        stk = ExitStack()
        wb = [stk.enter_context(nc.sbuf_tensor(f"wbp{i}", [128, NDC * 128], BF16)) for i in range(2)]
        gevac = [stk.enter_context(nc.sbuf_tensor("gev0", [128, NT], BF16))] * 2
        xevac = [stk.enter_context(nc.sbuf_tensor("xev0", [128, NT], BF16))] * 2
        convraw = stk.enter_context(nc.sbuf_tensor("convraw", [128, NT + 3], F32))
        convacc = stk.enter_context(nc.sbuf_tensor("convacc", [128, NT], F32))
        psW = [stk.enter_context(nc.psum_tensor(f"psW{i}", [128, NT], F32)) for i in range(2)]
        ldwB = [Sem(stk, nc, f"ldwB{i}") for i in range(2)]
        wrG = [Sem(stk, nc, f"wrG{i}") for i in range(2)]
        wrX = [Sem(stk, nc, f"wrX{i}") for i in range(2)]
        lwm = {}

        pe0, dve0, act0 = S["pe"].n, S["dve"].n, S["act"].n
        mmp = lambda rc: pe0 + rc + 1
        evbase = {}
        cum = dve0 + 1
        for rc in range(NRC):
            evbase[rc] = cum
            cum += 1 if (rc < 64 or rc == 144) else 5
        evW = lambda rc: evbase[rc] + 1
        convm = lambda rc: evbase[rc] + 5
        silm = lambda crc: act0 + crc + 1
        wrGm, wrXm = {}, {}
        cG, cX = [0, 0], [0, 0]
        for rc in range(64):
            cG[rc % 2] += 16 * NCH
            wrGm[rc] = cG[rc % 2]
        for crc in range(80):
            cX[crc % 2] += 16 * NCH
            wrXm[crc] = cX[crc % 2]

        with nc.Block() as blk:
            @blk.gpsimd
            def _(g):
                p = Eng(g, S["pool"])
                for rc in range(NRC):
                    if rc >= 2:
                        p.wait(S["pe"], mmp(rc - 2))
                    lwm[rc] = p.dma(wb[rc % 2][:], WinB[rc].rearrange("p a b -> p (a b)"), ldwB[rc % 2])

            @blk.tensor
            def _(t):
                p = MM(t, S["pe"])
                for rc in range(NRC):
                    p.wait(ldwB[rc % 2], lwm[rc])
                    if rc >= 2:
                        p.wait(S["dve"], evW(rc - 2))
                    for dc in range(NDC):
                        for tt in range(NTT):
                            mk = p.mm(psW[rc % 2][:, tt * 512:(tt + 1) * 512],
                                      wb[rc % 2][:, dc * 128:(dc + 1) * 128],
                                      xln[:, dc * NT + tt * 512:dc * NT + (tt + 1) * 512],
                                      dc == 0, dc == NDC - 1,
                                      mark=(dc == NDC - 1 and tt == NTT - 1))
                    assert mk == mmp(rc)

            @blk.vector
            def _(v):
                d = Eng(v, S["dve"])
                d.op("memset", convraw[:, 0:3], 0.0)
                for rc in range(NRC):
                    d.wait(S["pe"], mmp(rc))
                    if rc < 64:
                        if rc >= 1:
                            d.wait(wrG[(rc - 1) % 2], wrGm[rc - 1])
                        assert d.op("tensor_copy", gevac[rc % 2][:], psW[rc % 2][:]) == evW(rc)
                    elif rc < 144:
                        crc = rc - 64
                        if crc >= 1:
                            d.wait(S["act"], silm(crc - 1))
                        assert d.op("tensor_copy", convraw[:, 3:3 + NT], psW[rc % 2][:]) == evW(rc)
                        d.op("tensor_scalar", convacc[:], convraw[:, 0:NT],
                             cwcol[:, crc * 4:crc * 4 + 1], None, ALU.mult)
                        for kk in range(1, 4):
                            mk = d.op("scalar_tensor_tensor", convacc[:], convraw[:, kk:kk + NT],
                                      cwcol[:, crc * 4 + kk:crc * 4 + kk + 1], convacc[:],
                                      ALU.mult, ALU.add)
                        assert mk == convm(rc)
                    else:
                        d.wait(S["act"], act0 + 80)
                        assert d.op("tensor_copy", convacc[:], psW[rc % 2][:]) == evW(rc)
                d.op("tensor_scalar", convacc[:], convacc[:], cons[:, 3:4], None, ALU.add, dep=True)
                marks["dt0"] = S["dve"].n
                d.wait(S["act"], act0 + 80 + 1)
                d.op("tensor_scalar", convacc[:], convacc[:], 0.0, None, ALU.max, dep=True)
                marks["dt1"] = S["dve"].n
                d.wait(S["act"], act0 + 80 + 3)
                marks["adt"] = d.op("tensor_tensor", dtv[:], convacc[:], convraw[:, 3:3 + NT], ALU.add, dep=True)

            @blk.scalar
            def _(a):
                c = Eng(a, S["act"])
                for crc in range(80):
                    c.wait(S["dve"], convm(crc + 64))
                    if crc >= 1:
                        c.wait(wrX[(crc - 1) % 2], wrXm[crc - 1])
                    assert c.op("activation", xevac[crc % 2][:], convacc[:], AF.Silu,
                                bias=cbcol[:, crc:crc + 1]) == silm(crc)
                    c.wait(S["act"], silm(crc))
                    for cb in range(NCH):
                        c.dma(xcs[cb, crc], xevac[crc % 2][:, cb * 128:(cb + 1) * 128], wrX[crc % 2])
                c.wait(S["dve"], marks["dt0"])
                c.op("activation", convraw[:, 3:3 + NT], convacc[:], AF.Abs)
                c.op("activation", convraw[:, 3:3 + NT], convraw[:, 3:3 + NT], AF.Exp, scale=-1.0, dep=True)
                c.op("activation", convraw[:, 3:3 + NT], convraw[:, 3:3 + NT], AF.Ln, bias=cons[:, 0:1], dep=True)
                c.wait(wrX[0], wrX[0].n)
                c.wait(wrX[1], wrX[1].n)

            @blk.sync
            def _(sy):
                s = Eng(sy, None)
                for rc in range(64):
                    s.wait(S["dve"], evW(rc))
                    for cb in range(NCH):
                        s.dma(gates[cb, rc], gevac[rc % 2][:, cb * 128:(cb + 1) * 128], wrG[rc % 2])
                s.wait(S["dve"], marks["adt"])
                s.wait(wrG[0], wrG[0].n)
                s.wait(wrG[1], wrG[1].n)
        stk.close()
        lnd["stk"].close()


    def phase_d():
        stk = ExitStack()
        ygh = stk.enter_context(nc.sbuf_tensor("ygh", [128, NCC * (NT // 2)], BF16))
        wob = [stk.enter_context(nc.sbuf_tensor(f"wob{i}", [128, NCC * 128], BF16)) for i in range(2)]
        xr = [stk.enter_context(nc.sbuf_tensor(f"xrd{i}", [128, NT // 2], F32)) for i in range(2)]
        x2t = [stk.enter_context(nc.sbuf_tensor(f"x2t{i}", [128, NT // 2], F32)) for i in range(2)]
        psD = [stk.enter_context(nc.psum_tensor(f"psD{i}", [128, NT // 2], F32)) for i in range(2)]
        ldy = Sem(stk, nc, "ldy")
        ldwD = [Sem(stk, nc, f"ldwD{i}") for i in range(2)]
        ldxD = [Sem(stk, nc, f"ldxD{i}") for i in range(2)]
        wrD = [Sem(stk, nc, f"wrD{i}") for i in range(2)]
        HT = NT // 2
        NH2 = NCH // 2
        TS = min(512, HT)
        NT2 = HT // TS

        for hf in range(2):
            lym, lwm, lxm = {}, {}, {}
            pe0, dve0 = S["pe"].n, S["dve"].n
            wrb = (wrD[0].n, wrD[1].n)
            nstD = 2 if "dbg_x2" in dbg else 1
            mmdc = lambda dc: pe0 + dc + 1
            evdc = lambda dc: dve0 + dc + 1
            with nc.Block() as blk:
                @blk.sync
                def _(sy):
                    s = Eng(sy, None)
                    for cc in range(NCC):
                        k = s.dma(ygh[:, cc * HT:(cc + 1) * HT].rearrange("p (a t) -> p a t", a=NH2),
                                  ygs[hf * NH2:(hf + 1) * NH2, cc].rearrange("a p t -> p a t"), ldy)
                    lym[0] = k
                    for dc in range(NDC):
                        if dc >= 2:
                            s.wait(S["dve"], evdc(dc - 2))
                        lxm[dc] = s.dma(xr[dc % 2][:], x1s[dc][:, hf * HT:(hf + 1) * HT], ldxD[dc % 2])

                @blk.gpsimd
                def _(g):
                    p = Eng(g, S["pool"])
                    for dc in range(NDC):
                        if dc >= 2:
                            p.wait(S["pe"], mmdc(dc - 2))
                        lwm[dc] = p.dma(wob[dc % 2][:], WoutB[dc].rearrange("p a b -> p (a b)"), ldwD[dc % 2])

                @blk.tensor
                def _(t):
                    p = MM(t, S["pe"])
                    p.wait(ldy, lym[0])
                    for dc in range(NDC):
                        p.wait(ldwD[dc % 2], lwm[dc])
                        if dc >= 2:
                            p.wait(S["dve"], evdc(dc - 2))
                        for cc in range(NCC):
                            for tt in range(NT2):
                                mk = p.mm(psD[dc % 2][:, tt * TS:(tt + 1) * TS],
                                          wob[dc % 2][:, cc * 128:(cc + 1) * 128],
                                          ygh[:, cc * HT + tt * TS:cc * HT + (tt + 1) * TS],
                                          cc == 0, cc == NCC - 1,
                                          mark=(cc == NCC - 1 and tt == NT2 - 1))
                        assert mk == mmdc(dc)

                @blk.vector
                def _(v):
                    d = Eng(v, S["dve"])
                    for dc in range(NDC):
                        d.wait(S["pe"], mmdc(dc))
                        d.wait(ldxD[dc % 2], lxm[dc])
                        if dc >= 2:
                            d.wait(wrD[dc % 2], wrb[dc % 2] + 16 * nstD * ((dc - 2) // 2 + 1))
                        assert d.op("tensor_tensor", x2t[dc % 2][:], psD[dc % 2][:], xr[dc % 2][:],
                                    ALU.add) == evdc(dc)

                @blk.scalar
                def _(a):
                    c = Eng(a, S["act"])
                    for dc in range(NDC):
                        c.wait(S["dve"], evdc(dc))
                        c.dma(x2s[dc][:, hf * HT:(hf + 1) * HT], x2t[dc % 2][:], wrD[dc % 2])
                        if "dbg_x2" in dbg:
                            c.dma(dbg["dbg_x2"][dc][:, hf * HT:(hf + 1) * HT], x2t[dc % 2][:], wrD[dc % 2])
                    c.wait(wrD[0], wrD[0].n)
                    c.wait(wrD[1], wrD[1].n)
            for sm in (ldy, ldwD[0], ldwD[1], ldxD[0], ldxD[1], wrD[0], wrD[1]):
                pass
        stk.close()


    stkAB = ExitStack()
    xln = stkAB.enter_context(nc.sbuf_tensor("xlnAB", [128, NDC * NT], BF16))

    lnd = ln_phase(x0T, xln, "a")
    ffn_phase(xln, x0T, x1s, "fa", lnd, dbg_out=dbg.get("dbg_x1"))
    lnd = ln_phase(x1s, xln, "b")
    ln_apply_block(lnd, xln)
    proj_phase(xln)
    stkAB.close()

    stkC = ExitStack()
    ctxC = dict(NT=NT, NCH=NCH, cons=cons, identF=identF, identB=identB, maskT=maskT,
                onesrowB=onesrowB, onecolB=onecolB, iot=iot, nwTc=nwTc, Dex=Dex, dtv=dtv, Acol=cons[:, 4:5],
                gates=gates, xcs=xcs, ygs=ygs, bncA=bncA, bncB=bncB, Eng=Eng, MM=MM)
    emit_phase_c(nc, S, Sem, ctxC, stkC)
    stkC.close()

    phase_d()

    stkE = ExitStack()
    xlnE = stkE.enter_context(nc.sbuf_tensor("xlnE", [128, NDC * NT], BF16))
    lnd = ln_phase(x2s, xlnE, "e")
    ffn_phase(xlnE, x2s, outT, "fe", lnd)
    stkE.close()

    if DEBUG and ("gates" in debug_names or "xcs" in debug_names or "dtv" in debug_names):
        with nc.Block() as blk:
            @blk.sync
            def _(sy):
                s = Eng(sy, None)
                if "gates" in debug_names:
                    s.dma(dbg["dbg_gates"].rearrange("a b p t -> (a b) (p t)"),
                          gates.rearrange("a b p t -> (a b) (p t)"), S["fin"])
                if "xcs" in debug_names:
                    s.dma(dbg["dbg_xcs"].rearrange("a b p t -> (a b) (p t)"),
                          xcs.rearrange("a b p t -> (a b) (p t)"), S["fin"])
                s.wait(S["fin"], S["fin"].n)
        if "dtv" in debug_names:
            with nc.Block() as blk:
                @blk.sync
                def _(sy):
                    s = Eng(sy, None)
                    s.dma(dbg["dbg_dtv"], dtv[:], S["fin"])
                    s.wait(S["fin"], S["fin"].n)

    if DEBUG and "yg" in debug_names:
        with nc.Block() as blk:
            @blk.sync
            def _(sy):
                s = Eng(sy, None)
                s.dma(dbg["dbg_yg"].rearrange("a b p t -> (a b) (p t)"),
                      ygs.rearrange("a b p t -> (a b) (p t)"), S["fin"])
                s.wait(S["fin"], S["fin"].n)

    st.close()
    return nc, dbg


def stage_inputs(hs_b, in_proj_w, out_proj_w, sign, conv_w, conv_b, dt_bias, A_log, Dv, norm_w, scale, NT=2048):
    m = {}
    m["x0T"] = np.ascontiguousarray(hs_b[:NT].T).reshape(NDC, 128, NT)
    m["WinB"] = np.ascontiguousarray(in_proj_w.reshape(NRC, 128, NDC, 128).transpose(0, 3, 2, 1))
    m["WoutB"] = np.ascontiguousarray(out_proj_w.reshape(NDC, 128, NCC, 128).transpose(0, 3, 2, 1))
    m["SgB"] = np.ascontiguousarray(sign.reshape(NDC, 128, NDC, 128).transpose(2, 1, 0, 3))
    m["convW"] = np.ascontiguousarray(conv_w.reshape(80, 128, 4).transpose(1, 0, 2))
    m["convBi"] = np.ascontiguousarray(conv_b.reshape(80, 128).T)
    m["dtBi"] = np.ascontiguousarray(dt_bias.reshape(128, 1))
    m["Alog"] = np.ascontiguousarray(A_log.reshape(128, 1))
    m["Dexp"] = np.ascontiguousarray(
        np.concatenate([np.broadcast_to(Dv[0::2], (64, 64)), np.broadcast_to(Dv[1::2], (64, 64))], axis=0))
    m["nwT"] = np.ascontiguousarray(norm_w.reshape(NCC, 128).T)
    m["scal"] = np.asarray(scale, np.float32).reshape(1, 1)
    return m



def kernel(L0_attn_hidden_states, L0_attn_in_proj_w, L0_attn_conv_w, L0_attn_conv_b,
           L0_attn_dt_bias, L0_attn_A_log, L0_attn_D, L0_attn_norm_w, L0_attn_out_proj_w,
           L0_ffn_sign, L0_ffn_scale):
    from concourse.bass_utils import run_bass_kernel_spmd
    hs = np.asarray(L0_attn_hidden_states, np.float32)
    NT = hs.shape[1]
    nc, _ = build(NT)
    base = stage_inputs(hs[0], np.asarray(L0_attn_in_proj_w, np.float32),
                        np.asarray(L0_attn_out_proj_w, np.float32),
                        np.asarray(L0_ffn_sign, np.float32),
                        np.asarray(L0_attn_conv_w, np.float32),
                        np.asarray(L0_attn_conv_b, np.float32),
                        np.asarray(L0_attn_dt_bias, np.float32),
                        np.asarray(L0_attn_A_log, np.float32),
                        np.asarray(L0_attn_D, np.float32),
                        np.asarray(L0_attn_norm_w, np.float32),
                        np.asarray(L0_ffn_scale, np.float32), NT=NT)
    in_maps = []
    for b in range(8):
        m = dict(base)
        m["x0T"] = np.ascontiguousarray(hs[b].T).reshape(NDC, 128, NT)
        in_maps.append(m)
    res = run_bass_kernel_spmd(nc, in_maps, core_ids=list(range(8)))
    out = np.stack([np.ascontiguousarray(
        res.results[b]["outT"].reshape(DM, NT).T) for b in range(8)])
    return out.astype(np.float32)
